# version 18
# baseline (speedup 1.0000x reference)
import contextlib
import numpy as np
import ml_dtypes
import concourse.bass as bass
import concourse.mybir as mybir
from concourse.bass_utils import run_bass_kernel_spmd
from concourse.bass_core_rust import IndirectOffsetOnAxis

F32 = mybir.dt.float32
BF16 = mybir.dt.bfloat16
I32 = mybir.dt.int32
AF = mybir.ActivationFunctionType
ALU = mybir.AluOpType
AX = mybir.AxisListType

B, T, C, H, D = 4, 4096, 1024, 16, 64
KT = 3072
NCH = KT // 128
QB = [[0, 3, 4, 7], [1, 2, 5, 6]]
LCH = [6, 12, 18, 24]
NSLOT = 4
TQ = 2048
EPS = 1e-5
LOG8 = float(np.log(8.0))

DEBUG_OUTS = []


def _orig(j):
    return 4 * (j // 3) + j % 3


def host_masks(r):
    m = np.zeros((24, 128, 512), dtype=np.float32)
    for i in range(NSLOT):
        L = LCH[i]
        t = 3 * QB[r][i] + 3
        for j in range(6):
            c = L - 6 + j
            w = 6 * i + j
            if c < t and c >= t - 3:
                k = c - (t - 3)
                s = np.arange(128)
                og = _orig(128 * k + s)
                m[w] = np.where(og[:, None] <= np.arange(512)[None, :], 0.0, -60000.0)
    return np.ascontiguousarray(m)


def host_padb(r):
    p = np.zeros((128, 12), dtype=np.float32)
    for i in range(NSLOT):
        L = LCH[i]
        t = 3 * QB[r][i] + 3
        for k in range(3):
            if L - 3 + k >= t:
                p[:, 3 * i + k] = -1e30
    return np.ascontiguousarray(p)


class Sem:
    def __init__(self, nc, name):
        self.h = nc.alloc_semaphore(name=name)
        self.v = 0

    def inc(self, inst, k=1):
        inst.then_inc(self.h, k)
        self.v += k
        return self.v


def build_nc():
    nc = bass.Bass("TRN2", target_bir_lowering=False)

    xq_d = nc.dram_tensor("xq", [TQ, C], F32, kind="ExternalInput").ap()
    xkv_d = nc.dram_tensor("xkv", [KT, C], F32, kind="ExternalInput").ap()
    wq_d = nc.dram_tensor("wq", [C, C], F32, kind="ExternalInput").ap()
    wk_d = nc.dram_tensor("wk", [C, C], F32, kind="ExternalInput").ap()
    wv_d = nc.dram_tensor("wv", [C, C], F32, kind="ExternalInput").ap()
    wp_d = nc.dram_tensor("wp", [C, C], F32, kind="ExternalInput").ap()
    bq_d = nc.dram_tensor("bq", [128, 8], F32, kind="ExternalInput").ap()
    bk_d = nc.dram_tensor("bk", [128, 8], F32, kind="ExternalInput").ap()
    bv_d = nc.dram_tensor("bv", [1, C], F32, kind="ExternalInput").ap()
    bp_d = nc.dram_tensor("bp", [1, C], F32, kind="ExternalInput").ap()
    wf_d = nc.dram_tensor("wf", [8 * C, C], F32, kind="ExternalInput").ap()
    bf_d = nc.dram_tensor("bf", [8, C], F32, kind="ExternalInput").ap()
    lu_d = nc.dram_tensor("lu", [1, 16], F32, kind="ExternalInput").ap()
    mk_d = nc.dram_tensor("mk", [24, 128, 512], F32, kind="ExternalInput").ap()
    pb_d = nc.dram_tensor("padb", [128, 12], F32, kind="ExternalInput").ap()
    out_d = nc.dram_tensor("out", [TQ, C], F32, kind="ExternalOutput").ap()
    weff_d = nc.dram_tensor("weff_scr", [8, 128, C], BF16, kind="Internal").ap()

    dbg = {}
    for name, shape, dt in [
        ("dbg_q", [128, 8, TQ], BF16), ("dbg_k", [128, 8, KT], BF16),
        ("dbg_v", [128, NCH, 1040], BF16), ("dbg_yn", [128, 8, 512 * NSLOT], BF16),
        ("dbg_negc", [128, 8], F32), ("dbg_weff", [8, 128, C], BF16),
    ]:
        if name in DEBUG_OUTS:
            dbg[name] = nc.dram_tensor(name, shape, dt, kind="ExternalOutput").ap()

    ones_1x128b = nc.inline_tensor(np.ones((1, 128), dtype=ml_dtypes.bfloat16), name="c_on1b").ap()
    ones_1x128f = nc.inline_tensor(np.ones((1, 128), dtype=np.float32), name="c_on1f").ap()
    ones_128x1b = nc.inline_tensor(np.ones((128, 1), dtype=ml_dtypes.bfloat16), name="c_onc").ap()
    iota8_c = nc.inline_tensor(np.arange(8, dtype=np.float32).reshape(1, 8), name="c_iota8").ap()
    id128_c = nc.inline_tensor(np.eye(128, dtype=np.float32), name="c_id128").ap()
    id128b_c = nc.inline_tensor(np.eye(128, dtype=ml_dtypes.bfloat16), name="c_id128b").ap()

    est = contextlib.ExitStack()

    def sb(name, shape, dt):
        return est.enter_context(nc.sbuf_tensor(name, shape, dt))

    Ksb = sb("Ksb", [128, 8, KT], BF16)
    Vsb = sb("Vsb", [128, NCH, 1040], BF16)
    Qsb = sb("Qsb", [128, 8, TQ], BF16)
    wvb = sb("wvb", [128, 8, C], BF16)
    bqc = sb("bqc", [128, 8], F32)
    bkc = sb("bkc", [128, 8], F32)
    bpr_b = sb("bprb", [1, C], BF16)
    beff_b = sb("beffb", [2, C], BF16)
    bfo = sb("bfo", [2, 1], I32)
    negc = sb("negc", [128, 8], F32)
    negcP = sb("negcP", [128, 12, 8], F32)
    padbs = sb("padbs", [128, 12], F32)
    on1b = sb("on1b", [1, 128], BF16)
    on1f = sb("on1f", [1, 128], F32)
    onc = sb("onc", [128, 1], BF16)
    id128 = sb("id128", [128, 128], F32)
    id128b = sb("idb", [128, 128], BF16)
    srow = sb("srow", [1, 64], F32)
    sbf = sb("sbf", [1, 8], BF16)
    crow = sb("crow", [1, 16], F32)
    lusb = sb("lusb", [1, 16], F32)
    gsc = sb("gsc", [1, 16], F32)
    idxi = sb("idxi", [1, 1], I32)
    gcol = sb("gcol", [128, 8], I32)
    gcol8f = sb("gcol8f", [128, 8], F32)
    iotac = sb("iotac", [128, 1], I32)
    iotacf = sb("iotacf", [128, 1], F32)
    gcolf = sb("gcolf", [128, 1], F32)
    tmpc = sb("tmpc", [128, 1], F32)
    lnst = sb("lnst", [128, 32], F32)
    cst = sb("cst", [128, 2], F32)
    p2s = sb("p2s", [128, 16], F32)

    sps0 = est.enter_context(nc.psum_tensor("sps0", [128, 1024], F32))
    sps1 = est.enter_context(nc.psum_tensor("sps1", [128, 1024], F32))
    sps2 = est.enter_context(nc.psum_tensor("sps2", [128, 1024], F32))
    ya = est.enter_context(nc.psum_tensor("ya", [65, 512], F32))
    yb = est.enter_context(nc.psum_tensor("yb", [65, 512], F32))
    sps = [sps0, sps1, sps2]
    spp = sps2[:, 0:512]

    sIN = Sem(nc, "sIN")
    sCN = Sem(nc, "sCN")
    sWL = Sem(nc, "sWL")
    sTR = Sem(nc, "sTR")
    sGA = Sem(nc, "sGA")
    sDV = Sem(nc, "sDV")
    sSC = Sem(nc, "sSC")
    sPE = Sem(nc, "sPE")
    sPL = Sem(nc, "sPL")
    sWS = Sem(nc, "sWS")
    sOUT = Sem(nc, "sOUT")
    sM2 = Sem(nc, "sM2")

    V = nc.vector
    SC = nc.scalar
    PE = nc.tensor
    GP = nc.gpsimd
    SY = nc.sync

    def vo(method, *a, **kw):
        V.wait_ge(sDV.h, sDV.v)
        return sDV.inc(method(*a, **kw))

    def so(method, *a, **kw):
        SC.wait_ge(sSC.h, sSC.v)
        return sSC.inc(method(*a, **kw))

    def po(method, *a, **kw):
        GP.wait_ge(sPL.h, sPL.v)
        return sPL.inc(method(*a, **kw))

    def sy_dma(sem, out, in_, **kw):
        SY.wait_ge(sem.h, sem.v)
        SY.dma_start(out=out, in_=in_, **kw).then_inc(sem.h, 16)
        sem.v += 16
        return sem.v

    def gp_dma(sem, out, in_, **kw):
        GP.wait_ge(sem.h, sem.v)
        GP.dma_start(out=out, in_=in_, **kw).then_inc(sem.h, 16)
        sem.v += 16
        return sem.v

    sps_last_exp = [0, 0, 0]
    sps_last_dve = [0, 0, 0]

    for dst, srcc in [(on1b, ones_1x128b), (on1f, ones_1x128f), (onc, ones_128x1b),
                      (id128, id128_c), (id128b, id128b_c), (bqc, bq_d), (bkc, bk_d)]:
        sy_dma(sCN, dst[:], srcc)
    v_consts = sCN.v
    gp_dma(sGA, lusb[:], lu_d)
    gp_dma(sGA, srow[0:1, 0:8], iota8_c)
    gp_dma(sGA, bpr_b[0:1, :], bp_d)
    gp_dma(sGA, padbs[:], pb_d)
    v_small = sGA.v
    po(GP.iota, iotac[:], pattern=[[0, 1]], base=0, channel_multiplier=1)
    v_iota = po(GP.tensor_copy, out=iotacf[:], in_=iotac[:])

    vo(V.memset, cst[:, 0:1], EPS)
    v_cst = vo(V.memset, cst[:, 1:2], -LOG8)

    SC.wait_ge(sGA.h, v_small)
    so(SC.activation, out=gsc[0:1, 0:8], in_=lusb[0:1, 8:16], func=AF.Ln)
    v_g = so(SC.activation, out=gsc[0:1, 8:16], in_=gsc[0:1, 0:8], func=AF.Ln, scale=-1.0)
    V.wait_ge(sSC.h, v_g)
    vo(V.tensor_tensor, out=gsc[0:1, 0:8], in0=lusb[0:1, 0:8], in1=gsc[0:1, 8:16], op=ALU.subtract)
    vo(V.reduce_max, out=srow[0:1, 26:27], in_=gsc[0:1, 0:8], axis=AX.X)
    vo(V.tensor_scalar, out=gsc[0:1, 8:16], in0=gsc[0:1, 0:8], scalar1=srow[0:1, 26:27],
       scalar2=None, op0=ALU.is_ge)
    vo(V.tensor_tensor, out=gsc[0:1, 8:16], in0=gsc[0:1, 8:16], in1=srow[0:1, 0:8], op=ALU.mult)
    vo(V.reduce_max, out=srow[0:1, 27:28], in_=gsc[0:1, 8:16], axis=AX.X)
    v_idxi = vo(V.tensor_copy, out=idxi[:], in_=srow[0:1, 27:28])

    vo(V.tensor_scalar, out=srow[0:1, 28:29], in0=srow[0:1, 27:28], scalar1=1024.0,
       scalar2=None, op0=ALU.mult)
    v_idxb = vo(V.tensor_copy, out=sbf[0:1, 0:1], in_=srow[0:1, 28:29])
    PE.wait_ge(sDV.h, v_idxb)
    PE.wait_ge(sCN.h, v_consts)
    v_pe_idx = sPE.inc(PE.matmul(spp[:, 0:1], on1b[:], sbf[0:1, 0:1], start=True, stop=True))
    V.wait_ge(sPE.h, v_pe_idx)
    vo(V.tensor_copy, out=gcolf[:], in_=spp[:, 0:1])
    V.wait_ge(sPL.h, v_iota)
    vo(V.tensor_tensor, out=tmpc[:], in0=iotacf[:], in1=gcolf[:], op=ALU.add)
    for c in range(8):
        vo(V.tensor_scalar, out=gcol8f[:, c:c + 1], in0=tmpc[:], scalar1=float(128 * c),
           scalar2=None, op0=ALU.add)
    v_gcol = vo(V.tensor_copy, out=gcol[:], in_=gcol8f[:])

    SC.wait_ge(sDV.h, v_cst)

    def load_wbf(wd, wbf, stage=None):
        GP.wait_ge(sPE.h, sPE.v)
        for c in range(8):
            gp_dma(sWL, wbf[:, c, :], wd[128 * c:128 * c + 128, :])
        return sWL.v

    ps_ctr = [0]
    ps_evac = [sps_last_dve[0], sps_last_dve[0], 0, 0, sps_last_dve[2], sps_last_dve[2]]

    def ps_view(i):
        return sps[i // 2][:, 512 * (i % 2):512 * (i % 2) + 512]

    def proj_mm(lhsT_fn, rhs_fn, evac_fn):
        i = ps_ctr[0] % 6
        ps = ps_view(i)
        ps_ctr[0] += 1
        if ps_evac[i]:
            PE.wait_ge(sDV.h, ps_evac[i])
        last = None
        for c in range(8):
            last = PE.matmul(ps, lhsT_fn(c), rhs_fn(c), start=(c == 0), stop=(c == 7))
        v_pe = sPE.inc(last)
        V.wait_ge(sPE.h, v_pe)
        evac_fn(ps)
        ps_evac[i] = sDV.v

    def emit_ln2t(xbuf, tokdst, base):
        ls = lnst[:].rearrange("p (t w) -> p t w", w=16)
        for t in range(2):
            vo(V.bn_stats, out=ls[:, t, 0:6], in_=xbuf[:, t, 0:512])
            vo(V.bn_stats, out=ls[:, t, 6:12], in_=xbuf[:, t, 512:1024])
            vo(V.bn_aggr, out=ls[:, t, 12:14], in_=ls[:, t, 0:12])
            vo(V.tensor_scalar, out=ls[:, t, 14:15], in0=ls[:, t, 12:13],
               scalar1=-1.0, scalar2=None, op0=ALU.mult)
        v_stats = sDV.v
        SC.wait_ge(sDV.h, v_stats)
        so(SC.activation, out=ls[:, :, 15:16], in_=ls[:, :, 13:14], func=AF.Ln,
           bias=cst[:, 0:1], scale=1.0)
        v_sq = so(SC.activation, out=ls[:, :, 15:16], in_=ls[:, :, 15:16], func=AF.Exp, scale=-0.5)
        V.wait_ge(sSC.h, v_sq)
        for t in range(2):
            vo(V.tensor_scalar, out=tokdst(base + t), in0=xbuf[:, t, :],
               scalar1=ls[:, t, 14:15], scalar2=ls[:, t, 15:16],
               op0=ALU.add, op1=ALU.mult)
        return sDV.v, v_sq

    def emit_path(xd, ntok, sg_fns, prefix):
        nsg = ntok // 512
        est_p = contextlib.ExitStack()
        xbuf = est_p.enter_context(nc.sbuf_tensor(prefix + "_xb", [128, 2, C], F32))
        tok4 = est_p.enter_context(nc.sbuf_tensor(prefix + "_tok", [128, 8, 4, 128], BF16))
        fm2 = est_p.enter_context(nc.sbuf_tensor(prefix + "_fm", [128, 2, 8, 512], BF16))
        nh = 2 * nsg
        v_loads = [0] * (nh + 1)
        for tt in range(2):
            sy_dma(sIN, xbuf[:, tt, :], xd[128 * tt:128 * tt + 128, :])
        v_loads[0] = sIN.v
        vtr_prev = 0
        v_pe_fm = 0

        def emit_ln_sg(sg):
            for h in range(2):
                g = 2 * sg + h
                V.wait_ge(sIN.h, v_loads[g])
                if vtr_prev and h == 0:
                    V.wait_ge(sTR.h, vtr_prev)
                SC.wait_ge(sIN.h, v_loads[g])
                v_ln, v_sq = emit_ln2t(xbuf, lambda tt: tok4[:, :, tt, :], 2 * h)
                if g + 1 < nh:
                    SY.wait_ge(sDV.h, v_ln)
                    for tt in range(2):
                        row0 = 256 * (g + 1) + 128 * tt
                        sy_dma(sIN, xbuf[:, tt, :], xd[row0:row0 + 128, :])
                    v_loads[g + 1] = sIN.v
            return sDV.v

        v_ln_done = emit_ln_sg(0)
        v_pe_fm2 = [0, 0]
        for sg in range(nsg):
            fm = fm2[:, sg % 2, :, :]
            SY.wait_ge(sDV.h, v_ln_done)
            if v_pe_fm2[sg % 2]:
                SY.wait_ge(sPE.h, v_pe_fm2[sg % 2])
            sy_dma(sTR, fm.rearrange("p c f -> p (c f)").rearrange("p (m g) -> p m g", g=128),
                   tok4[:].rearrange("p a b f -> p (a b f)"), transpose=True)
            vtr = sTR.v
            vtr_prev = vtr
            if sg + 1 < nsg:
                v_ln_done = emit_ln_sg(sg + 1)
            PE.wait_ge(sTR.h, vtr)
            for fn in sg_fns:
                fn(sg, fm)
            v_pe_fm2[sg % 2] = sPE.v
        est_p.close()

    est_w = contextlib.ExitStack()
    wkb = est_w.enter_context(nc.sbuf_tensor("wkb", [128, 8, C], BF16))
    bvb = est_w.enter_context(nc.sbuf_tensor("bvb", [128, C], BF16))
    gp_dma(sGA, bvb[0:1, :], bv_d)
    PE.wait_ge(sGA.h, sGA.v)
    PE.wait_ge(sDV.h, v_gcol)
    for half in range(2):
        v_pe_b = sPE.inc(PE.matmul(spp[:, 0:512], on1b[:], bvb[0:1, 512 * half:512 * half + 512],
                                   start=True, stop=True))
        V.wait_ge(sPE.h, v_pe_b)
        vo(V.tensor_copy, out=bvb[:, 512 * half:512 * half + 512], in_=spp[:, 0:512])
        PE.wait_ge(sDV.h, sDV.v)
    sps_last_dve[2] = sDV.v
    v_wk = load_wbf(wk_d, wkb)
    v_wv = load_wbf(wv_d, wvb)

    wgr = Qsb[:, :, 0:1024]
    gsems = [Sem(nc, f"sG{c}") for c in range(8)]
    GP.wait_ge(sDV.h, v_gcol)
    for c in range(8):
        GP.indirect_dma_start(out=wgr[:, c, :], out_offset=None, in_=wf_d,
                              in_offset=IndirectOffsetOnAxis(gcol[:, c:c + 1], 0)).then_inc(gsems[c].h, 16)
        gsems[c].v += 16
    for c in range(8):
        GP.wait_ge(gsems[c].h, 16)
        gp_dma(sWS, weff_d[c], wgr[:, c, :])
    v_weff = sWS.v
    sBF = Sem(nc, "sBF")
    vo(V.tensor_scalar, out=tmpc[0:2, 0:1], in0=gcolf[0:2, 0:1],
       scalar1=1.0 / 1024.0, scalar2=None, op0=ALU.mult)
    v_bfo = vo(V.tensor_copy, out=bfo[:], in_=tmpc[0:2, 0:1])
    GP.wait_ge(sDV.h, v_bfo)
    GP.indirect_dma_start(out=beff_b[:], out_offset=None, in_=bf_d,
                          in_offset=IndirectOffsetOnAxis(bfo[:], 0)).then_inc(sBF.h, 16)
    sBF.v += 16
    v_kvw = vo(V.memset, Vsb[:].rearrange("p t (h c) -> p (t h) c", c=65)[:, :, 64:65], 1.0)

    def kproj(sg, fm):
        PE.wait_ge(sDV.h, v_kvw)
        PE.wait_ge(sWL.h, v_wk)
        for e in range(8):
            def lf(c, e=e):
                return wkb[:, c, 128 * e:128 * e + 128]

            def rf(c, fm=fm):
                return fm[:, c, :]

            def ev(ps, e=e, sg=sg):
                vo(V.tensor_scalar, out=Ksb[:, e, 512 * sg:512 * sg + 512], in0=ps,
                   scalar1=bkc[:, e:e + 1], scalar2=None, op0=ALU.add)
            proj_mm(lf, rf, ev)

    def vproj(sg, fm):
        PE.wait_ge(sWL.h, v_wv)
        for sc in range(4):
            for eh in range(2):
                def lf(c, fm=fm, sc=sc):
                    return fm[:, c, 128 * sc:128 * sc + 128]

                def rf(c, eh=eh):
                    return wvb[:, c, 512 * eh:512 * eh + 512]

                def ev(ps, sg=sg, sc=sc, eh=eh):
                    tile = 4 * sg + sc
                    dst = Vsb[:, tile, :].rearrange("p (h c) -> p h c", c=65)[:, 8 * eh:8 * eh + 8, 0:64]
                    vo(V.tensor_tensor, out=dst, in0=ps.rearrange("p (h c) -> p h c", c=64),
                       in1=bvb[:, 512 * eh:512 * eh + 512].rearrange("p (h c) -> p h c", c=64),
                       op=ALU.add)
                proj_mm(lf, rf, ev)

    emit_path(xkv_d, KT, [kproj, vproj], "kv")

    V.wait_ge(sWS.h, v_weff)
    v_wq = load_wbf(wq_d, wkb)

    def qproj(sg, fm):
        PE.wait_ge(sWL.h, v_wq)
        for e in range(8):
            def lf(c, e=e):
                return wkb[:, c, 128 * e:128 * e + 128]

            def rf(c, fm=fm):
                return fm[:, c, :]

            def ev(ps, e=e, sg=sg):
                vo(V.tensor_scalar, out=Qsb[:, e, 512 * sg:512 * sg + 512], in0=ps,
                   scalar1=bqc[:, e:e + 1], scalar2=None, op0=ALU.add)
            proj_mm(lf, rf, ev)

    emit_path(xq_d, TQ, [qproj], "q")

    v_wp_wl = load_wbf(wp_d, wvb)
    wpb = wvb
    est_w.close()
    scr = sb("scr", [128, 4096], BF16)
    V.wait_ge(sPE.h, sPE.v)

    v_spp_read = sDV.v
    for hp in range(8):
        for src in [Qsb, Ksb]:
            n = TQ if src is Qsb else KT
            nwin = n // 128
            v_sq = vo(V.tensor_tensor, out=scr[:, 0:n], in0=src[:, hp, 0:n],
                      in1=src[:, hp, 0:n], op=ALU.mult)
            PE.wait_ge(sDV.h, max(v_sq, v_spp_read))
            last = None
            for w in range(nwin):
                last = PE.matmul(spp[:, w:w + 1], scr[:, 128 * w:128 * w + 128], onc[:],
                                 start=True, stop=True)
            v_pe1 = sPE.inc(last)
            V.wait_ge(sPE.h, v_pe1)
            v_rm = vo(V.reduce_max, out=tmpc[:], in_=spp[:, 0:nwin], axis=AX.X)
            PE.wait_ge(sDV.h, v_rm)
            v_pe2 = sPE.inc(PE.transpose(spp[0:1, 0:128], tmpc[:], id128[:]))
            V.wait_ge(sPE.h, v_pe2)
            cidx = hp if src is Qsb else 8 + hp
            v_spp_read = vo(V.reduce_max, out=crow[0:1, cidx:cidx + 1],
                            in_=spp[0:1, 0:128], axis=AX.X)
    vo(V.tensor_tensor, out=crow[0:1, 0:8], in0=crow[0:1, 0:8], in1=crow[0:1, 8:16], op=ALU.mult)
    v_c1 = vo(V.tensor_scalar, out=crow[0:1, 0:8], in0=crow[0:1, 0:8], scalar1=1e-12,
              scalar2=None, op0=ALU.max)
    SC.wait_ge(sDV.h, v_c1)
    so(SC.activation, out=crow[0:1, 0:8], in_=crow[0:1, 0:8], func=AF.Ln)
    v_c2 = so(SC.activation, out=crow[0:1, 0:8], in_=crow[0:1, 0:8], func=AF.Exp,
              scale=0.5, bias=cst[0:1, 1:2])
    V.wait_ge(sSC.h, v_c2)
    vo(V.tensor_scalar, out=crow[0:1, 0:8], in0=crow[0:1, 0:8], scalar1=0.5, scalar2=-1.0,
       op0=ALU.add, op1=ALU.mult)
    v_c3 = vo(V.tensor_copy, out=sbf[0:1, 0:8], in_=crow[0:1, 0:8])
    PE.wait_ge(sDV.h, v_c3)
    v_pe3 = sPE.inc(PE.matmul(spp[:, 0:8], on1b[:], sbf[0:1, 0:8], start=True, stop=True))
    V.wait_ge(sPE.h, v_pe3)
    v_negc = vo(V.tensor_copy, out=negc[:], in_=spp[:, 0:8])
    sps_last_dve[2] = v_negc
    V.wait_ge(sGA.h, v_small)
    for w in range(12):
        vo(V.tensor_scalar, out=negcP[:, w, :], in0=negc[:],
           scalar1=padbs[:, w:w + 1], scalar2=None, op0=ALU.add)
    v_negcP = sDV.v

    v_projs_done = sDV.v
    if "dbg_negc" in dbg:
        SY.wait_ge(sDV.h, v_negc)
        SY.dma_start(out=dbg["dbg_negc"], in_=negc[:]).then_inc(sOUT.h, 16); sOUT.v += 16
    if "dbg_q" in dbg:
        SY.wait_ge(sDV.h, v_projs_done)
        SY.dma_start(out=dbg["dbg_q"], in_=Qsb[:]).then_inc(sOUT.h, 16); sOUT.v += 16
    if "dbg_k" in dbg:
        SY.wait_ge(sDV.h, v_projs_done)
        SY.dma_start(out=dbg["dbg_k"], in_=Ksb[:]).then_inc(sOUT.h, 16); sOUT.v += 16
    if "dbg_v" in dbg:
        SY.wait_ge(sDV.h, v_projs_done)
        SY.dma_start(out=dbg["dbg_v"], in_=Vsb[:]).then_inc(sOUT.h, 16); sOUT.v += 16
    if "dbg_weff" in dbg:
        SY.wait_ge(sWS.h, v_weff)
        SY.dma_start(out=dbg["dbg_weff"], in_=weff_d).then_inc(sOUT.h, 16); sOUT.v += 16
    v_dbg_gate = [sOUT.v]

    nc.all_engine_barrier()
    mksb = sb("mksb", [128, 6, 512], BF16)
    Ynsb = sb("Ynsb", [128, 8, 512], BF16)
    zinvr = sb("zinvr", [1, 1024], BF16)
    wffb = sb("wffb", [128, 8, C], BF16)
    Yevb = sb("Yevb", [65, C], F32)
    tbuf = sb("tbuf", [128, C], F32)
    x1buf = sb("x1buf", [128, C], F32)
    ln3 = sb("ln3", [128, C], BF16)
    ln3T = sb("ln3t", [128, C], BF16)

    SY.wait_ge(sWS.h, v_weff)
    sy_dma(sIN, wffb[:], weff_d.rearrange("c p f -> p c f"))
    v_wffb = sIN.v

    Pv = scr[:].rearrange("p (r f) -> p r f", r=4)

    def yn_view(s):
        if s % 2 == 0:
            return Ynsb[:]
        return Qsb[:, :, 512 * (s - 1):512 * (s - 1) + 512]

    v_mask_free = 0
    v_pring = {"spe": 0}
    v_ya_free = [0]
    v_zinv_free_pe = [0]
    v_attn_start = sDV.v
    yn_ready = {}
    pend_epi = [None]
    pend_tail = [None]

    pend_ffn = [None]
    v_ffn_last = [0]

    def ln_p2(xin, xnout):
        vo(V.bn_stats, out=p2s[:, 0:6], in_=xin[:, 0:512])
        vo(V.bn_stats, out=p2s[:, 6:12], in_=xin[:, 512:1024])
        vo(V.bn_aggr, out=p2s[:, 12:14], in_=p2s[:, 0:12])
        v2 = vo(V.tensor_scalar, out=p2s[:, 14:15], in0=p2s[:, 12:13],
                scalar1=-1.0, scalar2=None, op0=ALU.mult)
        SC.wait_ge(sDV.h, v2)
        so(SC.activation, out=p2s[:, 15:16], in_=p2s[:, 13:14], func=AF.Ln,
           bias=cst[:, 0:1], scale=1.0)
        v3 = so(SC.activation, out=p2s[:, 15:16], in_=p2s[:, 15:16], func=AF.Exp, scale=-0.5)
        V.wait_ge(sSC.h, v3)
        return vo(V.tensor_scalar, out=xnout, in0=xin, scalar1=p2s[:, 14:15],
                  scalar2=p2s[:, 15:16], op0=ALU.add, op1=ALU.mult)

    v_xq_pref = [0]

    def emit_ffn_parcel(pref_row0):
        st = {}
        fillers = []
        pend = pend_ffn[0]
        pend_ffn[0] = None
        if pend is not None:
            row02, v_l3t2 = pend
            l3v = ln3T[:].rearrange("p (a f) -> p a f", a=8)

            def mmh(eh):
                def f():
                    if eh == 0:
                        PE.wait_ge(sTR.h, v_l3t2)
                        PE.wait_ge(sIN.h, v_wffb)
                        PE.wait_ge(sDV.h, sps_last_dve[2])
                        PE.wait_ge(sBF.h, 16)
                        if sps_last_exp[2]:
                            PE.wait_ge(sSC.h, sps_last_exp[2])
                    last = None
                    for cc in range(8):
                        last = PE.matmul(sps2[:, 512 * eh:512 * eh + 512], l3v[:, cc, :],
                                         wffb[:, cc, 512 * eh:512 * eh + 512],
                                         start=(cc == 0), stop=False)
                    last = PE.matmul(sps2[:, 512 * eh:512 * eh + 512], on1b[:],
                                     beff_b[0:1, 512 * eh:512 * eh + 512], start=False, stop=True)
                    st[eh] = sPE.inc(last)
                return f
            fillers = [mmh(0), mmh(1)]

        def tail():
            if pend is not None:
                v_ffn = st[1]
                v_ffn_last[0] = v_ffn
                V.wait_ge(sPE.h, v_ffn)
                v_ln4 = ln_p2(sps2[:], tbuf[:])
                sps_last_dve[2] = v_ln4
                v_outc = vo(V.tensor_tensor, out=x1buf[:], in0=x1buf[:], in1=tbuf[:], op=ALU.add)
                SY.wait_ge(sDV.h, v_outc)
                sy_dma(sOUT, out_d[row02:row02 + 128, :], x1buf[:])
            if pref_row0 is not None:
                sy_dma(sM2, tbuf[:], xq_d[pref_row0:pref_row0 + 128, :])
                v_xq_pref[0] = sM2.v
        return fillers, tail

    def emit_op_parcel(ssrc, k):
        Ysrc = yn_view(ssrc)
        row0 = 512 * ssrc + 128 * k
        st = {}

        def mmh(eh):
            def f():
                if eh == 0:
                    PE.wait_ge(sDV.h, max(yn_ready[ssrc], sps_last_dve[2]))
                    PE.wait_ge(sWL.h, v_wp_wl)
                    PE.wait_ge(sGA.h, v_small)
                    if sps_last_exp[2]:
                        PE.wait_ge(sSC.h, sps_last_exp[2])
                last = None
                for hp2 in range(8):
                    last = PE.matmul(sps2[:, 512 * eh:512 * eh + 512],
                                     Ysrc[:, hp2, 128 * k:128 * k + 128],
                                     wpb[:, hp2, 512 * eh:512 * eh + 512],
                                     start=(hp2 == 0), stop=False)
                last = PE.matmul(sps2[:, 512 * eh:512 * eh + 512], on1b[:],
                                 bpr_b[0:1, 512 * eh:512 * eh + 512], start=False, stop=True)
                st[eh] = sPE.inc(last)
            return f

        def tail():
            v_op = st[1]
            v_out_prev = sOUT.v
            V.wait_ge(sPE.h, v_op)
            V.wait_ge(sOUT.h, v_out_prev)
            v_ln2 = ln_p2(sps2[:], x1buf[:])
            sps_last_dve[2] = v_ln2
            V.wait_ge(sM2.h, v_xq_pref[0])
            vo(V.tensor_tensor, out=x1buf[:], in0=x1buf[:], in1=tbuf[:], op=ALU.add)
            v_ln3 = ln_p2(x1buf[:], ln3[:])
            SY.wait_ge(sDV.h, v_ln3)
            SY.wait_ge(sPE.h, v_ffn_last[0])
            sy_dma(sTR, ln3T[:].rearrange("p (a f) -> p a f", a=8), ln3[:], transpose=True)
            pend_ffn[0] = (row0, sTR.v)
        return [mmh(0), mmh(1)], tail

    for slot in range(NSLOT):
        L = LCH[slot]
        v_pool_slot = sPL.v
        if v_mask_free:
            GP.wait_ge(sPE.h, v_mask_free)
        gp_dma(sGA, mksb[:], mk_d[6 * slot:6 * slot + 6].rearrange("w p f -> p w f"))
        v_mk = sGA.v

        for hp in range(8):
            fillers, parcel_tail = [], None
            if slot >= 1:
                if hp % 2 == 0:
                    fillers, parcel_tail = emit_ffn_parcel(512 * (slot - 1) + 128 * (hp // 2))
                else:
                    fillers, parcel_tail = emit_op_parcel(slot - 1, hp // 2)

            v_exp = [0] * L
            v_pvc = [0] * L

            def emit_qk(c):
                ti = c % 2
                ps = sps[ti]
                msk = c >= L - 6
                PE.wait_ge(sDV.h, max(v_negcP, v_attn_start, sps_last_dve[ti]))
                if sps_last_exp[ti]:
                    PE.wait_ge(sSC.h, sps_last_exp[ti])
                PE.matmul(ps[:, 0:512], Ksb[0:64, hp, 128 * c:128 * c + 128],
                          Qsb[0:64, hp, 512 * slot:512 * slot + 512],
                          start=True, stop=not msk, tile_position=(0, 0))
                last = PE.matmul(ps[:, 512:1024], Ksb[64:128, hp, 128 * c:128 * c + 128],
                                 Qsb[64:128, hp, 512 * slot:512 * slot + 512],
                                 start=True, stop=not msk, tile_position=(64, 0))
                if msk:
                    j = c - (L - 6)
                    PE.wait_ge(sGA.h, v_mk)
                    PE.matmul(ps[:, 0:512], id128b[:], mksb[:, j, :],
                              start=False, stop=True)
                    last = PE.matmul(ps[:, 512:1024], id128b[:], mksb[:, j, :],
                                     start=False, stop=True)
                return sPE.inc(last)

            def emit_exp(c, v_qk):
                pr = c % 4
                SC.wait_ge(sPE.h, v_qk)
                if c >= 4:
                    SC.wait_ge(sPE.h, v_pvc[c - 4])
                else:
                    if v_pring["spe"]:
                        SC.wait_ge(sPE.h, v_pring["spe"])
                if c >= L - 3:
                    bias_ap = negcP[:, 3 * slot + (c - (L - 3)), hp:hp + 1]
                else:
                    bias_ap = negc[:, hp:hp + 1]
                v_exp[c] = so(SC.activation, out=Pv[:, pr, :], in_=sps[c % 2][:], func=AF.Exp,
                              bias=bias_ap, scale=0.125)
                sps_last_exp[c % 2] = v_exp[c]

            def emit_pv(c, hp=hp, L=L, v_exp=v_exp, v_pvc=v_pvc):
                pr = c % 4
                if c == 0 and v_ya_free[0]:
                    PE.wait_ge(sDV.h, v_ya_free[0])
                PE.wait_ge(sSC.h, v_exp[c])
                PE.matmul(ya[:], Vsb[:, c, 65 * (2 * hp):65 * (2 * hp) + 65],
                          Pv[:, pr, 0:512], start=(c == 0), stop=(c == L - 1))
                v_pvc[c] = sPE.inc(PE.matmul(yb[:], Vsb[:, c, 65 * (2 * hp + 1):65 * (2 * hp + 1) + 65],
                                             Pv[:, pr, 512:1024], start=(c == 0), stop=(c == L - 1)))

            pv_next = 0
            for c in range(L):
                v_qk = emit_qk(c)
                if c == 0 and pend_tail[0] is not None:
                    pend_tail[0]()
                    pend_tail[0] = None
                emit_exp(c, v_qk)
                if pend_epi[0] is not None:
                    if c == 1:
                        pend_epi[0][0]()
                    elif c == 2:
                        pend_epi[0][1]()
                        pend_epi[0] = None
                else:
                    n = 0
                    while pv_next <= min(c - 1, L - 3) and n < 2:
                        emit_pv(pv_next)
                        pv_next += 1
                        n += 1
                if c in (4, 6) and fillers:
                    fillers.pop(0)()
                if c == L - 2 and parcel_tail is not None and not fillers:
                    parcel_tail()
                    parcel_tail = None
            for f in fillers:
                f()
            if parcel_tail is not None:
                parcel_tail()

            def mk_tail(emit_pv_e, v_pvc_e, slot_e, hp_e, L_e):
                def tail():
                    emit_pv_e(L_e - 2)
                    emit_pv_e(L_e - 1)
                    V.wait_ge(sPE.h, v_pvc_e[L_e - 1])
                    vo(V.tensor_copy, out=Yevb[0:64, 0:512], in_=ya[0:64, :])
                    v_yev = vo(V.tensor_copy, out=Yevb[0:64, 512:1024], in_=yb[0:64, :])
                    v_pring["spe"] = v_pvc_e[L_e - 1]
                    pend_epi[0] = mk_epi(slot_e, hp_e, v_yev, v_pvc_e[L_e - 1])
                return tail

            def mk_epi(slot_e, hp_e, v_yev_e, v_pv_last):
                st = {}

                def emit_zchain():
                    SC.wait_ge(sPE.h, max(v_pv_last, v_zinv_free_pe[0]))
                    so(SC.activation, out=Yevb[64:65, 0:512], in_=ya[64:65, :], func=AF.Ln)
                    so(SC.activation, out=Yevb[64:65, 512:1024], in_=yb[64:65, :], func=AF.Ln)
                    v_zi = so(SC.activation, out=zinvr[:], in_=Yevb[64:65, :], func=AF.Exp, scale=-1.0)
                    st["v_zi"] = v_zi

                def emit_ynmul():
                    ynv_e = yn_view(slot_e)
                    PE.wait_ge(sSC.h, st["v_zi"])
                    PE.wait_ge(sDV.h, v_yev_e)
                    PE.matmul(ya[0:64, :], on1b[0:1, 0:64], zinvr[0:1, 0:512],
                              start=True, stop=True)
                    v_zbB = sPE.inc(PE.matmul(yb[0:64, :], on1b[0:1, 0:64],
                                              zinvr[0:1, 512:1024], start=True, stop=True))
                    v_zinv_free_pe[0] = v_zbB
                    V.wait_ge(sPE.h, v_zbB)
                    if dbg and v_dbg_gate[0]:
                        V.wait_ge(sOUT.h, v_dbg_gate[0])
                    vo(V.tensor_tensor, out=ynv_e[0:64, hp_e, :], in0=Yevb[0:64, 0:512],
                       in1=ya[0:64, :], op=ALU.mult)
                    v_done = vo(V.tensor_tensor, out=ynv_e[64:128, hp_e, :],
                                in0=Yevb[0:64, 512:1024], in1=yb[0:64, :], op=ALU.mult)
                    v_ya_free[0] = v_done
                    if hp_e == 7:
                        yn_ready[slot_e] = v_done
                        if "dbg_yn" in dbg:
                            SY.wait_ge(sDV.h, v_done)
                            SY.dma_start(out=dbg["dbg_yn"][:, :, 512 * slot_e:512 * slot_e + 512],
                                         in_=ynv_e).then_inc(sOUT.h, 16)
                            sOUT.v += 16
                            v_dbg_gate[0] = sOUT.v

                return emit_zchain, emit_ynmul

            pend_tail[0] = mk_tail(emit_pv, v_pvc, slot, hp, L)

        v_mask_free = sPE.v

    pend_tail[0]()
    pend_tail[0] = None
    pend_epi[0][0]()
    pend_epi[0][1]()
    pend_epi[0] = None

    for k in range(4):
        fl, tl = emit_ffn_parcel(512 * 3 + 128 * k)
        for f in fl:
            f()
        tl()
        fl, tl = emit_op_parcel(3, k)
        for f in fl:
            f()
        tl()
    fl, tl = emit_ffn_parcel(None)
    for f in fl:
        f()
    tl()

    SY.wait_ge(sOUT.h, sOUT.v)
    est.close()
    return nc


_NC_CACHE = {}


def _get_nc():
    if "nc" not in _NC_CACHE:
        _NC_CACHE["nc"] = build_nc()
    return _NC_CACHE["nc"]


def kernel(L0_attn_x, L0_attn_Wq, L0_attn_bq, L0_attn_Wk, L0_attn_bk,
           L0_attn_Wv, L0_attn_bv, L0_attn_Wp, L0_attn_bp,
           L0_ffn_W, L0_ffn_b, L0_ffn_logits, L0_ffn_u):
    x = np.ascontiguousarray(np.asarray(L0_attn_x, dtype=np.float32))
    wq = np.ascontiguousarray(np.asarray(L0_attn_Wq, dtype=np.float32))
    wk = np.ascontiguousarray(np.asarray(L0_attn_Wk, dtype=np.float32))
    wv = np.ascontiguousarray(np.asarray(L0_attn_Wv, dtype=np.float32))
    wp = np.ascontiguousarray(np.asarray(L0_attn_Wp, dtype=np.float32))
    bq = np.asarray(L0_attn_bq, dtype=np.float32)
    bk = np.asarray(L0_attn_bk, dtype=np.float32)
    bv = np.asarray(L0_attn_bv, dtype=np.float32)
    bp = np.asarray(L0_attn_bp, dtype=np.float32)
    wf = np.ascontiguousarray(np.asarray(L0_ffn_W, dtype=np.float32).reshape(8 * C, C))
    bf = np.ascontiguousarray(np.asarray(L0_ffn_b, dtype=np.float32))
    lg = np.asarray(L0_ffn_logits, dtype=np.float32)
    u = np.asarray(L0_ffn_u, dtype=np.float32)

    bq_c = np.ascontiguousarray(bq.reshape(8, 128).T)
    bk_c = np.ascontiguousarray(bk.reshape(8, 128).T)
    bv_r = np.ascontiguousarray(bv.reshape(1, C))
    bp_r = np.ascontiguousarray(bp.reshape(1, C))
    lu = np.ascontiguousarray(np.concatenate([lg, u]).reshape(1, 16))
    masks = [host_masks(0), host_masks(1)]
    padbs_h = [host_padb(0), host_padb(1)]

    in_maps = []
    for core in range(8):
        b, r = core // 2, core % 2
        xb = x[b]
        rows = np.concatenate([np.arange(512 * j, 512 * j + 512) for j in QB[r]])
        xq = np.ascontiguousarray(xb[rows])
        xkv = np.ascontiguousarray(xb.reshape(T // 4, 4, C)[:, 0:3, :].reshape(KT, C))
        in_maps.append({
            "xq": xq, "xkv": xkv, "wq": wq, "wk": wk, "wv": wv, "wp": wp,
            "bq": bq_c, "bk": bk_c, "bv": bv_r, "bp": bp_r,
            "wf": wf, "bf": bf, "lu": lu, "mk": masks[r], "padb": padbs_h[r],
        })

    nc = _get_nc()
    res = run_bass_kernel_spmd(nc, in_maps, core_ids=list(range(8)))

    out = np.empty((B, T, C), dtype=np.float32)
    for core in range(8):
        b, r = core // 2, core % 2
        o = res.results[core]["out"]
        for i, j in enumerate(QB[r]):
            out[b, 512 * j:512 * j + 512] = o[512 * i:512 * i + 512]
    return out


# version 19
# speedup vs baseline: 1.0066x; 1.0066x over previous
import contextlib
import numpy as np
import ml_dtypes
import concourse.bass as bass
import concourse.mybir as mybir
from concourse.bass_utils import run_bass_kernel_spmd
from concourse.bass_core_rust import IndirectOffsetOnAxis

F32 = mybir.dt.float32
BF16 = mybir.dt.bfloat16
I32 = mybir.dt.int32
AF = mybir.ActivationFunctionType
ALU = mybir.AluOpType
AX = mybir.AxisListType

B, T, C, H, D = 4, 4096, 1024, 16, 64
KT = 3072
NCH = KT // 128
QB = [[0, 3, 4, 7], [1, 2, 5, 6]]
LCH = [6, 12, 18, 24]
NSLOT = 4
TQ = 2048
EPS = 1e-5
LOG8 = float(np.log(8.0))

DEBUG_OUTS = []


def _orig(j):
    return 4 * (j // 3) + j % 3


def host_masks(r):
    m = np.zeros((24, 128, 512), dtype=np.float32)
    for i in range(NSLOT):
        L = LCH[i]
        t = 3 * QB[r][i] + 3
        for j in range(6):
            c = L - 6 + j
            w = 6 * i + j
            if c < t and c >= t - 3:
                k = c - (t - 3)
                s = np.arange(128)
                og = _orig(128 * k + s)
                m[w] = np.where(og[:, None] <= np.arange(512)[None, :], 0.0, -60000.0)
    return np.ascontiguousarray(m)


def host_padb(r):
    p = np.zeros((128, 12), dtype=np.float32)
    for i in range(NSLOT):
        L = LCH[i]
        t = 3 * QB[r][i] + 3
        for k in range(3):
            if L - 3 + k >= t:
                p[:, 3 * i + k] = -1e30
    return np.ascontiguousarray(p)


class Sem:
    def __init__(self, nc, name):
        self.h = nc.alloc_semaphore(name=name)
        self.v = 0

    def inc(self, inst, k=1):
        inst.then_inc(self.h, k)
        self.v += k
        return self.v


def build_nc():
    nc = bass.Bass("TRN2", target_bir_lowering=False)

    xq_d = nc.dram_tensor("xq", [TQ, C], F32, kind="ExternalInput").ap()
    xkv_d = nc.dram_tensor("xkv", [KT, C], F32, kind="ExternalInput").ap()
    wq_d = nc.dram_tensor("wq", [C, C], F32, kind="ExternalInput").ap()
    wk_d = nc.dram_tensor("wk", [C, C], F32, kind="ExternalInput").ap()
    wv_d = nc.dram_tensor("wv", [C, C], F32, kind="ExternalInput").ap()
    wp_d = nc.dram_tensor("wp", [C, C], F32, kind="ExternalInput").ap()
    bq_d = nc.dram_tensor("bq", [128, 8], F32, kind="ExternalInput").ap()
    bk_d = nc.dram_tensor("bk", [128, 8], F32, kind="ExternalInput").ap()
    bv_d = nc.dram_tensor("bv", [1, C], F32, kind="ExternalInput").ap()
    bp_d = nc.dram_tensor("bp", [1, C], F32, kind="ExternalInput").ap()
    wf_d = nc.dram_tensor("wf", [8 * C, C], F32, kind="ExternalInput").ap()
    bf_d = nc.dram_tensor("bf", [8, C], F32, kind="ExternalInput").ap()
    lu_d = nc.dram_tensor("lu", [1, 16], F32, kind="ExternalInput").ap()
    mk_d = nc.dram_tensor("mk", [24, 128, 512], F32, kind="ExternalInput").ap()
    pb_d = nc.dram_tensor("padb", [128, 12], F32, kind="ExternalInput").ap()
    out_d = nc.dram_tensor("out", [TQ, C], F32, kind="ExternalOutput").ap()
    weff_d = nc.dram_tensor("weff_scr", [8, 128, C], BF16, kind="Internal").ap()

    dbg = {}
    for name, shape, dt in [
        ("dbg_q", [128, 8, TQ], BF16), ("dbg_k", [128, 8, KT], BF16),
        ("dbg_v", [128, NCH, 1040], BF16), ("dbg_yn", [128, 8, 512 * NSLOT], BF16),
        ("dbg_negc", [128, 8], F32), ("dbg_weff", [8, 128, C], BF16),
    ]:
        if name in DEBUG_OUTS:
            dbg[name] = nc.dram_tensor(name, shape, dt, kind="ExternalOutput").ap()

    ones_1x128b = nc.inline_tensor(np.ones((1, 128), dtype=ml_dtypes.bfloat16), name="c_on1b").ap()
    ones_1x128f = nc.inline_tensor(np.ones((1, 128), dtype=np.float32), name="c_on1f").ap()
    ones_128x1b = nc.inline_tensor(np.ones((128, 1), dtype=ml_dtypes.bfloat16), name="c_onc").ap()
    iota8_c = nc.inline_tensor(np.arange(8, dtype=np.float32).reshape(1, 8), name="c_iota8").ap()
    id128_c = nc.inline_tensor(np.eye(128, dtype=np.float32), name="c_id128").ap()
    id128b_c = nc.inline_tensor(np.eye(128, dtype=ml_dtypes.bfloat16), name="c_id128b").ap()

    est = contextlib.ExitStack()

    def sb(name, shape, dt):
        return est.enter_context(nc.sbuf_tensor(name, shape, dt))

    Ksb = sb("Ksb", [128, 8, KT], BF16)
    Vsb = sb("Vsb", [128, NCH, 1040], BF16)
    Qsb = sb("Qsb", [128, 8, TQ], BF16)
    wvb = sb("wvb", [128, 8, C], BF16)
    bqc = sb("bqc", [128, 8], F32)
    bkc = sb("bkc", [128, 8], F32)
    bpr_b = sb("bprb", [1, C], BF16)
    beff_b = sb("beffb", [2, C], BF16)
    bfo = sb("bfo", [2, 1], I32)
    negc = sb("negc", [128, 8], F32)
    negcP = sb("negcP", [128, 12, 8], F32)
    padbs = sb("padbs", [128, 12], F32)
    on1b = sb("on1b", [1, 128], BF16)
    on1f = sb("on1f", [1, 128], F32)
    onc = sb("onc", [128, 1], BF16)
    id128 = sb("id128", [128, 128], F32)
    id128b = sb("idb", [128, 128], BF16)
    srow = sb("srow", [1, 64], F32)
    sbf = sb("sbf", [1, 8], BF16)
    crow = sb("crow", [1, 16], F32)
    lusb = sb("lusb", [1, 16], F32)
    gsc = sb("gsc", [1, 16], F32)
    idxi = sb("idxi", [1, 1], I32)
    gcol = sb("gcol", [128, 8], I32)
    gcol8f = sb("gcol8f", [128, 8], F32)
    iotac = sb("iotac", [128, 1], I32)
    iotacf = sb("iotacf", [128, 1], F32)
    gcolf = sb("gcolf", [128, 1], F32)
    tmpc = sb("tmpc", [128, 1], F32)
    lnst = sb("lnst", [128, 32], F32)
    cst = sb("cst", [128, 2], F32)
    p2s = sb("p2s", [128, 16], F32)

    sps0 = est.enter_context(nc.psum_tensor("sps0", [128, 1024], F32))
    sps1 = est.enter_context(nc.psum_tensor("sps1", [128, 1024], F32))
    sps2 = est.enter_context(nc.psum_tensor("sps2", [128, 1024], F32))
    ya = est.enter_context(nc.psum_tensor("ya", [65, 512], F32))
    yb = est.enter_context(nc.psum_tensor("yb", [65, 512], F32))
    sps = [sps0, sps1, sps2]
    spp = sps2[:, 0:512]

    sIN = Sem(nc, "sIN")
    sCN = Sem(nc, "sCN")
    sWL = Sem(nc, "sWL")
    sTR = Sem(nc, "sTR")
    sGA = Sem(nc, "sGA")
    sDV = Sem(nc, "sDV")
    sSC = Sem(nc, "sSC")
    sPE = Sem(nc, "sPE")
    sPL = Sem(nc, "sPL")
    sWS = Sem(nc, "sWS")
    sOUT = Sem(nc, "sOUT")
    sM2 = Sem(nc, "sM2")

    V = nc.vector
    SC = nc.scalar
    PE = nc.tensor
    GP = nc.gpsimd
    SY = nc.sync

    def vo(method, *a, **kw):
        V.wait_ge(sDV.h, sDV.v)
        return sDV.inc(method(*a, **kw))

    def so(method, *a, **kw):
        SC.wait_ge(sSC.h, sSC.v)
        return sSC.inc(method(*a, **kw))

    def po(method, *a, **kw):
        GP.wait_ge(sPL.h, sPL.v)
        return sPL.inc(method(*a, **kw))

    def sy_dma(sem, out, in_, **kw):
        SY.wait_ge(sem.h, sem.v)
        SY.dma_start(out=out, in_=in_, **kw).then_inc(sem.h, 16)
        sem.v += 16
        return sem.v

    def gp_dma(sem, out, in_, **kw):
        GP.wait_ge(sem.h, sem.v)
        GP.dma_start(out=out, in_=in_, **kw).then_inc(sem.h, 16)
        sem.v += 16
        return sem.v

    sps_last_exp = [0, 0, 0]
    sps_last_dve = [0, 0, 0]

    for dst, srcc in [(on1b, ones_1x128b), (on1f, ones_1x128f), (onc, ones_128x1b),
                      (id128, id128_c), (id128b, id128b_c), (bqc, bq_d), (bkc, bk_d)]:
        sy_dma(sCN, dst[:], srcc)
    v_consts = sCN.v
    gp_dma(sGA, lusb[:], lu_d)
    gp_dma(sGA, srow[0:1, 0:8], iota8_c)
    gp_dma(sGA, bpr_b[0:1, :], bp_d)
    gp_dma(sGA, padbs[:], pb_d)
    v_small = sGA.v
    po(GP.iota, iotac[:], pattern=[[0, 1]], base=0, channel_multiplier=1)
    v_iota = po(GP.tensor_copy, out=iotacf[:], in_=iotac[:])

    vo(V.memset, cst[:, 0:1], EPS)
    v_cst = vo(V.memset, cst[:, 1:2], -LOG8)

    SC.wait_ge(sGA.h, v_small)
    so(SC.activation, out=gsc[0:1, 0:8], in_=lusb[0:1, 8:16], func=AF.Ln)
    v_g = so(SC.activation, out=gsc[0:1, 8:16], in_=gsc[0:1, 0:8], func=AF.Ln, scale=-1.0)
    V.wait_ge(sSC.h, v_g)
    vo(V.tensor_tensor, out=gsc[0:1, 0:8], in0=lusb[0:1, 0:8], in1=gsc[0:1, 8:16], op=ALU.subtract)
    vo(V.reduce_max, out=srow[0:1, 26:27], in_=gsc[0:1, 0:8], axis=AX.X)
    vo(V.tensor_scalar, out=gsc[0:1, 8:16], in0=gsc[0:1, 0:8], scalar1=srow[0:1, 26:27],
       scalar2=None, op0=ALU.is_ge)
    vo(V.tensor_tensor, out=gsc[0:1, 8:16], in0=gsc[0:1, 8:16], in1=srow[0:1, 0:8], op=ALU.mult)
    vo(V.reduce_max, out=srow[0:1, 27:28], in_=gsc[0:1, 8:16], axis=AX.X)
    v_idxi = vo(V.tensor_copy, out=idxi[:], in_=srow[0:1, 27:28])

    vo(V.tensor_scalar, out=srow[0:1, 28:29], in0=srow[0:1, 27:28], scalar1=1024.0,
       scalar2=None, op0=ALU.mult)
    v_idxb = vo(V.tensor_copy, out=sbf[0:1, 0:1], in_=srow[0:1, 28:29])
    PE.wait_ge(sDV.h, v_idxb)
    PE.wait_ge(sCN.h, v_consts)
    v_pe_idx = sPE.inc(PE.matmul(spp[:, 0:1], on1b[:], sbf[0:1, 0:1], start=True, stop=True))
    V.wait_ge(sPE.h, v_pe_idx)
    vo(V.tensor_copy, out=gcolf[:], in_=spp[:, 0:1])
    V.wait_ge(sPL.h, v_iota)
    vo(V.tensor_tensor, out=tmpc[:], in0=iotacf[:], in1=gcolf[:], op=ALU.add)
    for c in range(8):
        vo(V.tensor_scalar, out=gcol8f[:, c:c + 1], in0=tmpc[:], scalar1=float(128 * c),
           scalar2=None, op0=ALU.add)
    v_gcol = vo(V.tensor_copy, out=gcol[:], in_=gcol8f[:])

    SC.wait_ge(sDV.h, v_cst)

    def load_wbf(wd, wbf, stage=None):
        GP.wait_ge(sPE.h, sPE.v)
        for c in range(8):
            gp_dma(sWL, wbf[:, c, :], wd[128 * c:128 * c + 128, :])
        return sWL.v

    ps_ctr = [0]
    ps_evac = [sps_last_dve[0], sps_last_dve[0], 0, 0, sps_last_dve[2], sps_last_dve[2]]

    def ps_view(i):
        return sps[i // 2][:, 512 * (i % 2):512 * (i % 2) + 512]

    def proj_mm(lhsT_fn, rhs_fn, evac_fn):
        i = ps_ctr[0] % 6
        ps = ps_view(i)
        ps_ctr[0] += 1
        if ps_evac[i]:
            PE.wait_ge(sDV.h, ps_evac[i])
        last = None
        for c in range(8):
            last = PE.matmul(ps, lhsT_fn(c), rhs_fn(c), start=(c == 0), stop=(c == 7))
        v_pe = sPE.inc(last)
        V.wait_ge(sPE.h, v_pe)
        evac_fn(ps)
        ps_evac[i] = sDV.v

    def emit_ln2t(xbuf, tokdst, base):
        ls = lnst[:].rearrange("p (t w) -> p t w", w=16)
        for t in range(2):
            vo(V.bn_stats, out=ls[:, t, 0:6], in_=xbuf[:, t, 0:512])
            vo(V.bn_stats, out=ls[:, t, 6:12], in_=xbuf[:, t, 512:1024])
            vo(V.bn_aggr, out=ls[:, t, 12:14], in_=ls[:, t, 0:12])
            vo(V.tensor_scalar, out=ls[:, t, 14:15], in0=ls[:, t, 12:13],
               scalar1=-1.0, scalar2=None, op0=ALU.mult)
        v_stats = sDV.v
        SC.wait_ge(sDV.h, v_stats)
        so(SC.activation, out=ls[:, :, 15:16], in_=ls[:, :, 13:14], func=AF.Ln,
           bias=cst[:, 0:1], scale=1.0)
        v_sq = so(SC.activation, out=ls[:, :, 15:16], in_=ls[:, :, 15:16], func=AF.Exp, scale=-0.5)
        V.wait_ge(sSC.h, v_sq)
        for t in range(2):
            vo(V.tensor_scalar, out=tokdst(base + t), in0=xbuf[:, t, :],
               scalar1=ls[:, t, 14:15], scalar2=ls[:, t, 15:16],
               op0=ALU.add, op1=ALU.mult)
        return sDV.v, v_sq

    def emit_path(xd, ntok, sg_fns, prefix):
        nsg = ntok // 512
        est_p = contextlib.ExitStack()
        xbuf = est_p.enter_context(nc.sbuf_tensor(prefix + "_xb", [128, 2, C], F32))
        tok4 = est_p.enter_context(nc.sbuf_tensor(prefix + "_tok", [128, 8, 4, 128], BF16))
        fm2 = est_p.enter_context(nc.sbuf_tensor(prefix + "_fm", [128, 2, 8, 512], BF16))
        nh = 2 * nsg
        v_loads = [0] * (nh + 1)
        for tt in range(2):
            sy_dma(sIN, xbuf[:, tt, :], xd[128 * tt:128 * tt + 128, :])
        v_loads[0] = sIN.v
        vtr_prev = 0
        v_pe_fm = 0

        def emit_ln_sg(sg):
            for h in range(2):
                g = 2 * sg + h
                V.wait_ge(sIN.h, v_loads[g])
                if vtr_prev and h == 0:
                    V.wait_ge(sTR.h, vtr_prev)
                SC.wait_ge(sIN.h, v_loads[g])
                v_ln, v_sq = emit_ln2t(xbuf, lambda tt: tok4[:, :, tt, :], 2 * h)
                if g + 1 < nh:
                    SY.wait_ge(sDV.h, v_ln)
                    for tt in range(2):
                        row0 = 256 * (g + 1) + 128 * tt
                        sy_dma(sIN, xbuf[:, tt, :], xd[row0:row0 + 128, :])
                    v_loads[g + 1] = sIN.v
            return sDV.v

        v_ln_done = emit_ln_sg(0)
        v_pe_fm2 = [0, 0]
        for sg in range(nsg):
            fm = fm2[:, sg % 2, :, :]
            SY.wait_ge(sDV.h, v_ln_done)
            if v_pe_fm2[sg % 2]:
                SY.wait_ge(sPE.h, v_pe_fm2[sg % 2])
            sy_dma(sTR, fm.rearrange("p c f -> p (c f)").rearrange("p (m g) -> p m g", g=128),
                   tok4[:].rearrange("p a b f -> p (a b f)"), transpose=True)
            vtr = sTR.v
            vtr_prev = vtr
            if sg + 1 < nsg:
                v_ln_done = emit_ln_sg(sg + 1)
            PE.wait_ge(sTR.h, vtr)
            for fn in sg_fns:
                fn(sg, fm)
            v_pe_fm2[sg % 2] = sPE.v
        est_p.close()

    est_w = contextlib.ExitStack()
    wkb = est_w.enter_context(nc.sbuf_tensor("wkb", [128, 8, C], BF16))
    bvb = est_w.enter_context(nc.sbuf_tensor("bvb", [128, C], BF16))
    gp_dma(sGA, bvb[0:1, :], bv_d)
    PE.wait_ge(sGA.h, sGA.v)
    PE.wait_ge(sDV.h, v_gcol)
    for half in range(2):
        v_pe_b = sPE.inc(PE.matmul(spp[:, 0:512], on1b[:], bvb[0:1, 512 * half:512 * half + 512],
                                   start=True, stop=True))
        V.wait_ge(sPE.h, v_pe_b)
        vo(V.tensor_copy, out=bvb[:, 512 * half:512 * half + 512], in_=spp[:, 0:512])
        PE.wait_ge(sDV.h, sDV.v)
    sps_last_dve[2] = sDV.v
    v_wk = load_wbf(wk_d, wkb)
    v_wv = load_wbf(wv_d, wvb)

    wgr = Qsb[:, :, 0:1024]
    gsems = [Sem(nc, f"sG{c}") for c in range(8)]
    GP.wait_ge(sDV.h, v_gcol)
    for c in range(8):
        GP.indirect_dma_start(out=wgr[:, c, :], out_offset=None, in_=wf_d,
                              in_offset=IndirectOffsetOnAxis(gcol[:, c:c + 1], 0)).then_inc(gsems[c].h, 16)
        gsems[c].v += 16
    for c in range(8):
        GP.wait_ge(gsems[c].h, 16)
        gp_dma(sWS, weff_d[c], wgr[:, c, :])
    v_weff = sWS.v
    sBF = Sem(nc, "sBF")
    vo(V.tensor_scalar, out=tmpc[0:2, 0:1], in0=gcolf[0:2, 0:1],
       scalar1=1.0 / 1024.0, scalar2=None, op0=ALU.mult)
    v_bfo = vo(V.tensor_copy, out=bfo[:], in_=tmpc[0:2, 0:1])
    GP.wait_ge(sDV.h, v_bfo)
    GP.indirect_dma_start(out=beff_b[:], out_offset=None, in_=bf_d,
                          in_offset=IndirectOffsetOnAxis(bfo[:], 0)).then_inc(sBF.h, 16)
    sBF.v += 16
    v_kvw = vo(V.memset, Vsb[:].rearrange("p t (h c) -> p (t h) c", c=65)[:, :, 64:65], 1.0)

    def kproj(sg, fm):
        PE.wait_ge(sDV.h, v_kvw)
        PE.wait_ge(sWL.h, v_wk)
        for e in range(8):
            def lf(c, e=e):
                return wkb[:, c, 128 * e:128 * e + 128]

            def rf(c, fm=fm):
                return fm[:, c, :]

            def ev(ps, e=e, sg=sg):
                vo(V.tensor_scalar, out=Ksb[:, e, 512 * sg:512 * sg + 512], in0=ps,
                   scalar1=bkc[:, e:e + 1], scalar2=None, op0=ALU.add)
            proj_mm(lf, rf, ev)

    def vproj(sg, fm):
        PE.wait_ge(sWL.h, v_wv)
        for sc in range(4):
            for eh in range(2):
                def lf(c, fm=fm, sc=sc):
                    return fm[:, c, 128 * sc:128 * sc + 128]

                def rf(c, eh=eh):
                    return wvb[:, c, 512 * eh:512 * eh + 512]

                def ev(ps, sg=sg, sc=sc, eh=eh):
                    tile = 4 * sg + sc
                    dst = Vsb[:, tile, :].rearrange("p (h c) -> p h c", c=65)[:, 8 * eh:8 * eh + 8, 0:64]
                    vo(V.tensor_tensor, out=dst, in0=ps.rearrange("p (h c) -> p h c", c=64),
                       in1=bvb[:, 512 * eh:512 * eh + 512].rearrange("p (h c) -> p h c", c=64),
                       op=ALU.add)
                proj_mm(lf, rf, ev)

    emit_path(xkv_d, KT, [kproj, vproj], "kv")

    V.wait_ge(sWS.h, v_weff)
    v_wq = load_wbf(wq_d, wkb)

    def qproj(sg, fm):
        PE.wait_ge(sWL.h, v_wq)
        for e in range(8):
            def lf(c, e=e):
                return wkb[:, c, 128 * e:128 * e + 128]

            def rf(c, fm=fm):
                return fm[:, c, :]

            def ev(ps, e=e, sg=sg):
                vo(V.tensor_scalar, out=Qsb[:, e, 512 * sg:512 * sg + 512], in0=ps,
                   scalar1=bqc[:, e:e + 1], scalar2=None, op0=ALU.add)
            proj_mm(lf, rf, ev)

    emit_path(xq_d, TQ, [qproj], "q")

    v_wp_wl = load_wbf(wp_d, wvb)
    wpb = wvb
    est_w.close()
    scr = sb("scr", [128, 4096], BF16)
    V.wait_ge(sPE.h, sPE.v)

    v_spp_read = sDV.v
    for hp in range(8):
        for src in [Qsb, Ksb]:
            n = TQ if src is Qsb else KT
            nwin = n // 128
            v_sq = vo(V.tensor_tensor, out=scr[:, 0:n], in0=src[:, hp, 0:n],
                      in1=src[:, hp, 0:n], op=ALU.mult)
            PE.wait_ge(sDV.h, max(v_sq, v_spp_read))
            last = None
            for w in range(nwin):
                last = PE.matmul(spp[:, w:w + 1], scr[:, 128 * w:128 * w + 128], onc[:],
                                 start=True, stop=True)
            v_pe1 = sPE.inc(last)
            V.wait_ge(sPE.h, v_pe1)
            v_rm = vo(V.reduce_max, out=tmpc[:], in_=spp[:, 0:nwin], axis=AX.X)
            PE.wait_ge(sDV.h, v_rm)
            v_pe2 = sPE.inc(PE.transpose(spp[0:1, 0:128], tmpc[:], id128[:]))
            V.wait_ge(sPE.h, v_pe2)
            cidx = hp if src is Qsb else 8 + hp
            v_spp_read = vo(V.reduce_max, out=crow[0:1, cidx:cidx + 1],
                            in_=spp[0:1, 0:128], axis=AX.X)
    vo(V.tensor_tensor, out=crow[0:1, 0:8], in0=crow[0:1, 0:8], in1=crow[0:1, 8:16], op=ALU.mult)
    v_c1 = vo(V.tensor_scalar, out=crow[0:1, 0:8], in0=crow[0:1, 0:8], scalar1=1e-12,
              scalar2=None, op0=ALU.max)
    SC.wait_ge(sDV.h, v_c1)
    so(SC.activation, out=crow[0:1, 0:8], in_=crow[0:1, 0:8], func=AF.Ln)
    v_c2 = so(SC.activation, out=crow[0:1, 0:8], in_=crow[0:1, 0:8], func=AF.Exp,
              scale=0.5, bias=cst[0:1, 1:2])
    V.wait_ge(sSC.h, v_c2)
    vo(V.tensor_scalar, out=crow[0:1, 0:8], in0=crow[0:1, 0:8], scalar1=0.5, scalar2=-1.0,
       op0=ALU.add, op1=ALU.mult)
    v_c3 = vo(V.tensor_copy, out=sbf[0:1, 0:8], in_=crow[0:1, 0:8])
    PE.wait_ge(sDV.h, v_c3)
    v_pe3 = sPE.inc(PE.matmul(spp[:, 0:8], on1b[:], sbf[0:1, 0:8], start=True, stop=True))
    V.wait_ge(sPE.h, v_pe3)
    v_negc = vo(V.tensor_copy, out=negc[:], in_=spp[:, 0:8])
    sps_last_dve[2] = v_negc
    V.wait_ge(sGA.h, v_small)
    for w in range(12):
        vo(V.tensor_scalar, out=negcP[:, w, :], in0=negc[:],
           scalar1=padbs[:, w:w + 1], scalar2=None, op0=ALU.add)
    v_negcP = sDV.v

    v_projs_done = sDV.v
    if "dbg_negc" in dbg:
        SY.wait_ge(sDV.h, v_negc)
        SY.dma_start(out=dbg["dbg_negc"], in_=negc[:]).then_inc(sOUT.h, 16); sOUT.v += 16
    if "dbg_q" in dbg:
        SY.wait_ge(sDV.h, v_projs_done)
        SY.dma_start(out=dbg["dbg_q"], in_=Qsb[:]).then_inc(sOUT.h, 16); sOUT.v += 16
    if "dbg_k" in dbg:
        SY.wait_ge(sDV.h, v_projs_done)
        SY.dma_start(out=dbg["dbg_k"], in_=Ksb[:]).then_inc(sOUT.h, 16); sOUT.v += 16
    if "dbg_v" in dbg:
        SY.wait_ge(sDV.h, v_projs_done)
        SY.dma_start(out=dbg["dbg_v"], in_=Vsb[:]).then_inc(sOUT.h, 16); sOUT.v += 16
    if "dbg_weff" in dbg:
        SY.wait_ge(sWS.h, v_weff)
        SY.dma_start(out=dbg["dbg_weff"], in_=weff_d).then_inc(sOUT.h, 16); sOUT.v += 16
    v_dbg_gate = [sOUT.v]

    nc.all_engine_barrier()
    mksb = sb("mksb", [128, 6, 512], BF16)
    Ynsb = sb("Ynsb", [128, 8, 512], BF16)
    zinvr = sb("zinvr", [1, 1024], BF16)
    wffb = sb("wffb", [128, 8, C], BF16)
    Yevb = sb("Yevb", [65, C], F32)
    tbuf = sb("tbuf", [128, C], F32)
    x1buf = sb("x1buf", [128, C], F32)
    ln3 = sb("ln3", [128, C], BF16)
    ln3T = sb("ln3t", [128, C], BF16)

    SY.wait_ge(sWS.h, v_weff)
    sy_dma(sIN, wffb[:], weff_d.rearrange("c p f -> p c f"))
    v_wffb = sIN.v

    Pv = scr[:].rearrange("p (r f) -> p r f", r=4)

    def yn_view(s):
        if s % 2 == 0:
            return Ynsb[:]
        return Qsb[:, :, 512 * (s - 1):512 * (s - 1) + 512]

    v_mask_free = 0
    v_pring = {"spe": 0}
    v_ya_free = [0]
    v_zinv_free_pe = [0]
    v_attn_start = sDV.v
    yn_ready = {}
    pend_epi = [None]
    pend_tail = [None]

    pend_ffn = [None]
    v_ffn_last = [0]

    def ln_p2(xin, xnout):
        vo(V.bn_stats, out=p2s[:, 0:6], in_=xin[:, 0:512])
        vo(V.bn_stats, out=p2s[:, 6:12], in_=xin[:, 512:1024])
        vo(V.bn_aggr, out=p2s[:, 12:14], in_=p2s[:, 0:12])
        v2 = vo(V.tensor_scalar, out=p2s[:, 14:15], in0=p2s[:, 12:13],
                scalar1=-1.0, scalar2=None, op0=ALU.mult)
        SC.wait_ge(sDV.h, v2)
        so(SC.activation, out=p2s[:, 15:16], in_=p2s[:, 13:14], func=AF.Ln,
           bias=cst[:, 0:1], scale=1.0)
        v3 = so(SC.activation, out=p2s[:, 15:16], in_=p2s[:, 15:16], func=AF.Exp, scale=-0.5)
        V.wait_ge(sSC.h, v3)
        return vo(V.tensor_scalar, out=xnout, in0=xin, scalar1=p2s[:, 14:15],
                  scalar2=p2s[:, 15:16], op0=ALU.add, op1=ALU.mult)

    v_xq_pref = [0]

    def emit_ffn_parcel(pref_row0):
        st = {}
        fillers = []
        pend = pend_ffn[0]
        pend_ffn[0] = None
        if pend is not None:
            row02, v_l3t2 = pend
            l3v = ln3T[:].rearrange("p (a f) -> p a f", a=8)

            def mmh(eh):
                def f():
                    if eh == 0:
                        PE.wait_ge(sTR.h, v_l3t2)
                        PE.wait_ge(sIN.h, v_wffb)
                        PE.wait_ge(sDV.h, sps_last_dve[2])
                        PE.wait_ge(sBF.h, 16)
                        if sps_last_exp[2]:
                            PE.wait_ge(sSC.h, sps_last_exp[2])
                    last = None
                    for cc in range(8):
                        last = PE.matmul(sps2[:, 512 * eh:512 * eh + 512], l3v[:, cc, :],
                                         wffb[:, cc, 512 * eh:512 * eh + 512],
                                         start=(cc == 0), stop=False)
                    last = PE.matmul(sps2[:, 512 * eh:512 * eh + 512], on1b[:],
                                     beff_b[0:1, 512 * eh:512 * eh + 512], start=False, stop=True)
                    st[eh] = sPE.inc(last)
                return f
            fillers = [mmh(0), mmh(1)]

        def tail():
            if pend is not None:
                v_ffn = st[1]
                v_ffn_last[0] = v_ffn
                V.wait_ge(sPE.h, v_ffn)
                v_ln4 = ln_p2(sps2[:], tbuf[:])
                sps_last_dve[2] = v_ln4
                v_outc = vo(V.tensor_tensor, out=x1buf[:], in0=x1buf[:], in1=tbuf[:], op=ALU.add)
                SY.wait_ge(sDV.h, v_outc)
                sy_dma(sOUT, out_d[row02:row02 + 128, :], x1buf[:])
            if pref_row0 is not None:
                sy_dma(sM2, tbuf[:], xq_d[pref_row0:pref_row0 + 128, :])
                v_xq_pref[0] = sM2.v
        return fillers, tail

    def emit_op_parcel(ssrc, k):
        Ysrc = yn_view(ssrc)
        row0 = 512 * ssrc + 128 * k
        st = {}

        def mmh(eh):
            def f():
                if eh == 0:
                    PE.wait_ge(sDV.h, max(yn_ready[ssrc], sps_last_dve[2]))
                    PE.wait_ge(sWL.h, v_wp_wl)
                    PE.wait_ge(sGA.h, v_small)
                    if sps_last_exp[2]:
                        PE.wait_ge(sSC.h, sps_last_exp[2])
                last = None
                for hp2 in range(8):
                    last = PE.matmul(sps2[:, 512 * eh:512 * eh + 512],
                                     Ysrc[:, hp2, 128 * k:128 * k + 128],
                                     wpb[:, hp2, 512 * eh:512 * eh + 512],
                                     start=(hp2 == 0), stop=False)
                last = PE.matmul(sps2[:, 512 * eh:512 * eh + 512], on1b[:],
                                 bpr_b[0:1, 512 * eh:512 * eh + 512], start=False, stop=True)
                st[eh] = sPE.inc(last)
            return f

        def tail():
            v_op = st[1]
            v_out_prev = sOUT.v
            V.wait_ge(sPE.h, v_op)
            V.wait_ge(sOUT.h, v_out_prev)
            v_ln2 = ln_p2(sps2[:], x1buf[:])
            sps_last_dve[2] = v_ln2
            V.wait_ge(sM2.h, v_xq_pref[0])
            vo(V.tensor_tensor, out=x1buf[:], in0=x1buf[:], in1=tbuf[:], op=ALU.add)
            v_ln3 = ln_p2(x1buf[:], ln3[:])
            SY.wait_ge(sDV.h, v_ln3)
            SY.wait_ge(sPE.h, v_ffn_last[0])
            sy_dma(sTR, ln3T[:].rearrange("p (a f) -> p a f", a=8), ln3[:], transpose=True)
            pend_ffn[0] = (row0, sTR.v)
        return [mmh(0), mmh(1)], tail

    for slot in range(NSLOT):
        L = LCH[slot]
        v_pool_slot = sPL.v
        if v_mask_free:
            GP.wait_ge(sPE.h, v_mask_free)
        gp_dma(sGA, mksb[:], mk_d[6 * slot:6 * slot + 6].rearrange("w p f -> p w f"))
        v_mk = sGA.v

        for hp in range(8):
            fillers, parcel_tail = [], None
            if slot >= 1:
                if hp % 2 == 0:
                    fillers, parcel_tail = emit_ffn_parcel(512 * (slot - 1) + 128 * (hp // 2))
                else:
                    fillers, parcel_tail = emit_op_parcel(slot - 1, hp // 2)

            v_exp = [0] * L
            v_pvc = [0] * L

            def emit_qk(c):
                ti = c % 2
                ps = sps[ti]
                msk = c >= L - 6
                PE.wait_ge(sDV.h, max(v_negcP, v_attn_start, sps_last_dve[ti]))
                if sps_last_exp[ti]:
                    PE.wait_ge(sSC.h, sps_last_exp[ti])
                PE.matmul(ps[:, 0:512], Ksb[0:64, hp, 128 * c:128 * c + 128],
                          Qsb[0:64, hp, 512 * slot:512 * slot + 512],
                          start=True, stop=not msk, tile_position=(0, 0))
                last = PE.matmul(ps[:, 512:1024], Ksb[64:128, hp, 128 * c:128 * c + 128],
                                 Qsb[64:128, hp, 512 * slot:512 * slot + 512],
                                 start=True, stop=not msk, tile_position=(64, 0))
                if msk:
                    j = c - (L - 6)
                    PE.wait_ge(sGA.h, v_mk)
                    PE.matmul(ps[:, 0:512], id128b[:], mksb[:, j, :],
                              start=False, stop=True)
                    last = PE.matmul(ps[:, 512:1024], id128b[:], mksb[:, j, :],
                                     start=False, stop=True)
                return sPE.inc(last)

            def emit_exp(c, v_qk):
                pr = c % 4
                SC.wait_ge(sPE.h, v_qk)
                if c >= 4:
                    SC.wait_ge(sPE.h, v_pvc[c - 4])
                else:
                    if v_pring["spe"]:
                        SC.wait_ge(sPE.h, v_pring["spe"])
                if c >= L - 3:
                    bias_ap = negcP[:, 3 * slot + (c - (L - 3)), hp:hp + 1]
                else:
                    bias_ap = negc[:, hp:hp + 1]
                v_exp[c] = so(SC.activation, out=Pv[:, pr, :], in_=sps[c % 2][:], func=AF.Exp,
                              bias=bias_ap, scale=0.125)
                sps_last_exp[c % 2] = v_exp[c]

            def emit_pv(c, hp=hp, L=L, v_exp=v_exp, v_pvc=v_pvc):
                pr = c % 4
                if c == 0 and v_ya_free[0]:
                    PE.wait_ge(sDV.h, v_ya_free[0])
                PE.wait_ge(sSC.h, v_exp[c])
                PE.matmul(ya[:], Vsb[:, c, 65 * (2 * hp):65 * (2 * hp) + 65],
                          Pv[:, pr, 0:512], start=(c == 0), stop=(c == L - 1))
                v_pvc[c] = sPE.inc(PE.matmul(yb[:], Vsb[:, c, 65 * (2 * hp + 1):65 * (2 * hp + 1) + 65],
                                             Pv[:, pr, 512:1024], start=(c == 0), stop=(c == L - 1)))

            pv_next = 0
            for c in range(L):
                v_qk = emit_qk(c)
                if c == 0 and pend_tail[0] is not None:
                    pend_tail[0]()
                    pend_tail[0] = None
                emit_exp(c, v_qk)
                if pend_epi[0] is not None:
                    if c == 1:
                        pend_epi[0][0]()
                    elif c == 2:
                        pend_epi[0][1]()
                        pend_epi[0] = None
                else:
                    n = 0
                    while pv_next <= min(c - 1, L - 3) and n < 2:
                        emit_pv(pv_next)
                        pv_next += 1
                        n += 1
                if c in (4, 6) and fillers:
                    fillers.pop(0)()
            for f in fillers:
                f()
            if parcel_tail is not None:
                parcel_tail()

            def mk_tail(emit_pv_e, v_pvc_e, slot_e, hp_e, L_e):
                def tail():
                    emit_pv_e(L_e - 2)
                    emit_pv_e(L_e - 1)
                    V.wait_ge(sPE.h, v_pvc_e[L_e - 1])
                    vo(V.tensor_copy, out=Yevb[0:64, 0:512], in_=ya[0:64, :])
                    v_yev = vo(V.tensor_copy, out=Yevb[0:64, 512:1024], in_=yb[0:64, :])
                    v_pring["spe"] = v_pvc_e[L_e - 1]
                    pend_epi[0] = mk_epi(slot_e, hp_e, v_yev, v_pvc_e[L_e - 1])
                return tail

            def mk_epi(slot_e, hp_e, v_yev_e, v_pv_last):
                st = {}

                def emit_zchain():
                    SC.wait_ge(sPE.h, max(v_pv_last, v_zinv_free_pe[0]))
                    so(SC.activation, out=Yevb[64:65, 0:512], in_=ya[64:65, :], func=AF.Ln)
                    so(SC.activation, out=Yevb[64:65, 512:1024], in_=yb[64:65, :], func=AF.Ln)
                    v_zi = so(SC.activation, out=zinvr[:], in_=Yevb[64:65, :], func=AF.Exp, scale=-1.0)
                    st["v_zi"] = v_zi

                def emit_ynmul():
                    ynv_e = yn_view(slot_e)
                    PE.wait_ge(sSC.h, st["v_zi"])
                    PE.wait_ge(sDV.h, v_yev_e)
                    PE.matmul(ya[0:64, :], on1b[0:1, 0:64], zinvr[0:1, 0:512],
                              start=True, stop=True)
                    v_zbB = sPE.inc(PE.matmul(yb[0:64, :], on1b[0:1, 0:64],
                                              zinvr[0:1, 512:1024], start=True, stop=True))
                    v_zinv_free_pe[0] = v_zbB
                    V.wait_ge(sPE.h, v_zbB)
                    if dbg and v_dbg_gate[0]:
                        V.wait_ge(sOUT.h, v_dbg_gate[0])
                    vo(V.tensor_tensor, out=ynv_e[0:64, hp_e, :], in0=Yevb[0:64, 0:512],
                       in1=ya[0:64, :], op=ALU.mult)
                    v_done = vo(V.tensor_tensor, out=ynv_e[64:128, hp_e, :],
                                in0=Yevb[0:64, 512:1024], in1=yb[0:64, :], op=ALU.mult)
                    v_ya_free[0] = v_done
                    if hp_e == 7:
                        yn_ready[slot_e] = v_done
                        if "dbg_yn" in dbg:
                            SY.wait_ge(sDV.h, v_done)
                            SY.dma_start(out=dbg["dbg_yn"][:, :, 512 * slot_e:512 * slot_e + 512],
                                         in_=ynv_e).then_inc(sOUT.h, 16)
                            sOUT.v += 16
                            v_dbg_gate[0] = sOUT.v

                return emit_zchain, emit_ynmul

            pend_tail[0] = mk_tail(emit_pv, v_pvc, slot, hp, L)

        v_mask_free = sPE.v

    pend_tail[0]()
    pend_tail[0] = None
    pend_epi[0][0]()
    pend_epi[0][1]()
    pend_epi[0] = None

    for k in range(4):
        fl, tl = emit_ffn_parcel(512 * 3 + 128 * k)
        for f in fl:
            f()
        tl()
        fl, tl = emit_op_parcel(3, k)
        for f in fl:
            f()
        tl()
    fl, tl = emit_ffn_parcel(None)
    for f in fl:
        f()
    tl()

    SY.wait_ge(sOUT.h, sOUT.v)
    est.close()
    return nc


_NC_CACHE = {}


def _get_nc():
    if "nc" not in _NC_CACHE:
        _NC_CACHE["nc"] = build_nc()
    return _NC_CACHE["nc"]


def kernel(L0_attn_x, L0_attn_Wq, L0_attn_bq, L0_attn_Wk, L0_attn_bk,
           L0_attn_Wv, L0_attn_bv, L0_attn_Wp, L0_attn_bp,
           L0_ffn_W, L0_ffn_b, L0_ffn_logits, L0_ffn_u):
    x = np.ascontiguousarray(np.asarray(L0_attn_x, dtype=np.float32))
    wq = np.ascontiguousarray(np.asarray(L0_attn_Wq, dtype=np.float32))
    wk = np.ascontiguousarray(np.asarray(L0_attn_Wk, dtype=np.float32))
    wv = np.ascontiguousarray(np.asarray(L0_attn_Wv, dtype=np.float32))
    wp = np.ascontiguousarray(np.asarray(L0_attn_Wp, dtype=np.float32))
    bq = np.asarray(L0_attn_bq, dtype=np.float32)
    bk = np.asarray(L0_attn_bk, dtype=np.float32)
    bv = np.asarray(L0_attn_bv, dtype=np.float32)
    bp = np.asarray(L0_attn_bp, dtype=np.float32)
    wf = np.ascontiguousarray(np.asarray(L0_ffn_W, dtype=np.float32).reshape(8 * C, C))
    bf = np.ascontiguousarray(np.asarray(L0_ffn_b, dtype=np.float32))
    lg = np.asarray(L0_ffn_logits, dtype=np.float32)
    u = np.asarray(L0_ffn_u, dtype=np.float32)

    bq_c = np.ascontiguousarray(bq.reshape(8, 128).T)
    bk_c = np.ascontiguousarray(bk.reshape(8, 128).T)
    bv_r = np.ascontiguousarray(bv.reshape(1, C))
    bp_r = np.ascontiguousarray(bp.reshape(1, C))
    lu = np.ascontiguousarray(np.concatenate([lg, u]).reshape(1, 16))
    masks = [host_masks(0), host_masks(1)]
    padbs_h = [host_padb(0), host_padb(1)]

    in_maps = []
    for core in range(8):
        b, r = core // 2, core % 2
        xb = x[b]
        rows = np.concatenate([np.arange(512 * j, 512 * j + 512) for j in QB[r]])
        xq = np.ascontiguousarray(xb[rows])
        xkv = np.ascontiguousarray(xb.reshape(T // 4, 4, C)[:, 0:3, :].reshape(KT, C))
        in_maps.append({
            "xq": xq, "xkv": xkv, "wq": wq, "wk": wk, "wv": wv, "wp": wp,
            "bq": bq_c, "bk": bk_c, "bv": bv_r, "bp": bp_r,
            "wf": wf, "bf": bf, "lu": lu, "mk": masks[r], "padb": padbs_h[r],
        })

    nc = _get_nc()
    res = run_bass_kernel_spmd(nc, in_maps, core_ids=list(range(8)))

    out = np.empty((B, T, C), dtype=np.float32)
    for core in range(8):
        b, r = core // 2, core % 2
        o = res.results[core]["out"]
        for i, j in enumerate(QB[r]):
            out[b, 512 * j:512 * j + 512] = o[512 * i:512 * i + 512]
    return out


# version 20
# speedup vs baseline: 1.0175x; 1.0108x over previous
import contextlib
import numpy as np
import ml_dtypes
import concourse.bass as bass
import concourse.mybir as mybir
from concourse.bass_utils import run_bass_kernel_spmd
from concourse.bass_core_rust import IndirectOffsetOnAxis

F32 = mybir.dt.float32
BF16 = mybir.dt.bfloat16
I32 = mybir.dt.int32
AF = mybir.ActivationFunctionType
ALU = mybir.AluOpType
AX = mybir.AxisListType

B, T, C, H, D = 4, 4096, 1024, 16, 64
KT = 3072
NCH = KT // 128
QB = [[0, 3, 4, 7], [1, 2, 5, 6]]
LCH = [6, 12, 18, 24]
NSLOT = 4
TQ = 2048
EPS = 1e-5
LOG8 = float(np.log(8.0))

DEBUG_OUTS = []


def _orig(j):
    return 4 * (j // 3) + j % 3


def host_masks(r):
    m = np.zeros((24, 128, 512), dtype=np.float32)
    for i in range(NSLOT):
        L = LCH[i]
        t = 3 * QB[r][i] + 3
        for j in range(6):
            c = L - 6 + j
            w = 6 * i + j
            if c < t and c >= t - 3:
                k = c - (t - 3)
                s = np.arange(128)
                og = _orig(128 * k + s)
                m[w] = np.where(og[:, None] <= np.arange(512)[None, :], 0.0, -60000.0)
    return np.ascontiguousarray(m)


def host_padb(r):
    p = np.zeros((128, 12), dtype=np.float32)
    for i in range(NSLOT):
        L = LCH[i]
        t = 3 * QB[r][i] + 3
        for k in range(3):
            if L - 3 + k >= t:
                p[:, 3 * i + k] = -1e30
    return np.ascontiguousarray(p)


class Sem:
    def __init__(self, nc, name):
        self.h = nc.alloc_semaphore(name=name)
        self.v = 0

    def inc(self, inst, k=1):
        inst.then_inc(self.h, k)
        self.v += k
        return self.v


def build_nc():
    nc = bass.Bass("TRN2", target_bir_lowering=False)

    xq_d = nc.dram_tensor("xq", [TQ, C], F32, kind="ExternalInput").ap()
    xkv_d = nc.dram_tensor("xkv", [KT, C], F32, kind="ExternalInput").ap()
    wq_d = nc.dram_tensor("wq", [C, C], F32, kind="ExternalInput").ap()
    wk_d = nc.dram_tensor("wk", [C, C], F32, kind="ExternalInput").ap()
    wv_d = nc.dram_tensor("wv", [C, C], F32, kind="ExternalInput").ap()
    wp_d = nc.dram_tensor("wp", [C, C], F32, kind="ExternalInput").ap()
    bq_d = nc.dram_tensor("bq", [128, 8], F32, kind="ExternalInput").ap()
    bk_d = nc.dram_tensor("bk", [128, 8], F32, kind="ExternalInput").ap()
    bv_d = nc.dram_tensor("bv", [1, C], F32, kind="ExternalInput").ap()
    bp_d = nc.dram_tensor("bp", [1, C], F32, kind="ExternalInput").ap()
    wf_d = nc.dram_tensor("wf", [8 * C, C], F32, kind="ExternalInput").ap()
    bf_d = nc.dram_tensor("bf", [8, C], F32, kind="ExternalInput").ap()
    lu_d = nc.dram_tensor("lu", [1, 16], F32, kind="ExternalInput").ap()
    mk_d = nc.dram_tensor("mk", [24, 128, 512], F32, kind="ExternalInput").ap()
    pb_d = nc.dram_tensor("padb", [128, 12], F32, kind="ExternalInput").ap()
    out_d = nc.dram_tensor("out", [TQ, C], F32, kind="ExternalOutput").ap()
    weff_d = nc.dram_tensor("weff_scr", [8, 128, C], BF16, kind="Internal").ap()

    dbg = {}
    for name, shape, dt in [
        ("dbg_q", [128, 8, TQ], BF16), ("dbg_k", [128, 8, KT], BF16),
        ("dbg_v", [128, NCH, 1040], BF16), ("dbg_yn", [128, 8, 512 * NSLOT], BF16),
        ("dbg_negc", [128, 8], F32), ("dbg_weff", [8, 128, C], BF16),
    ]:
        if name in DEBUG_OUTS:
            dbg[name] = nc.dram_tensor(name, shape, dt, kind="ExternalOutput").ap()

    ones_1x128b = nc.inline_tensor(np.ones((1, 128), dtype=ml_dtypes.bfloat16), name="c_on1b").ap()
    ones_1x128f = nc.inline_tensor(np.ones((1, 128), dtype=np.float32), name="c_on1f").ap()
    ones_128x1b = nc.inline_tensor(np.ones((128, 1), dtype=ml_dtypes.bfloat16), name="c_onc").ap()
    iota8_c = nc.inline_tensor(np.arange(8, dtype=np.float32).reshape(1, 8), name="c_iota8").ap()
    id128_c = nc.inline_tensor(np.eye(128, dtype=np.float32), name="c_id128").ap()
    id128b_c = nc.inline_tensor(np.eye(128, dtype=ml_dtypes.bfloat16), name="c_id128b").ap()

    est = contextlib.ExitStack()

    def sb(name, shape, dt):
        return est.enter_context(nc.sbuf_tensor(name, shape, dt))

    Ksb = sb("Ksb", [128, 8, KT], BF16)
    Vsb = sb("Vsb", [128, NCH, 1040], BF16)
    Qsb = sb("Qsb", [128, 8, TQ], BF16)
    wvb = sb("wvb", [128, 8, C], BF16)
    bqc = sb("bqc", [128, 8], F32)
    bkc = sb("bkc", [128, 8], F32)
    bpr_b = sb("bprb", [1, C], BF16)
    beff_b = sb("beffb", [2, C], BF16)
    bfo = sb("bfo", [2, 1], I32)
    negc = sb("negc", [128, 8], F32)
    negcP = sb("negcP", [128, 12, 8], F32)
    padbs = sb("padbs", [128, 12], F32)
    on1b = sb("on1b", [1, 128], BF16)
    on1f = sb("on1f", [1, 128], F32)
    onc = sb("onc", [128, 1], BF16)
    id128 = sb("id128", [128, 128], F32)
    id128b = sb("idb", [128, 128], BF16)
    srow = sb("srow", [1, 64], F32)
    sbf = sb("sbf", [1, 8], BF16)
    crow = sb("crow", [1, 16], F32)
    lusb = sb("lusb", [1, 16], F32)
    gsc = sb("gsc", [1, 16], F32)
    idxi = sb("idxi", [1, 1], I32)
    gcol = sb("gcol", [128, 8], I32)
    gcol8f = sb("gcol8f", [128, 8], F32)
    iotac = sb("iotac", [128, 1], I32)
    iotacf = sb("iotacf", [128, 1], F32)
    gcolf = sb("gcolf", [128, 1], F32)
    tmpc = sb("tmpc", [128, 1], F32)
    lnst = sb("lnst", [128, 32], F32)
    cst = sb("cst", [128, 2], F32)
    p2s = sb("p2s", [128, 16], F32)

    sps0 = est.enter_context(nc.psum_tensor("sps0", [128, 1024], F32))
    sps1 = est.enter_context(nc.psum_tensor("sps1", [128, 1024], F32))
    sps2 = est.enter_context(nc.psum_tensor("sps2", [128, 1024], F32))
    ya = est.enter_context(nc.psum_tensor("ya", [65, 512], F32))
    yb = est.enter_context(nc.psum_tensor("yb", [65, 512], F32))
    sps = [sps0, sps1, sps2]
    spp = sps2[:, 0:512]

    sIN = Sem(nc, "sIN")
    sCN = Sem(nc, "sCN")
    sWL = Sem(nc, "sWL")
    sTR = Sem(nc, "sTR")
    sGA = Sem(nc, "sGA")
    sDV = Sem(nc, "sDV")
    sSC = Sem(nc, "sSC")
    sPE = Sem(nc, "sPE")
    sPL = Sem(nc, "sPL")
    sWS = Sem(nc, "sWS")
    sOUT = Sem(nc, "sOUT")
    sM2 = Sem(nc, "sM2")

    V = nc.vector
    SC = nc.scalar
    PE = nc.tensor
    GP = nc.gpsimd
    SY = nc.sync

    def vo(method, *a, **kw):
        V.wait_ge(sDV.h, sDV.v)
        return sDV.inc(method(*a, **kw))

    def so(method, *a, **kw):
        SC.wait_ge(sSC.h, sSC.v)
        return sSC.inc(method(*a, **kw))

    def po(method, *a, **kw):
        GP.wait_ge(sPL.h, sPL.v)
        return sPL.inc(method(*a, **kw))

    def sy_dma(sem, out, in_, **kw):
        SY.wait_ge(sem.h, sem.v)
        SY.dma_start(out=out, in_=in_, **kw).then_inc(sem.h, 16)
        sem.v += 16
        return sem.v

    def gp_dma(sem, out, in_, **kw):
        GP.wait_ge(sem.h, sem.v)
        GP.dma_start(out=out, in_=in_, **kw).then_inc(sem.h, 16)
        sem.v += 16
        return sem.v

    sps_last_exp = [0, 0, 0]
    sps_last_dve = [0, 0, 0]

    for dst, srcc in [(on1b, ones_1x128b), (on1f, ones_1x128f), (onc, ones_128x1b),
                      (id128, id128_c), (id128b, id128b_c), (bqc, bq_d), (bkc, bk_d)]:
        sy_dma(sCN, dst[:], srcc)
    v_consts = sCN.v
    gp_dma(sGA, lusb[:], lu_d)
    gp_dma(sGA, srow[0:1, 0:8], iota8_c)
    gp_dma(sGA, bpr_b[0:1, :], bp_d)
    gp_dma(sGA, padbs[:], pb_d)
    v_small = sGA.v
    po(GP.iota, iotac[:], pattern=[[0, 1]], base=0, channel_multiplier=1)
    v_iota = po(GP.tensor_copy, out=iotacf[:], in_=iotac[:])

    vo(V.memset, cst[:, 0:1], EPS)
    v_cst = vo(V.memset, cst[:, 1:2], -LOG8)

    SC.wait_ge(sGA.h, v_small)
    so(SC.activation, out=gsc[0:1, 0:8], in_=lusb[0:1, 8:16], func=AF.Ln)
    v_g = so(SC.activation, out=gsc[0:1, 8:16], in_=gsc[0:1, 0:8], func=AF.Ln, scale=-1.0)
    V.wait_ge(sSC.h, v_g)
    vo(V.tensor_tensor, out=gsc[0:1, 0:8], in0=lusb[0:1, 0:8], in1=gsc[0:1, 8:16], op=ALU.subtract)
    vo(V.reduce_max, out=srow[0:1, 26:27], in_=gsc[0:1, 0:8], axis=AX.X)
    vo(V.tensor_scalar, out=gsc[0:1, 8:16], in0=gsc[0:1, 0:8], scalar1=srow[0:1, 26:27],
       scalar2=None, op0=ALU.is_ge)
    vo(V.tensor_tensor, out=gsc[0:1, 8:16], in0=gsc[0:1, 8:16], in1=srow[0:1, 0:8], op=ALU.mult)
    vo(V.reduce_max, out=srow[0:1, 27:28], in_=gsc[0:1, 8:16], axis=AX.X)
    v_idxi = vo(V.tensor_copy, out=idxi[:], in_=srow[0:1, 27:28])

    vo(V.tensor_scalar, out=srow[0:1, 28:29], in0=srow[0:1, 27:28], scalar1=1024.0,
       scalar2=None, op0=ALU.mult)
    v_idxb = vo(V.tensor_copy, out=sbf[0:1, 0:1], in_=srow[0:1, 28:29])
    PE.wait_ge(sDV.h, v_idxb)
    PE.wait_ge(sCN.h, v_consts)
    v_pe_idx = sPE.inc(PE.matmul(spp[:, 0:1], on1b[:], sbf[0:1, 0:1], start=True, stop=True))
    V.wait_ge(sPE.h, v_pe_idx)
    vo(V.tensor_copy, out=gcolf[:], in_=spp[:, 0:1])
    V.wait_ge(sPL.h, v_iota)
    vo(V.tensor_tensor, out=tmpc[:], in0=iotacf[:], in1=gcolf[:], op=ALU.add)
    for c in range(8):
        vo(V.tensor_scalar, out=gcol8f[:, c:c + 1], in0=tmpc[:], scalar1=float(128 * c),
           scalar2=None, op0=ALU.add)
    v_gcol = vo(V.tensor_copy, out=gcol[:], in_=gcol8f[:])

    SC.wait_ge(sDV.h, v_cst)

    def load_wbf(wd, wbf, stage=None):
        GP.wait_ge(sPE.h, sPE.v)
        for c in range(8):
            gp_dma(sWL, wbf[:, c, :], wd[128 * c:128 * c + 128, :])
        return sWL.v

    ps_ctr = [0]
    ps_evac = [sps_last_dve[0], sps_last_dve[0], 0, 0, sps_last_dve[2], sps_last_dve[2]]

    def ps_view(i):
        return sps[i // 2][:, 512 * (i % 2):512 * (i % 2) + 512]

    def proj_mm(lhsT_fn, rhs_fn, evac_fn):
        i = ps_ctr[0] % 6
        ps = ps_view(i)
        ps_ctr[0] += 1
        if ps_evac[i]:
            PE.wait_ge(sDV.h, ps_evac[i])
        last = None
        for c in range(8):
            last = PE.matmul(ps, lhsT_fn(c), rhs_fn(c), start=(c == 0), stop=(c == 7))
        v_pe = sPE.inc(last)
        V.wait_ge(sPE.h, v_pe)
        evac_fn(ps)
        ps_evac[i] = sDV.v

    def emit_ln2t(xbuf, tokdst, base):
        ls = lnst[:].rearrange("p (t w) -> p t w", w=16)
        for t in range(2):
            vo(V.bn_stats, out=ls[:, t, 0:6], in_=xbuf[:, t, 0:512])
            vo(V.bn_stats, out=ls[:, t, 6:12], in_=xbuf[:, t, 512:1024])
            vo(V.bn_aggr, out=ls[:, t, 12:14], in_=ls[:, t, 0:12])
            vo(V.tensor_scalar, out=ls[:, t, 14:15], in0=ls[:, t, 12:13],
               scalar1=-1.0, scalar2=None, op0=ALU.mult)
        v_stats = sDV.v
        SC.wait_ge(sDV.h, v_stats)
        so(SC.activation, out=ls[:, :, 15:16], in_=ls[:, :, 13:14], func=AF.Ln,
           bias=cst[:, 0:1], scale=1.0)
        v_sq = so(SC.activation, out=ls[:, :, 15:16], in_=ls[:, :, 15:16], func=AF.Exp, scale=-0.5)
        V.wait_ge(sSC.h, v_sq)
        for t in range(2):
            vo(V.tensor_scalar, out=tokdst(base + t), in0=xbuf[:, t, :],
               scalar1=ls[:, t, 14:15], scalar2=ls[:, t, 15:16],
               op0=ALU.add, op1=ALU.mult)
        return sDV.v, v_sq

    def emit_path(xd, ntok, sg_fns, prefix):
        nsg = ntok // 512
        est_p = contextlib.ExitStack()
        xbuf = est_p.enter_context(nc.sbuf_tensor(prefix + "_xb", [128, 2, C], F32))
        tok4 = est_p.enter_context(nc.sbuf_tensor(prefix + "_tok", [128, 8, 4, 128], BF16))
        fm2 = est_p.enter_context(nc.sbuf_tensor(prefix + "_fm", [128, 2, 8, 512], BF16))
        nh = 2 * nsg
        v_loads = [0] * (nh + 1)
        for tt in range(2):
            sy_dma(sIN, xbuf[:, tt, :], xd[128 * tt:128 * tt + 128, :])
        v_loads[0] = sIN.v
        vtr_prev = 0
        v_pe_fm = 0

        def emit_ln_sg(sg):
            for h in range(2):
                g = 2 * sg + h
                V.wait_ge(sIN.h, v_loads[g])
                if vtr_prev and h == 0:
                    V.wait_ge(sTR.h, vtr_prev)
                SC.wait_ge(sIN.h, v_loads[g])
                v_ln, v_sq = emit_ln2t(xbuf, lambda tt: tok4[:, :, tt, :], 2 * h)
                if g + 1 < nh:
                    SY.wait_ge(sDV.h, v_ln)
                    for tt in range(2):
                        row0 = 256 * (g + 1) + 128 * tt
                        sy_dma(sIN, xbuf[:, tt, :], xd[row0:row0 + 128, :])
                    v_loads[g + 1] = sIN.v
            return sDV.v

        v_ln_done = emit_ln_sg(0)
        v_pe_fm2 = [0, 0]
        for sg in range(nsg):
            fm = fm2[:, sg % 2, :, :]
            SY.wait_ge(sDV.h, v_ln_done)
            if v_pe_fm2[sg % 2]:
                SY.wait_ge(sPE.h, v_pe_fm2[sg % 2])
            sy_dma(sTR, fm.rearrange("p c f -> p (c f)").rearrange("p (m g) -> p m g", g=128),
                   tok4[:].rearrange("p a b f -> p (a b f)"), transpose=True)
            vtr = sTR.v
            vtr_prev = vtr
            if sg + 1 < nsg:
                v_ln_done = emit_ln_sg(sg + 1)
            PE.wait_ge(sTR.h, vtr)
            for fn in sg_fns:
                fn(sg, fm)
            v_pe_fm2[sg % 2] = sPE.v
        est_p.close()

    est_w = contextlib.ExitStack()
    wkb = est_w.enter_context(nc.sbuf_tensor("wkb", [128, 8, C], BF16))
    bvb = est_w.enter_context(nc.sbuf_tensor("bvb", [128, C], BF16))
    gp_dma(sGA, bvb[0:1, :], bv_d)
    PE.wait_ge(sGA.h, sGA.v)
    PE.wait_ge(sDV.h, v_gcol)
    for half in range(2):
        v_pe_b = sPE.inc(PE.matmul(spp[:, 0:512], on1b[:], bvb[0:1, 512 * half:512 * half + 512],
                                   start=True, stop=True))
        V.wait_ge(sPE.h, v_pe_b)
        vo(V.tensor_copy, out=bvb[:, 512 * half:512 * half + 512], in_=spp[:, 0:512])
        PE.wait_ge(sDV.h, sDV.v)
    sps_last_dve[2] = sDV.v
    v_wk = load_wbf(wk_d, wkb)
    v_wv = load_wbf(wv_d, wvb)

    wgr = Qsb[:, :, 0:1024]
    gsems = [Sem(nc, f"sG{c}") for c in range(8)]
    GP.wait_ge(sDV.h, v_gcol)
    for c in range(8):
        GP.indirect_dma_start(out=wgr[:, c, :], out_offset=None, in_=wf_d,
                              in_offset=IndirectOffsetOnAxis(gcol[:, c:c + 1], 0)).then_inc(gsems[c].h, 16)
        gsems[c].v += 16
    for c in range(8):
        GP.wait_ge(gsems[c].h, 16)
        gp_dma(sWS, weff_d[c], wgr[:, c, :])
    v_weff = sWS.v
    sBF = Sem(nc, "sBF")
    vo(V.tensor_scalar, out=tmpc[0:2, 0:1], in0=gcolf[0:2, 0:1],
       scalar1=1.0 / 1024.0, scalar2=None, op0=ALU.mult)
    v_bfo = vo(V.tensor_copy, out=bfo[:], in_=tmpc[0:2, 0:1])
    GP.wait_ge(sDV.h, v_bfo)
    GP.indirect_dma_start(out=beff_b[:], out_offset=None, in_=bf_d,
                          in_offset=IndirectOffsetOnAxis(bfo[:], 0)).then_inc(sBF.h, 16)
    sBF.v += 16
    v_kvw = vo(V.memset, Vsb[:].rearrange("p t (h c) -> p (t h) c", c=65)[:, :, 64:65], 1.0)

    def kproj(sg, fm):
        PE.wait_ge(sDV.h, v_kvw)
        PE.wait_ge(sWL.h, v_wk)
        for e in range(8):
            def lf(c, e=e):
                return wkb[:, c, 128 * e:128 * e + 128]

            def rf(c, fm=fm):
                return fm[:, c, :]

            def ev(ps, e=e, sg=sg):
                vo(V.tensor_scalar, out=Ksb[:, e, 512 * sg:512 * sg + 512], in0=ps,
                   scalar1=bkc[:, e:e + 1], scalar2=None, op0=ALU.add)
            proj_mm(lf, rf, ev)

    def vproj(sg, fm):
        PE.wait_ge(sWL.h, v_wv)
        for sc in range(4):
            for eh in range(2):
                def lf(c, fm=fm, sc=sc):
                    return fm[:, c, 128 * sc:128 * sc + 128]

                def rf(c, eh=eh):
                    return wvb[:, c, 512 * eh:512 * eh + 512]

                def ev(ps, sg=sg, sc=sc, eh=eh):
                    tile = 4 * sg + sc
                    dst = Vsb[:, tile, :].rearrange("p (h c) -> p h c", c=65)[:, 8 * eh:8 * eh + 8, 0:64]
                    vo(V.tensor_tensor, out=dst, in0=ps.rearrange("p (h c) -> p h c", c=64),
                       in1=bvb[:, 512 * eh:512 * eh + 512].rearrange("p (h c) -> p h c", c=64),
                       op=ALU.add)
                proj_mm(lf, rf, ev)

    emit_path(xkv_d, KT, [kproj, vproj], "kv")

    def emit_norm(src, hp, n, cidx, scratch, v_gate):
        nwin = n // 128
        v_sq = vo(V.tensor_tensor, out=scratch[:, 0:n], in0=src[:, hp, 0:n],
                  in1=src[:, hp, 0:n], op=ALU.mult)
        PE.wait_ge(sDV.h, max(v_sq, v_gate[0]))
        last = None
        for w in range(nwin):
            last = PE.matmul(spp[:, w:w + 1], scratch[:, 128 * w:128 * w + 128], onc[:],
                             start=True, stop=True)
        v_pe1 = sPE.inc(last)
        V.wait_ge(sPE.h, v_pe1)
        v_rm = vo(V.reduce_max, out=tmpc[:], in_=spp[:, 0:nwin], axis=AX.X)
        PE.wait_ge(sDV.h, v_rm)
        v_pe2 = sPE.inc(PE.transpose(spp[0:1, 0:128], tmpc[:], id128[:]))
        V.wait_ge(sPE.h, v_pe2)
        v_gate[0] = vo(V.reduce_max, out=crow[0:1, cidx:cidx + 1],
                       in_=spp[0:1, 0:128], axis=AX.X)

    v_gate = [sDV.v]
    V.wait_ge(sPE.h, sPE.v)
    kscr = wkb[:].rearrange("p a f -> p (a f)")
    for hp in range(8):
        emit_norm(Ksb, hp, KT, 8 + hp, kscr, v_gate)
    ps_evac[4] = v_gate[0]
    ps_evac[5] = v_gate[0]

    V.wait_ge(sWS.h, v_weff)
    v_wq = load_wbf(wq_d, wkb)

    def qproj(sg, fm):
        PE.wait_ge(sWL.h, v_wq)
        for e in range(8):
            def lf(c, e=e):
                return wkb[:, c, 128 * e:128 * e + 128]

            def rf(c, fm=fm):
                return fm[:, c, :]

            def ev(ps, e=e, sg=sg):
                vo(V.tensor_scalar, out=Qsb[:, e, 512 * sg:512 * sg + 512], in0=ps,
                   scalar1=bqc[:, e:e + 1], scalar2=None, op0=ALU.add)
            proj_mm(lf, rf, ev)

    emit_path(xq_d, TQ, [qproj], "q")

    v_wp_wl = load_wbf(wp_d, wvb)
    wpb = wvb
    est_w.close()
    scr = sb("scr", [128, 4096], BF16)
    V.wait_ge(sPE.h, sPE.v)

    v_gate[0] = max(v_gate[0], sDV.v)
    for hp in range(8):
        emit_norm(Qsb, hp, TQ, hp, scr, v_gate)
    vo(V.tensor_tensor, out=crow[0:1, 0:8], in0=crow[0:1, 0:8], in1=crow[0:1, 8:16], op=ALU.mult)
    v_c1 = vo(V.tensor_scalar, out=crow[0:1, 0:8], in0=crow[0:1, 0:8], scalar1=1e-12,
              scalar2=None, op0=ALU.max)
    SC.wait_ge(sDV.h, v_c1)
    so(SC.activation, out=crow[0:1, 0:8], in_=crow[0:1, 0:8], func=AF.Ln)
    v_c2 = so(SC.activation, out=crow[0:1, 0:8], in_=crow[0:1, 0:8], func=AF.Exp,
              scale=0.5, bias=cst[0:1, 1:2])
    V.wait_ge(sSC.h, v_c2)
    vo(V.tensor_scalar, out=crow[0:1, 0:8], in0=crow[0:1, 0:8], scalar1=0.5, scalar2=-1.0,
       op0=ALU.add, op1=ALU.mult)
    v_c3 = vo(V.tensor_copy, out=sbf[0:1, 0:8], in_=crow[0:1, 0:8])
    PE.wait_ge(sDV.h, v_c3)
    v_pe3 = sPE.inc(PE.matmul(spp[:, 0:8], on1b[:], sbf[0:1, 0:8], start=True, stop=True))
    V.wait_ge(sPE.h, v_pe3)
    v_negc = vo(V.tensor_copy, out=negc[:], in_=spp[:, 0:8])
    sps_last_dve[2] = v_negc
    V.wait_ge(sGA.h, v_small)
    for w in range(12):
        vo(V.tensor_scalar, out=negcP[:, w, :], in0=negc[:],
           scalar1=padbs[:, w:w + 1], scalar2=None, op0=ALU.add)
    v_negcP = sDV.v

    v_projs_done = sDV.v
    if "dbg_negc" in dbg:
        SY.wait_ge(sDV.h, v_negc)
        SY.dma_start(out=dbg["dbg_negc"], in_=negc[:]).then_inc(sOUT.h, 16); sOUT.v += 16
    if "dbg_q" in dbg:
        SY.wait_ge(sDV.h, v_projs_done)
        SY.dma_start(out=dbg["dbg_q"], in_=Qsb[:]).then_inc(sOUT.h, 16); sOUT.v += 16
    if "dbg_k" in dbg:
        SY.wait_ge(sDV.h, v_projs_done)
        SY.dma_start(out=dbg["dbg_k"], in_=Ksb[:]).then_inc(sOUT.h, 16); sOUT.v += 16
    if "dbg_v" in dbg:
        SY.wait_ge(sDV.h, v_projs_done)
        SY.dma_start(out=dbg["dbg_v"], in_=Vsb[:]).then_inc(sOUT.h, 16); sOUT.v += 16
    if "dbg_weff" in dbg:
        SY.wait_ge(sWS.h, v_weff)
        SY.dma_start(out=dbg["dbg_weff"], in_=weff_d).then_inc(sOUT.h, 16); sOUT.v += 16
    v_dbg_gate = [sOUT.v]

    nc.all_engine_barrier()
    mksb = sb("mksb", [128, 6, 512], BF16)
    Ynsb = sb("Ynsb", [128, 8, 512], BF16)
    zinvr = sb("zinvr", [1, 1024], BF16)
    wffb = sb("wffb", [128, 8, C], BF16)
    Yevb = sb("Yevb", [65, C], F32)
    tbuf = sb("tbuf", [128, C], F32)
    x1buf = sb("x1buf", [128, C], F32)
    ln3 = sb("ln3", [128, C], BF16)
    ln3T = sb("ln3t", [128, C], BF16)

    SY.wait_ge(sWS.h, v_weff)
    sy_dma(sIN, wffb[:], weff_d.rearrange("c p f -> p c f"))
    v_wffb = sIN.v

    Pv = scr[:].rearrange("p (r f) -> p r f", r=4)

    def yn_view(s):
        if s % 2 == 0:
            return Ynsb[:]
        return Qsb[:, :, 512 * (s - 1):512 * (s - 1) + 512]

    v_mask_free = 0
    v_pring = {"spe": 0}
    v_ya_free = [0]
    v_zinv_free_pe = [0]
    v_attn_start = sDV.v
    yn_ready = {}
    pend_epi = [None]
    pend_tail = [None]

    pend_ffn = [None]
    v_ffn_last = [0]

    def ln_p2(xin, xnout):
        vo(V.bn_stats, out=p2s[:, 0:6], in_=xin[:, 0:512])
        vo(V.bn_stats, out=p2s[:, 6:12], in_=xin[:, 512:1024])
        vo(V.bn_aggr, out=p2s[:, 12:14], in_=p2s[:, 0:12])
        v2 = vo(V.tensor_scalar, out=p2s[:, 14:15], in0=p2s[:, 12:13],
                scalar1=-1.0, scalar2=None, op0=ALU.mult)
        SC.wait_ge(sDV.h, v2)
        so(SC.activation, out=p2s[:, 15:16], in_=p2s[:, 13:14], func=AF.Ln,
           bias=cst[:, 0:1], scale=1.0)
        v3 = so(SC.activation, out=p2s[:, 15:16], in_=p2s[:, 15:16], func=AF.Exp, scale=-0.5)
        V.wait_ge(sSC.h, v3)
        return vo(V.tensor_scalar, out=xnout, in0=xin, scalar1=p2s[:, 14:15],
                  scalar2=p2s[:, 15:16], op0=ALU.add, op1=ALU.mult)

    v_xq_pref = [0]

    def emit_ffn_parcel(pref_row0):
        st = {}
        fillers = []
        pend = pend_ffn[0]
        pend_ffn[0] = None
        if pend is not None:
            row02, v_l3t2 = pend
            l3v = ln3T[:].rearrange("p (a f) -> p a f", a=8)

            def mmh(eh):
                def f():
                    if eh == 0:
                        PE.wait_ge(sTR.h, v_l3t2)
                        PE.wait_ge(sIN.h, v_wffb)
                        PE.wait_ge(sDV.h, sps_last_dve[2])
                        PE.wait_ge(sBF.h, 16)
                        if sps_last_exp[2]:
                            PE.wait_ge(sSC.h, sps_last_exp[2])
                    last = None
                    for cc in range(8):
                        last = PE.matmul(sps2[:, 512 * eh:512 * eh + 512], l3v[:, cc, :],
                                         wffb[:, cc, 512 * eh:512 * eh + 512],
                                         start=(cc == 0), stop=False)
                    last = PE.matmul(sps2[:, 512 * eh:512 * eh + 512], on1b[:],
                                     beff_b[0:1, 512 * eh:512 * eh + 512], start=False, stop=True)
                    st[eh] = sPE.inc(last)
                return f
            fillers = [mmh(0), mmh(1)]

        def tail():
            if pend is not None:
                v_ffn = st[1]
                v_ffn_last[0] = v_ffn
                V.wait_ge(sPE.h, v_ffn)
                v_ln4 = ln_p2(sps2[:], tbuf[:])
                sps_last_dve[2] = v_ln4
                v_outc = vo(V.tensor_tensor, out=x1buf[:], in0=x1buf[:], in1=tbuf[:], op=ALU.add)
                SY.wait_ge(sDV.h, v_outc)
                sy_dma(sOUT, out_d[row02:row02 + 128, :], x1buf[:])
            if pref_row0 is not None:
                sy_dma(sM2, tbuf[:], xq_d[pref_row0:pref_row0 + 128, :])
                v_xq_pref[0] = sM2.v
        return fillers, tail

    def emit_op_parcel(ssrc, k):
        Ysrc = yn_view(ssrc)
        row0 = 512 * ssrc + 128 * k
        st = {}

        def mmh(eh):
            def f():
                if eh == 0:
                    PE.wait_ge(sDV.h, max(yn_ready[ssrc], sps_last_dve[2]))
                    PE.wait_ge(sWL.h, v_wp_wl)
                    PE.wait_ge(sGA.h, v_small)
                    if sps_last_exp[2]:
                        PE.wait_ge(sSC.h, sps_last_exp[2])
                last = None
                for hp2 in range(8):
                    last = PE.matmul(sps2[:, 512 * eh:512 * eh + 512],
                                     Ysrc[:, hp2, 128 * k:128 * k + 128],
                                     wpb[:, hp2, 512 * eh:512 * eh + 512],
                                     start=(hp2 == 0), stop=False)
                last = PE.matmul(sps2[:, 512 * eh:512 * eh + 512], on1b[:],
                                 bpr_b[0:1, 512 * eh:512 * eh + 512], start=False, stop=True)
                st[eh] = sPE.inc(last)
            return f

        def tail():
            v_op = st[1]
            v_out_prev = sOUT.v
            V.wait_ge(sPE.h, v_op)
            V.wait_ge(sOUT.h, v_out_prev)
            v_ln2 = ln_p2(sps2[:], x1buf[:])
            sps_last_dve[2] = v_ln2
            V.wait_ge(sM2.h, v_xq_pref[0])
            vo(V.tensor_tensor, out=x1buf[:], in0=x1buf[:], in1=tbuf[:], op=ALU.add)
            v_ln3 = ln_p2(x1buf[:], ln3[:])
            SY.wait_ge(sDV.h, v_ln3)
            SY.wait_ge(sPE.h, v_ffn_last[0])
            sy_dma(sTR, ln3T[:].rearrange("p (a f) -> p a f", a=8), ln3[:], transpose=True)
            pend_ffn[0] = (row0, sTR.v)
        return [mmh(0), mmh(1)], tail

    for slot in range(NSLOT):
        L = LCH[slot]
        v_pool_slot = sPL.v
        if v_mask_free:
            GP.wait_ge(sPE.h, v_mask_free)
        gp_dma(sGA, mksb[:], mk_d[6 * slot:6 * slot + 6].rearrange("w p f -> p w f"))
        v_mk = sGA.v

        for hp in range(8):
            fillers, parcel_tail = [], None
            if slot >= 1:
                if hp % 2 == 0:
                    fillers, parcel_tail = emit_ffn_parcel(512 * (slot - 1) + 128 * (hp // 2))
                else:
                    fillers, parcel_tail = emit_op_parcel(slot - 1, hp // 2)

            v_exp = [0] * L
            v_pvc = [0] * L

            def emit_qk(c):
                ti = c % 2
                ps = sps[ti]
                msk = c >= L - 6
                PE.wait_ge(sDV.h, max(v_negcP, v_attn_start, sps_last_dve[ti]))
                if sps_last_exp[ti]:
                    PE.wait_ge(sSC.h, sps_last_exp[ti])
                PE.matmul(ps[:, 0:512], Ksb[0:64, hp, 128 * c:128 * c + 128],
                          Qsb[0:64, hp, 512 * slot:512 * slot + 512],
                          start=True, stop=not msk, tile_position=(0, 0))
                last = PE.matmul(ps[:, 512:1024], Ksb[64:128, hp, 128 * c:128 * c + 128],
                                 Qsb[64:128, hp, 512 * slot:512 * slot + 512],
                                 start=True, stop=not msk, tile_position=(64, 0))
                if msk:
                    j = c - (L - 6)
                    PE.wait_ge(sGA.h, v_mk)
                    PE.matmul(ps[:, 0:512], id128b[:], mksb[:, j, :],
                              start=False, stop=True)
                    last = PE.matmul(ps[:, 512:1024], id128b[:], mksb[:, j, :],
                                     start=False, stop=True)
                return sPE.inc(last)

            def emit_exp(c, v_qk):
                pr = c % 4
                SC.wait_ge(sPE.h, v_qk)
                if c >= 4:
                    SC.wait_ge(sPE.h, v_pvc[c - 4])
                else:
                    if v_pring["spe"]:
                        SC.wait_ge(sPE.h, v_pring["spe"])
                if c >= L - 3:
                    bias_ap = negcP[:, 3 * slot + (c - (L - 3)), hp:hp + 1]
                else:
                    bias_ap = negc[:, hp:hp + 1]
                v_exp[c] = so(SC.activation, out=Pv[:, pr, :], in_=sps[c % 2][:], func=AF.Exp,
                              bias=bias_ap, scale=0.125)
                sps_last_exp[c % 2] = v_exp[c]

            def emit_pv(c, hp=hp, L=L, v_exp=v_exp, v_pvc=v_pvc):
                pr = c % 4
                if c == 0 and v_ya_free[0]:
                    PE.wait_ge(sDV.h, v_ya_free[0])
                PE.wait_ge(sSC.h, v_exp[c])
                PE.matmul(ya[:], Vsb[:, c, 65 * (2 * hp):65 * (2 * hp) + 65],
                          Pv[:, pr, 0:512], start=(c == 0), stop=(c == L - 1))
                v_pvc[c] = sPE.inc(PE.matmul(yb[:], Vsb[:, c, 65 * (2 * hp + 1):65 * (2 * hp + 1) + 65],
                                             Pv[:, pr, 512:1024], start=(c == 0), stop=(c == L - 1)))

            for c in range(L):
                v_qk = emit_qk(c)
                if c == 0 and pend_tail[0] is not None:
                    pend_tail[0]()
                    pend_tail[0] = None
                emit_exp(c, v_qk)
                if pend_epi[0] is not None:
                    if c == 1:
                        pend_epi[0][0]()
                    elif c == 2:
                        pend_epi[0][1]()
                    elif c == 3:
                        pend_epi[0] = None
                        emit_pv(0)
                        emit_pv(1)
                        emit_pv(2)
                elif 1 <= c <= L - 2:
                    emit_pv(c - 1)
                if c in (4, 6) and fillers:
                    fillers.pop(0)()
            for f in fillers:
                f()
            if parcel_tail is not None:
                parcel_tail()

            def mk_tail(emit_pv_e, v_pvc_e, slot_e, hp_e, L_e):
                def tail():
                    emit_pv_e(L_e - 2)
                    emit_pv_e(L_e - 1)
                    V.wait_ge(sPE.h, v_pvc_e[L_e - 1])
                    vo(V.tensor_copy, out=Yevb[0:64, 0:512], in_=ya[0:64, :])
                    v_yev = vo(V.tensor_copy, out=Yevb[0:64, 512:1024], in_=yb[0:64, :])
                    v_pring["spe"] = v_pvc_e[L_e - 1]
                    pend_epi[0] = mk_epi(slot_e, hp_e, v_yev, v_pvc_e[L_e - 1])
                return tail

            def mk_epi(slot_e, hp_e, v_yev_e, v_pv_last):
                st = {}

                def emit_zchain():
                    SC.wait_ge(sPE.h, max(v_pv_last, v_zinv_free_pe[0]))
                    so(SC.activation, out=Yevb[64:65, 0:512], in_=ya[64:65, :], func=AF.Ln)
                    so(SC.activation, out=Yevb[64:65, 512:1024], in_=yb[64:65, :], func=AF.Ln)
                    v_zi = so(SC.activation, out=zinvr[:], in_=Yevb[64:65, :], func=AF.Exp, scale=-1.0)
                    st["v_zi"] = v_zi

                def emit_ynmul():
                    ynv_e = yn_view(slot_e)
                    PE.wait_ge(sSC.h, st["v_zi"])
                    PE.wait_ge(sDV.h, v_yev_e)
                    PE.matmul(ya[0:64, :], on1b[0:1, 0:64], zinvr[0:1, 0:512],
                              start=True, stop=True)
                    v_zbB = sPE.inc(PE.matmul(yb[0:64, :], on1b[0:1, 0:64],
                                              zinvr[0:1, 512:1024], start=True, stop=True))
                    v_zinv_free_pe[0] = v_zbB
                    V.wait_ge(sPE.h, v_zbB)
                    if dbg and v_dbg_gate[0]:
                        V.wait_ge(sOUT.h, v_dbg_gate[0])
                    vo(V.tensor_tensor, out=ynv_e[0:64, hp_e, :], in0=Yevb[0:64, 0:512],
                       in1=ya[0:64, :], op=ALU.mult)
                    v_done = vo(V.tensor_tensor, out=ynv_e[64:128, hp_e, :],
                                in0=Yevb[0:64, 512:1024], in1=yb[0:64, :], op=ALU.mult)
                    v_ya_free[0] = v_done
                    if hp_e == 7:
                        yn_ready[slot_e] = v_done
                        if "dbg_yn" in dbg:
                            SY.wait_ge(sDV.h, v_done)
                            SY.dma_start(out=dbg["dbg_yn"][:, :, 512 * slot_e:512 * slot_e + 512],
                                         in_=ynv_e).then_inc(sOUT.h, 16)
                            sOUT.v += 16
                            v_dbg_gate[0] = sOUT.v

                return emit_zchain, emit_ynmul

            pend_tail[0] = mk_tail(emit_pv, v_pvc, slot, hp, L)

        v_mask_free = sPE.v

    pend_tail[0]()
    pend_tail[0] = None
    pend_epi[0][0]()
    pend_epi[0][1]()
    pend_epi[0] = None

    for k in range(4):
        fl, tl = emit_ffn_parcel(512 * 3 + 128 * k)
        for f in fl:
            f()
        tl()
        fl, tl = emit_op_parcel(3, k)
        for f in fl:
            f()
        tl()
    fl, tl = emit_ffn_parcel(None)
    for f in fl:
        f()
    tl()

    SY.wait_ge(sOUT.h, sOUT.v)
    est.close()
    return nc


_NC_CACHE = {}


def _get_nc():
    if "nc" not in _NC_CACHE:
        _NC_CACHE["nc"] = build_nc()
    return _NC_CACHE["nc"]


def kernel(L0_attn_x, L0_attn_Wq, L0_attn_bq, L0_attn_Wk, L0_attn_bk,
           L0_attn_Wv, L0_attn_bv, L0_attn_Wp, L0_attn_bp,
           L0_ffn_W, L0_ffn_b, L0_ffn_logits, L0_ffn_u):
    x = np.ascontiguousarray(np.asarray(L0_attn_x, dtype=np.float32))
    wq = np.ascontiguousarray(np.asarray(L0_attn_Wq, dtype=np.float32))
    wk = np.ascontiguousarray(np.asarray(L0_attn_Wk, dtype=np.float32))
    wv = np.ascontiguousarray(np.asarray(L0_attn_Wv, dtype=np.float32))
    wp = np.ascontiguousarray(np.asarray(L0_attn_Wp, dtype=np.float32))
    bq = np.asarray(L0_attn_bq, dtype=np.float32)
    bk = np.asarray(L0_attn_bk, dtype=np.float32)
    bv = np.asarray(L0_attn_bv, dtype=np.float32)
    bp = np.asarray(L0_attn_bp, dtype=np.float32)
    wf = np.ascontiguousarray(np.asarray(L0_ffn_W, dtype=np.float32).reshape(8 * C, C))
    bf = np.ascontiguousarray(np.asarray(L0_ffn_b, dtype=np.float32))
    lg = np.asarray(L0_ffn_logits, dtype=np.float32)
    u = np.asarray(L0_ffn_u, dtype=np.float32)

    bq_c = np.ascontiguousarray(bq.reshape(8, 128).T)
    bk_c = np.ascontiguousarray(bk.reshape(8, 128).T)
    bv_r = np.ascontiguousarray(bv.reshape(1, C))
    bp_r = np.ascontiguousarray(bp.reshape(1, C))
    lu = np.ascontiguousarray(np.concatenate([lg, u]).reshape(1, 16))
    masks = [host_masks(0), host_masks(1)]
    padbs_h = [host_padb(0), host_padb(1)]

    in_maps = []
    for core in range(8):
        b, r = core // 2, core % 2
        xb = x[b]
        rows = np.concatenate([np.arange(512 * j, 512 * j + 512) for j in QB[r]])
        xq = np.ascontiguousarray(xb[rows])
        xkv = np.ascontiguousarray(xb.reshape(T // 4, 4, C)[:, 0:3, :].reshape(KT, C))
        in_maps.append({
            "xq": xq, "xkv": xkv, "wq": wq, "wk": wk, "wv": wv, "wp": wp,
            "bq": bq_c, "bk": bk_c, "bv": bv_r, "bp": bp_r,
            "wf": wf, "bf": bf, "lu": lu, "mk": masks[r], "padb": padbs_h[r],
        })

    nc = _get_nc()
    res = run_bass_kernel_spmd(nc, in_maps, core_ids=list(range(8)))

    out = np.empty((B, T, C), dtype=np.float32)
    for core in range(8):
        b, r = core // 2, core % 2
        o = res.results[core]["out"]
        for i, j in enumerate(QB[r]):
            out[b, 512 * j:512 * j + 512] = o[512 * i:512 * i + 512]
    return out


# version 21
# speedup vs baseline: 1.0353x; 1.0175x over previous
import contextlib
import numpy as np
import ml_dtypes
import concourse.bass as bass
import concourse.mybir as mybir
from concourse.bass_utils import run_bass_kernel_spmd
from concourse.bass_core_rust import IndirectOffsetOnAxis

F32 = mybir.dt.float32
BF16 = mybir.dt.bfloat16
I32 = mybir.dt.int32
AF = mybir.ActivationFunctionType
ALU = mybir.AluOpType
AX = mybir.AxisListType

B, T, C, H, D = 4, 4096, 1024, 16, 64
KT = 3072
NCH = KT // 128
QB = [[0, 3, 4, 7], [1, 2, 5, 6]]
LCH = [6, 12, 18, 24]
NSLOT = 4
TQ = 2048
EPS = 1e-5
LOG8 = float(np.log(8.0))

DEBUG_OUTS = []


def _orig(j):
    return 4 * (j // 3) + j % 3


def host_masks(r):
    m = np.zeros((24, 128, 512), dtype=np.float32)
    for i in range(NSLOT):
        L = LCH[i]
        t = 3 * QB[r][i] + 3
        for j in range(6):
            c = L - 6 + j
            w = 6 * i + j
            if c < t and c >= t - 3:
                k = c - (t - 3)
                s = np.arange(128)
                og = _orig(128 * k + s)
                m[w] = np.where(og[:, None] <= np.arange(512)[None, :], 0.0, -60000.0)
    return np.ascontiguousarray(m)


def host_padb(r):
    p = np.zeros((128, 12), dtype=np.float32)
    for i in range(NSLOT):
        L = LCH[i]
        t = 3 * QB[r][i] + 3
        for k in range(3):
            if L - 3 + k >= t:
                p[:, 3 * i + k] = -1e30
    return np.ascontiguousarray(p)


class Sem:
    def __init__(self, nc, name):
        self.h = nc.alloc_semaphore(name=name)
        self.v = 0

    def inc(self, inst, k=1):
        inst.then_inc(self.h, k)
        self.v += k
        return self.v


def build_nc():
    nc = bass.Bass("TRN2", target_bir_lowering=False)

    xq_d = nc.dram_tensor("xq", [TQ, C], F32, kind="ExternalInput").ap()
    xkv_d = nc.dram_tensor("xkv", [KT, C], F32, kind="ExternalInput").ap()
    wq_d = nc.dram_tensor("wq", [C, C], F32, kind="ExternalInput").ap()
    wk_d = nc.dram_tensor("wk", [C, C], F32, kind="ExternalInput").ap()
    wv_d = nc.dram_tensor("wv", [C, C], F32, kind="ExternalInput").ap()
    wp_d = nc.dram_tensor("wp", [C, C], F32, kind="ExternalInput").ap()
    bq_d = nc.dram_tensor("bq", [128, 8], F32, kind="ExternalInput").ap()
    bk_d = nc.dram_tensor("bk", [128, 8], F32, kind="ExternalInput").ap()
    bv_d = nc.dram_tensor("bv", [1, C], F32, kind="ExternalInput").ap()
    bp_d = nc.dram_tensor("bp", [1, C], F32, kind="ExternalInput").ap()
    wf_d = nc.dram_tensor("wf", [8 * C, C], F32, kind="ExternalInput").ap()
    bf_d = nc.dram_tensor("bf", [8, C], F32, kind="ExternalInput").ap()
    lu_d = nc.dram_tensor("lu", [1, 16], F32, kind="ExternalInput").ap()
    mk_d = nc.dram_tensor("mk", [24, 128, 512], F32, kind="ExternalInput").ap()
    pb_d = nc.dram_tensor("padb", [128, 12], F32, kind="ExternalInput").ap()
    out_d = nc.dram_tensor("out", [TQ, C], F32, kind="ExternalOutput").ap()
    weff_d = nc.dram_tensor("weff_scr", [8, 128, C], BF16, kind="Internal").ap()

    dbg = {}
    for name, shape, dt in [
        ("dbg_q", [128, 8, TQ], BF16), ("dbg_k", [128, 8, KT], BF16),
        ("dbg_v", [128, NCH, 1040], BF16), ("dbg_yn", [128, 8, 512 * NSLOT], BF16),
        ("dbg_negc", [128, 8], F32), ("dbg_weff", [8, 128, C], BF16),
    ]:
        if name in DEBUG_OUTS:
            dbg[name] = nc.dram_tensor(name, shape, dt, kind="ExternalOutput").ap()

    ones_1x128b = nc.inline_tensor(np.ones((1, 128), dtype=ml_dtypes.bfloat16), name="c_on1b").ap()
    ones_1x128f = nc.inline_tensor(np.ones((1, 128), dtype=np.float32), name="c_on1f").ap()
    ones_128x1b = nc.inline_tensor(np.ones((128, 1), dtype=ml_dtypes.bfloat16), name="c_onc").ap()
    iota8_c = nc.inline_tensor(np.arange(8, dtype=np.float32).reshape(1, 8), name="c_iota8").ap()
    id128_c = nc.inline_tensor(np.eye(128, dtype=np.float32), name="c_id128").ap()
    id128b_c = nc.inline_tensor(np.eye(128, dtype=ml_dtypes.bfloat16), name="c_id128b").ap()

    est = contextlib.ExitStack()

    def sb(name, shape, dt):
        return est.enter_context(nc.sbuf_tensor(name, shape, dt))

    Ksb = sb("Ksb", [128, 8, KT], BF16)
    Vsb = sb("Vsb", [128, NCH, 1040], BF16)
    Qsb = sb("Qsb", [128, 8, TQ], BF16)
    wvb = sb("wvb", [128, 8, C], BF16)
    bqc = sb("bqc", [128, 8], F32)
    bkc = sb("bkc", [128, 8], F32)
    bpr_b = sb("bprb", [1, C], BF16)
    beff_b = sb("beffb", [2, C], BF16)
    bfo = sb("bfo", [2, 1], I32)
    negc = sb("negc", [128, 8], F32)
    negcP = sb("negcP", [128, 12, 8], F32)
    padbs = sb("padbs", [128, 12], F32)
    on1b = sb("on1b", [1, 128], BF16)
    on1f = sb("on1f", [1, 128], F32)
    onc = sb("onc", [128, 1], BF16)
    id128 = sb("id128", [128, 128], F32)
    id128b = sb("idb", [128, 128], BF16)
    srow = sb("srow", [1, 64], F32)
    sbf = sb("sbf", [1, 8], BF16)
    crow = sb("crow", [1, 16], F32)
    lusb = sb("lusb", [1, 16], F32)
    gsc = sb("gsc", [1, 16], F32)
    idxi = sb("idxi", [1, 1], I32)
    gcol = sb("gcol", [128, 8], I32)
    gcol8f = sb("gcol8f", [128, 8], F32)
    iotac = sb("iotac", [128, 1], I32)
    iotacf = sb("iotacf", [128, 1], F32)
    gcolf = sb("gcolf", [128, 1], F32)
    tmpc = sb("tmpc", [128, 1], F32)
    lnst = sb("lnst", [128, 32], F32)
    cst = sb("cst", [128, 2], F32)
    p2s = sb("p2s", [128, 16], F32)

    sps0 = est.enter_context(nc.psum_tensor("sps0", [128, 1024], F32))
    sps1 = est.enter_context(nc.psum_tensor("sps1", [128, 1024], F32))
    sps2 = est.enter_context(nc.psum_tensor("sps2", [128, 1024], F32))
    ya = est.enter_context(nc.psum_tensor("ya", [65, 512], F32))
    yb = est.enter_context(nc.psum_tensor("yb", [65, 512], F32))
    sps = [sps0, sps1, sps2]
    spp = sps2[:, 0:512]

    sIN = Sem(nc, "sIN")
    sCN = Sem(nc, "sCN")
    sWL = Sem(nc, "sWL")
    sTR = Sem(nc, "sTR")
    sGA = Sem(nc, "sGA")
    sDV = Sem(nc, "sDV")
    sSC = Sem(nc, "sSC")
    sPE = Sem(nc, "sPE")
    sPL = Sem(nc, "sPL")
    sWS = Sem(nc, "sWS")
    sOUT = Sem(nc, "sOUT")
    sM2 = Sem(nc, "sM2")

    V = nc.vector
    SC = nc.scalar
    PE = nc.tensor
    GP = nc.gpsimd
    SY = nc.sync

    def vo(method, *a, **kw):
        V.wait_ge(sDV.h, sDV.v)
        return sDV.inc(method(*a, **kw))

    def so(method, *a, **kw):
        SC.wait_ge(sSC.h, sSC.v)
        return sSC.inc(method(*a, **kw))

    def po(method, *a, **kw):
        GP.wait_ge(sPL.h, sPL.v)
        return sPL.inc(method(*a, **kw))

    def sy_dma(sem, out, in_, **kw):
        SY.wait_ge(sem.h, sem.v)
        SY.dma_start(out=out, in_=in_, **kw).then_inc(sem.h, 16)
        sem.v += 16
        return sem.v

    def gp_dma(sem, out, in_, **kw):
        GP.wait_ge(sem.h, sem.v)
        GP.dma_start(out=out, in_=in_, **kw).then_inc(sem.h, 16)
        sem.v += 16
        return sem.v

    sps_last_exp = [0, 0, 0]
    sps_last_dve = [0, 0, 0]

    for dst, srcc in [(on1b, ones_1x128b), (on1f, ones_1x128f), (onc, ones_128x1b),
                      (id128, id128_c), (id128b, id128b_c), (bqc, bq_d), (bkc, bk_d)]:
        sy_dma(sCN, dst[:], srcc)
    v_consts = sCN.v
    gp_dma(sGA, lusb[:], lu_d)
    gp_dma(sGA, srow[0:1, 0:8], iota8_c)
    gp_dma(sGA, bpr_b[0:1, :], bp_d)
    gp_dma(sGA, padbs[:], pb_d)
    v_small = sGA.v
    po(GP.iota, iotac[:], pattern=[[0, 1]], base=0, channel_multiplier=1)
    v_iota = po(GP.tensor_copy, out=iotacf[:], in_=iotac[:])

    vo(V.memset, cst[:, 0:1], EPS)
    v_cst = vo(V.memset, cst[:, 1:2], -LOG8)

    SC.wait_ge(sGA.h, v_small)
    so(SC.activation, out=gsc[0:1, 0:8], in_=lusb[0:1, 8:16], func=AF.Ln)
    v_g = so(SC.activation, out=gsc[0:1, 8:16], in_=gsc[0:1, 0:8], func=AF.Ln, scale=-1.0)
    V.wait_ge(sSC.h, v_g)
    vo(V.tensor_tensor, out=gsc[0:1, 0:8], in0=lusb[0:1, 0:8], in1=gsc[0:1, 8:16], op=ALU.subtract)
    vo(V.reduce_max, out=srow[0:1, 26:27], in_=gsc[0:1, 0:8], axis=AX.X)
    vo(V.tensor_scalar, out=gsc[0:1, 8:16], in0=gsc[0:1, 0:8], scalar1=srow[0:1, 26:27],
       scalar2=None, op0=ALU.is_ge)
    vo(V.tensor_tensor, out=gsc[0:1, 8:16], in0=gsc[0:1, 8:16], in1=srow[0:1, 0:8], op=ALU.mult)
    vo(V.reduce_max, out=srow[0:1, 27:28], in_=gsc[0:1, 8:16], axis=AX.X)
    v_idxi = vo(V.tensor_copy, out=idxi[:], in_=srow[0:1, 27:28])

    vo(V.tensor_scalar, out=srow[0:1, 28:29], in0=srow[0:1, 27:28], scalar1=1024.0,
       scalar2=None, op0=ALU.mult)
    v_idxb = vo(V.tensor_copy, out=sbf[0:1, 0:1], in_=srow[0:1, 28:29])
    PE.wait_ge(sDV.h, v_idxb)
    PE.wait_ge(sCN.h, v_consts)
    v_pe_idx = sPE.inc(PE.matmul(spp[:, 0:1], on1b[:], sbf[0:1, 0:1], start=True, stop=True))
    V.wait_ge(sPE.h, v_pe_idx)
    vo(V.tensor_copy, out=gcolf[:], in_=spp[:, 0:1])
    V.wait_ge(sPL.h, v_iota)
    vo(V.tensor_tensor, out=tmpc[:], in0=iotacf[:], in1=gcolf[:], op=ALU.add)
    for c in range(8):
        vo(V.tensor_scalar, out=gcol8f[:, c:c + 1], in0=tmpc[:], scalar1=float(128 * c),
           scalar2=None, op0=ALU.add)
    v_gcol = vo(V.tensor_copy, out=gcol[:], in_=gcol8f[:])

    SC.wait_ge(sDV.h, v_cst)

    def load_wbf(wd, wbf, stage=None):
        GP.wait_ge(sPE.h, sPE.v)
        for c in range(8):
            gp_dma(sWL, wbf[:, c, :], wd[128 * c:128 * c + 128, :])
        return sWL.v

    ps_ctr = [0]
    ps_evac = [sps_last_dve[0], sps_last_dve[0], 0, 0, sps_last_dve[2], sps_last_dve[2]]

    def ps_view(i):
        return sps[i // 2][:, 512 * (i % 2):512 * (i % 2) + 512]

    def proj_mm(lhsT_fn, rhs_fn, evac_fn):
        i = ps_ctr[0] % 6
        ps = ps_view(i)
        ps_ctr[0] += 1
        if ps_evac[i]:
            PE.wait_ge(sDV.h, ps_evac[i])
        last = None
        for c in range(8):
            last = PE.matmul(ps, lhsT_fn(c), rhs_fn(c), start=(c == 0), stop=(c == 7))
        v_pe = sPE.inc(last)
        V.wait_ge(sPE.h, v_pe)
        evac_fn(ps)
        ps_evac[i] = sDV.v

    def emit_ln2t(xbuf, tokdst, base):
        ls = lnst[:].rearrange("p (t w) -> p t w", w=16)
        for t in range(2):
            vo(V.bn_stats, out=ls[:, t, 0:6], in_=xbuf[:, t, 0:512])
            vo(V.bn_stats, out=ls[:, t, 6:12], in_=xbuf[:, t, 512:1024])
            vo(V.bn_aggr, out=ls[:, t, 12:14], in_=ls[:, t, 0:12])
            vo(V.tensor_scalar, out=ls[:, t, 14:15], in0=ls[:, t, 12:13],
               scalar1=-1.0, scalar2=None, op0=ALU.mult)
        v_stats = sDV.v
        SC.wait_ge(sDV.h, v_stats)
        so(SC.activation, out=ls[:, :, 15:16], in_=ls[:, :, 13:14], func=AF.Ln,
           bias=cst[:, 0:1], scale=1.0)
        v_sq = so(SC.activation, out=ls[:, :, 15:16], in_=ls[:, :, 15:16], func=AF.Exp, scale=-0.5)
        V.wait_ge(sSC.h, v_sq)
        for t in range(2):
            vo(V.tensor_scalar, out=tokdst(base + t), in0=xbuf[:, t, :],
               scalar1=ls[:, t, 14:15], scalar2=ls[:, t, 15:16],
               op0=ALU.add, op1=ALU.mult)
        return sDV.v, v_sq

    def emit_path(xd, ntok, sg_fns, prefix, xbufB=None):
        nsg = ntok // 512
        est_p = contextlib.ExitStack()
        xbuf = est_p.enter_context(nc.sbuf_tensor(prefix + "_xb", [128, 2, C], F32))
        tok4 = est_p.enter_context(nc.sbuf_tensor(prefix + "_tok", [128, 8, 4, 128], BF16))
        fm2 = est_p.enter_context(nc.sbuf_tensor(prefix + "_fm", [128, 2, 8, 512], BF16))
        nh = 2 * nsg
        grps = [xbuf, xbufB if xbufB is not None else xbuf]
        depth = 2 if xbufB is not None else 1
        v_loads = [0] * (nh + 2)
        for g in range(min(depth, nh)):
            for tt in range(2):
                row0 = 256 * g + 128 * tt
                sy_dma(sIN, grps[g % 2][:, tt, :], xd[row0:row0 + 128, :])
            v_loads[g] = sIN.v
        vtr_prev = 0
        v_pe_fm = 0

        def emit_ln_sg(sg):
            for h in range(2):
                g = 2 * sg + h
                V.wait_ge(sIN.h, v_loads[g])
                if vtr_prev and h == 0:
                    V.wait_ge(sTR.h, vtr_prev)
                v_ln, v_sq = emit_ln2t(grps[g % 2], lambda tt: tok4[:, :, tt, :], 2 * h)
                if g + depth < nh:
                    SY.wait_ge(sDV.h, v_ln)
                    for tt in range(2):
                        row0 = 256 * (g + depth) + 128 * tt
                        sy_dma(sIN, grps[(g + depth) % 2][:, tt, :], xd[row0:row0 + 128, :])
                    v_loads[g + depth] = sIN.v
            return sDV.v

        v_ln_done = emit_ln_sg(0)
        v_pe_fm2 = [0, 0]
        for sg in range(nsg):
            fm = fm2[:, sg % 2, :, :]
            SY.wait_ge(sDV.h, v_ln_done)
            if v_pe_fm2[sg % 2]:
                SY.wait_ge(sPE.h, v_pe_fm2[sg % 2])
            sy_dma(sTR, fm.rearrange("p c f -> p (c f)").rearrange("p (m g) -> p m g", g=128),
                   tok4[:].rearrange("p a b f -> p (a b f)"), transpose=True)
            vtr = sTR.v
            vtr_prev = vtr
            if sg + 1 < nsg:
                v_ln_done = emit_ln_sg(sg + 1)
            PE.wait_ge(sTR.h, vtr)
            for fn in sg_fns:
                fn(sg, fm)
            v_pe_fm2[sg % 2] = sPE.v
        est_p.close()

    est_w = contextlib.ExitStack()
    wkb = est_w.enter_context(nc.sbuf_tensor("wkb", [128, 8, C], BF16))
    bvb = est_w.enter_context(nc.sbuf_tensor("bvb", [128, C], BF16))
    gp_dma(sGA, bvb[0:1, :], bv_d)
    PE.wait_ge(sGA.h, sGA.v)
    PE.wait_ge(sDV.h, v_gcol)
    for half in range(2):
        v_pe_b = sPE.inc(PE.matmul(spp[:, 0:512], on1b[:], bvb[0:1, 512 * half:512 * half + 512],
                                   start=True, stop=True))
        V.wait_ge(sPE.h, v_pe_b)
        vo(V.tensor_copy, out=bvb[:, 512 * half:512 * half + 512], in_=spp[:, 0:512])
        PE.wait_ge(sDV.h, sDV.v)
    sps_last_dve[2] = sDV.v
    v_wk = load_wbf(wk_d, wkb)
    v_wv = load_wbf(wv_d, wvb)

    wgr = Qsb[:, 4:8, :].rearrange("p a (b f) -> p (a b) f", b=2)
    gsems = [Sem(nc, f"sG{c}") for c in range(8)]
    GP.wait_ge(sDV.h, v_gcol)
    for c in range(8):
        GP.indirect_dma_start(out=wgr[:, c, :], out_offset=None, in_=wf_d,
                              in_offset=IndirectOffsetOnAxis(gcol[:, c:c + 1], 0)).then_inc(gsems[c].h, 16)
        gsems[c].v += 16
    for c in range(8):
        GP.wait_ge(gsems[c].h, 16)
        gp_dma(sWS, weff_d[c], wgr[:, c, :])
    v_weff = sWS.v
    sBF = Sem(nc, "sBF")
    vo(V.tensor_scalar, out=tmpc[0:2, 0:1], in0=gcolf[0:2, 0:1],
       scalar1=1.0 / 1024.0, scalar2=None, op0=ALU.mult)
    v_bfo = vo(V.tensor_copy, out=bfo[:], in_=tmpc[0:2, 0:1])
    GP.wait_ge(sDV.h, v_bfo)
    GP.indirect_dma_start(out=beff_b[:], out_offset=None, in_=bf_d,
                          in_offset=IndirectOffsetOnAxis(bfo[:], 0)).then_inc(sBF.h, 16)
    sBF.v += 16
    v_kvw = vo(V.memset, Vsb[:].rearrange("p t (h c) -> p (t h) c", c=65)[:, :, 64:65], 1.0)

    def kproj(sg, fm):
        PE.wait_ge(sDV.h, v_kvw)
        PE.wait_ge(sWL.h, v_wk)
        for e in range(8):
            def lf(c, e=e):
                return wkb[:, c, 128 * e:128 * e + 128]

            def rf(c, fm=fm):
                return fm[:, c, :]

            def ev(ps, e=e, sg=sg):
                vo(V.tensor_scalar, out=Ksb[:, e, 512 * sg:512 * sg + 512], in0=ps,
                   scalar1=bkc[:, e:e + 1], scalar2=None, op0=ALU.add)
            proj_mm(lf, rf, ev)

    def vproj(sg, fm):
        PE.wait_ge(sWL.h, v_wv)
        for sc in range(4):
            for eh in range(2):
                def lf(c, fm=fm, sc=sc):
                    return fm[:, c, 128 * sc:128 * sc + 128]

                def rf(c, eh=eh):
                    return wvb[:, c, 512 * eh:512 * eh + 512]

                def ev(ps, sg=sg, sc=sc, eh=eh):
                    tile = 4 * sg + sc
                    dst = Vsb[:, tile, :].rearrange("p (h c) -> p h c", c=65)[:, 8 * eh:8 * eh + 8, 0:64]
                    vo(V.tensor_tensor, out=dst, in0=ps.rearrange("p (h c) -> p h c", c=64),
                       in1=bvb[:, 512 * eh:512 * eh + 512].rearrange("p (h c) -> p h c", c=64),
                       op=ALU.add)
                proj_mm(lf, rf, ev)

    xbufB_kv = Qsb[:, 0:2, :].bitcast(F32)
    emit_path(xkv_d, KT, [kproj, vproj], "kv", xbufB=xbufB_kv)

    V.wait_ge(sWS.h, v_weff)
    v_wq = load_wbf(wq_d, wkb)

    def qproj(sg, fm):
        PE.wait_ge(sWL.h, v_wq)
        for e in range(8):
            def lf(c, e=e):
                return wkb[:, c, 128 * e:128 * e + 128]

            def rf(c, fm=fm):
                return fm[:, c, :]

            def ev(ps, e=e, sg=sg):
                vo(V.tensor_scalar, out=Qsb[:, e, 512 * sg:512 * sg + 512], in0=ps,
                   scalar1=bqc[:, e:e + 1], scalar2=None, op0=ALU.add)
            proj_mm(lf, rf, ev)

    emit_path(xq_d, TQ, [qproj], "q")

    v_wp_wl = load_wbf(wp_d, wvb)
    wpb = wvb
    est_w.close()
    scr = sb("scr", [128, 4096], BF16)
    V.wait_ge(sPE.h, sPE.v)

    v_spp_read = sDV.v
    for hp in range(8):
        for src in [Qsb, Ksb]:
            n = TQ if src is Qsb else KT
            nwin = n // 128
            v_sq = vo(V.tensor_tensor, out=scr[:, 0:n], in0=src[:, hp, 0:n],
                      in1=src[:, hp, 0:n], op=ALU.mult)
            PE.wait_ge(sDV.h, max(v_sq, v_spp_read))
            last = None
            for w in range(nwin):
                last = PE.matmul(spp[:, w:w + 1], scr[:, 128 * w:128 * w + 128], onc[:],
                                 start=True, stop=True)
            v_pe1 = sPE.inc(last)
            V.wait_ge(sPE.h, v_pe1)
            v_rm = vo(V.reduce_max, out=tmpc[:], in_=spp[:, 0:nwin], axis=AX.X)
            PE.wait_ge(sDV.h, v_rm)
            v_pe2 = sPE.inc(PE.transpose(spp[0:1, 0:128], tmpc[:], id128[:]))
            V.wait_ge(sPE.h, v_pe2)
            cidx = hp if src is Qsb else 8 + hp
            v_spp_read = vo(V.reduce_max, out=crow[0:1, cidx:cidx + 1],
                            in_=spp[0:1, 0:128], axis=AX.X)
    vo(V.tensor_tensor, out=crow[0:1, 0:8], in0=crow[0:1, 0:8], in1=crow[0:1, 8:16], op=ALU.mult)
    v_c1 = vo(V.tensor_scalar, out=crow[0:1, 0:8], in0=crow[0:1, 0:8], scalar1=1e-12,
              scalar2=None, op0=ALU.max)
    SC.wait_ge(sDV.h, v_c1)
    so(SC.activation, out=crow[0:1, 0:8], in_=crow[0:1, 0:8], func=AF.Ln)
    v_c2 = so(SC.activation, out=crow[0:1, 0:8], in_=crow[0:1, 0:8], func=AF.Exp,
              scale=0.5, bias=cst[0:1, 1:2])
    V.wait_ge(sSC.h, v_c2)
    vo(V.tensor_scalar, out=crow[0:1, 0:8], in0=crow[0:1, 0:8], scalar1=0.5, scalar2=-1.0,
       op0=ALU.add, op1=ALU.mult)
    v_c3 = vo(V.tensor_copy, out=sbf[0:1, 0:8], in_=crow[0:1, 0:8])
    PE.wait_ge(sDV.h, v_c3)
    v_pe3 = sPE.inc(PE.matmul(spp[:, 0:8], on1b[:], sbf[0:1, 0:8], start=True, stop=True))
    V.wait_ge(sPE.h, v_pe3)
    v_negc = vo(V.tensor_copy, out=negc[:], in_=spp[:, 0:8])
    sps_last_dve[2] = v_negc
    V.wait_ge(sGA.h, v_small)
    for w in range(12):
        vo(V.tensor_scalar, out=negcP[:, w, :], in0=negc[:],
           scalar1=padbs[:, w:w + 1], scalar2=None, op0=ALU.add)
    v_negcP = sDV.v

    v_projs_done = sDV.v
    if "dbg_negc" in dbg:
        SY.wait_ge(sDV.h, v_negc)
        SY.dma_start(out=dbg["dbg_negc"], in_=negc[:]).then_inc(sOUT.h, 16); sOUT.v += 16
    if "dbg_q" in dbg:
        SY.wait_ge(sDV.h, v_projs_done)
        SY.dma_start(out=dbg["dbg_q"], in_=Qsb[:]).then_inc(sOUT.h, 16); sOUT.v += 16
    if "dbg_k" in dbg:
        SY.wait_ge(sDV.h, v_projs_done)
        SY.dma_start(out=dbg["dbg_k"], in_=Ksb[:]).then_inc(sOUT.h, 16); sOUT.v += 16
    if "dbg_v" in dbg:
        SY.wait_ge(sDV.h, v_projs_done)
        SY.dma_start(out=dbg["dbg_v"], in_=Vsb[:]).then_inc(sOUT.h, 16); sOUT.v += 16
    if "dbg_weff" in dbg:
        SY.wait_ge(sWS.h, v_weff)
        SY.dma_start(out=dbg["dbg_weff"], in_=weff_d).then_inc(sOUT.h, 16); sOUT.v += 16
    v_dbg_gate = [sOUT.v]

    nc.all_engine_barrier()
    mksb = sb("mksb", [128, 6, 512], BF16)
    Ynsb = sb("Ynsb", [128, 8, 512], BF16)
    zinvr = sb("zinvr", [1, 1024], BF16)
    wffb = sb("wffb", [128, 8, C], BF16)
    Yevb = sb("Yevb", [65, C], F32)
    tbuf = sb("tbuf", [128, C], F32)
    x1buf = sb("x1buf", [128, C], F32)
    ln3 = sb("ln3", [128, C], BF16)
    ln3T = sb("ln3t", [128, C], BF16)

    SY.wait_ge(sWS.h, v_weff)
    sy_dma(sIN, wffb[:], weff_d.rearrange("c p f -> p c f"))
    v_wffb = sIN.v

    Pv = scr[:].rearrange("p (r f) -> p r f", r=4)

    def yn_view(s):
        if s % 2 == 0:
            return Ynsb[:]
        return Qsb[:, :, 512 * (s - 1):512 * (s - 1) + 512]

    v_mask_free = 0
    v_pring = {"spe": 0}
    v_ya_free = [0]
    v_zinv_free_pe = [0]
    v_attn_start = sDV.v
    yn_ready = {}
    pend_epi = [None]
    pend_tail = [None]

    pend_ffn = [None]
    v_ffn_last = [0]

    def ln_p2(xin, xnout):
        vo(V.bn_stats, out=p2s[:, 0:6], in_=xin[:, 0:512])
        vo(V.bn_stats, out=p2s[:, 6:12], in_=xin[:, 512:1024])
        vo(V.bn_aggr, out=p2s[:, 12:14], in_=p2s[:, 0:12])
        v2 = vo(V.tensor_scalar, out=p2s[:, 14:15], in0=p2s[:, 12:13],
                scalar1=-1.0, scalar2=None, op0=ALU.mult)
        SC.wait_ge(sDV.h, v2)
        so(SC.activation, out=p2s[:, 15:16], in_=p2s[:, 13:14], func=AF.Ln,
           bias=cst[:, 0:1], scale=1.0)
        v3 = so(SC.activation, out=p2s[:, 15:16], in_=p2s[:, 15:16], func=AF.Exp, scale=-0.5)
        V.wait_ge(sSC.h, v3)
        return vo(V.tensor_scalar, out=xnout, in0=xin, scalar1=p2s[:, 14:15],
                  scalar2=p2s[:, 15:16], op0=ALU.add, op1=ALU.mult)

    v_xq_pref = [0]

    def emit_ffn_parcel(pref_row0):
        st = {}
        fillers = []
        pend = pend_ffn[0]
        pend_ffn[0] = None
        if pend is not None:
            row02, v_l3t2 = pend
            l3v = ln3T[:].rearrange("p (a f) -> p a f", a=8)

            def mmh(eh):
                def f():
                    if eh == 0:
                        PE.wait_ge(sTR.h, v_l3t2)
                        PE.wait_ge(sIN.h, v_wffb)
                        PE.wait_ge(sDV.h, sps_last_dve[2])
                        PE.wait_ge(sBF.h, 16)
                        if sps_last_exp[2]:
                            PE.wait_ge(sSC.h, sps_last_exp[2])
                    last = None
                    for cc in range(8):
                        last = PE.matmul(sps2[:, 512 * eh:512 * eh + 512], l3v[:, cc, :],
                                         wffb[:, cc, 512 * eh:512 * eh + 512],
                                         start=(cc == 0), stop=False)
                    last = PE.matmul(sps2[:, 512 * eh:512 * eh + 512], on1b[:],
                                     beff_b[0:1, 512 * eh:512 * eh + 512], start=False, stop=True)
                    st[eh] = sPE.inc(last)
                return f
            fillers = [mmh(0), mmh(1)]

        def tail():
            if pend is not None:
                v_ffn = st[1]
                v_ffn_last[0] = v_ffn
                V.wait_ge(sPE.h, v_ffn)
                v_ln4 = ln_p2(sps2[:], tbuf[:])
                sps_last_dve[2] = v_ln4
                v_outc = vo(V.tensor_tensor, out=x1buf[:], in0=x1buf[:], in1=tbuf[:], op=ALU.add)
                SY.wait_ge(sDV.h, v_outc)
                sy_dma(sOUT, out_d[row02:row02 + 128, :], x1buf[:])
            if pref_row0 is not None:
                sy_dma(sM2, tbuf[:], xq_d[pref_row0:pref_row0 + 128, :])
                v_xq_pref[0] = sM2.v
        return fillers, tail

    def emit_op_parcel(ssrc, k):
        Ysrc = yn_view(ssrc)
        row0 = 512 * ssrc + 128 * k
        st = {}

        def mmh(eh):
            def f():
                if eh == 0:
                    PE.wait_ge(sDV.h, max(yn_ready[ssrc], sps_last_dve[2]))
                    PE.wait_ge(sWL.h, v_wp_wl)
                    PE.wait_ge(sGA.h, v_small)
                    if sps_last_exp[2]:
                        PE.wait_ge(sSC.h, sps_last_exp[2])
                last = None
                for hp2 in range(8):
                    last = PE.matmul(sps2[:, 512 * eh:512 * eh + 512],
                                     Ysrc[:, hp2, 128 * k:128 * k + 128],
                                     wpb[:, hp2, 512 * eh:512 * eh + 512],
                                     start=(hp2 == 0), stop=False)
                last = PE.matmul(sps2[:, 512 * eh:512 * eh + 512], on1b[:],
                                 bpr_b[0:1, 512 * eh:512 * eh + 512], start=False, stop=True)
                st[eh] = sPE.inc(last)
            return f

        def tail():
            v_op = st[1]
            v_out_prev = sOUT.v
            V.wait_ge(sPE.h, v_op)
            V.wait_ge(sOUT.h, v_out_prev)
            v_ln2 = ln_p2(sps2[:], x1buf[:])
            sps_last_dve[2] = v_ln2
            V.wait_ge(sM2.h, v_xq_pref[0])
            vo(V.tensor_tensor, out=x1buf[:], in0=x1buf[:], in1=tbuf[:], op=ALU.add)
            v_ln3 = ln_p2(x1buf[:], ln3[:])
            SY.wait_ge(sDV.h, v_ln3)
            SY.wait_ge(sPE.h, v_ffn_last[0])
            sy_dma(sTR, ln3T[:].rearrange("p (a f) -> p a f", a=8), ln3[:], transpose=True)
            pend_ffn[0] = (row0, sTR.v)
        return [mmh(0), mmh(1)], tail

    for slot in range(NSLOT):
        L = LCH[slot]
        v_pool_slot = sPL.v
        if v_mask_free:
            GP.wait_ge(sPE.h, v_mask_free)
        gp_dma(sGA, mksb[:], mk_d[6 * slot:6 * slot + 6].rearrange("w p f -> p w f"))
        v_mk = sGA.v

        for hp in range(8):
            fillers, parcel_tail = [], None
            if slot >= 1:
                if hp % 2 == 0:
                    fillers, parcel_tail = emit_ffn_parcel(512 * (slot - 1) + 128 * (hp // 2))
                else:
                    fillers, parcel_tail = emit_op_parcel(slot - 1, hp // 2)

            v_exp = [0] * L
            v_pvc = [0] * L

            def emit_qk(c):
                ti = c % 2
                ps = sps[ti]
                msk = c >= L - 6
                PE.wait_ge(sDV.h, max(v_negcP, v_attn_start, sps_last_dve[ti]))
                if sps_last_exp[ti]:
                    PE.wait_ge(sSC.h, sps_last_exp[ti])
                PE.matmul(ps[:, 0:512], Ksb[0:64, hp, 128 * c:128 * c + 128],
                          Qsb[0:64, hp, 512 * slot:512 * slot + 512],
                          start=True, stop=not msk, tile_position=(0, 0))
                last = PE.matmul(ps[:, 512:1024], Ksb[64:128, hp, 128 * c:128 * c + 128],
                                 Qsb[64:128, hp, 512 * slot:512 * slot + 512],
                                 start=True, stop=not msk, tile_position=(64, 0))
                if msk:
                    j = c - (L - 6)
                    PE.wait_ge(sGA.h, v_mk)
                    PE.matmul(ps[:, 0:512], id128b[:], mksb[:, j, :],
                              start=False, stop=True)
                    last = PE.matmul(ps[:, 512:1024], id128b[:], mksb[:, j, :],
                                     start=False, stop=True)
                return sPE.inc(last)

            def emit_exp(c, v_qk):
                pr = c % 4
                SC.wait_ge(sPE.h, v_qk)
                if c >= 4:
                    SC.wait_ge(sPE.h, v_pvc[c - 4])
                else:
                    if v_pring["spe"]:
                        SC.wait_ge(sPE.h, v_pring["spe"])
                if c >= L - 3:
                    bias_ap = negcP[:, 3 * slot + (c - (L - 3)), hp:hp + 1]
                else:
                    bias_ap = negc[:, hp:hp + 1]
                v_exp[c] = so(SC.activation, out=Pv[:, pr, :], in_=sps[c % 2][:], func=AF.Exp,
                              bias=bias_ap, scale=0.125)
                sps_last_exp[c % 2] = v_exp[c]

            def emit_pv(c, hp=hp, L=L, v_exp=v_exp, v_pvc=v_pvc):
                pr = c % 4
                if c == 0 and v_ya_free[0]:
                    PE.wait_ge(sDV.h, v_ya_free[0])
                PE.wait_ge(sSC.h, v_exp[c])
                PE.matmul(ya[:], Vsb[:, c, 65 * (2 * hp):65 * (2 * hp) + 65],
                          Pv[:, pr, 0:512], start=(c == 0), stop=(c == L - 1))
                v_pvc[c] = sPE.inc(PE.matmul(yb[:], Vsb[:, c, 65 * (2 * hp + 1):65 * (2 * hp + 1) + 65],
                                             Pv[:, pr, 512:1024], start=(c == 0), stop=(c == L - 1)))

            for c in range(L):
                v_qk = emit_qk(c)
                if c == 0 and pend_tail[0] is not None:
                    pend_tail[0]()
                    pend_tail[0] = None
                emit_exp(c, v_qk)
                if pend_epi[0] is not None:
                    if c == 1:
                        pend_epi[0][0]()
                    elif c == 2:
                        pend_epi[0][1]()
                    elif c == 3:
                        pend_epi[0] = None
                        emit_pv(0)
                        emit_pv(1)
                        emit_pv(2)
                elif 1 <= c <= L - 2:
                    emit_pv(c - 1)
                if c in (4, 6) and fillers:
                    fillers.pop(0)()
            for f in fillers:
                f()
            if parcel_tail is not None:
                parcel_tail()

            def mk_tail(emit_pv_e, v_pvc_e, slot_e, hp_e, L_e):
                def tail():
                    emit_pv_e(L_e - 2)
                    emit_pv_e(L_e - 1)
                    V.wait_ge(sPE.h, v_pvc_e[L_e - 1])
                    vo(V.tensor_copy, out=Yevb[0:64, 0:512], in_=ya[0:64, :])
                    v_yev = vo(V.tensor_copy, out=Yevb[0:64, 512:1024], in_=yb[0:64, :])
                    v_pring["spe"] = v_pvc_e[L_e - 1]
                    pend_epi[0] = mk_epi(slot_e, hp_e, v_yev, v_pvc_e[L_e - 1])
                return tail

            def mk_epi(slot_e, hp_e, v_yev_e, v_pv_last):
                st = {}

                def emit_zchain():
                    SC.wait_ge(sPE.h, max(v_pv_last, v_zinv_free_pe[0]))
                    so(SC.activation, out=Yevb[64:65, 0:512], in_=ya[64:65, :], func=AF.Ln)
                    so(SC.activation, out=Yevb[64:65, 512:1024], in_=yb[64:65, :], func=AF.Ln)
                    v_zi = so(SC.activation, out=zinvr[:], in_=Yevb[64:65, :], func=AF.Exp, scale=-1.0)
                    st["v_zi"] = v_zi

                def emit_ynmul():
                    ynv_e = yn_view(slot_e)
                    PE.wait_ge(sSC.h, st["v_zi"])
                    PE.wait_ge(sDV.h, v_yev_e)
                    PE.matmul(ya[0:64, :], on1b[0:1, 0:64], zinvr[0:1, 0:512],
                              start=True, stop=True)
                    v_zbB = sPE.inc(PE.matmul(yb[0:64, :], on1b[0:1, 0:64],
                                              zinvr[0:1, 512:1024], start=True, stop=True))
                    v_zinv_free_pe[0] = v_zbB
                    V.wait_ge(sPE.h, v_zbB)
                    if dbg and v_dbg_gate[0]:
                        V.wait_ge(sOUT.h, v_dbg_gate[0])
                    vo(V.tensor_tensor, out=ynv_e[0:64, hp_e, :], in0=Yevb[0:64, 0:512],
                       in1=ya[0:64, :], op=ALU.mult)
                    v_done = vo(V.tensor_tensor, out=ynv_e[64:128, hp_e, :],
                                in0=Yevb[0:64, 512:1024], in1=yb[0:64, :], op=ALU.mult)
                    v_ya_free[0] = v_done
                    if hp_e == 7:
                        yn_ready[slot_e] = v_done
                        if "dbg_yn" in dbg:
                            SY.wait_ge(sDV.h, v_done)
                            SY.dma_start(out=dbg["dbg_yn"][:, :, 512 * slot_e:512 * slot_e + 512],
                                         in_=ynv_e).then_inc(sOUT.h, 16)
                            sOUT.v += 16
                            v_dbg_gate[0] = sOUT.v

                return emit_zchain, emit_ynmul

            pend_tail[0] = mk_tail(emit_pv, v_pvc, slot, hp, L)

        v_mask_free = sPE.v

    pend_tail[0]()
    pend_tail[0] = None
    pend_epi[0][0]()
    pend_epi[0][1]()
    pend_epi[0] = None

    for k in range(4):
        fl, tl = emit_ffn_parcel(512 * 3 + 128 * k)
        for f in fl:
            f()
        tl()
        fl, tl = emit_op_parcel(3, k)
        for f in fl:
            f()
        tl()
    fl, tl = emit_ffn_parcel(None)
    for f in fl:
        f()
    tl()

    SY.wait_ge(sOUT.h, sOUT.v)
    est.close()
    return nc


_NC_CACHE = {}


def _get_nc():
    if "nc" not in _NC_CACHE:
        _NC_CACHE["nc"] = build_nc()
    return _NC_CACHE["nc"]


def kernel(L0_attn_x, L0_attn_Wq, L0_attn_bq, L0_attn_Wk, L0_attn_bk,
           L0_attn_Wv, L0_attn_bv, L0_attn_Wp, L0_attn_bp,
           L0_ffn_W, L0_ffn_b, L0_ffn_logits, L0_ffn_u):
    x = np.ascontiguousarray(np.asarray(L0_attn_x, dtype=np.float32))
    wq = np.ascontiguousarray(np.asarray(L0_attn_Wq, dtype=np.float32))
    wk = np.ascontiguousarray(np.asarray(L0_attn_Wk, dtype=np.float32))
    wv = np.ascontiguousarray(np.asarray(L0_attn_Wv, dtype=np.float32))
    wp = np.ascontiguousarray(np.asarray(L0_attn_Wp, dtype=np.float32))
    bq = np.asarray(L0_attn_bq, dtype=np.float32)
    bk = np.asarray(L0_attn_bk, dtype=np.float32)
    bv = np.asarray(L0_attn_bv, dtype=np.float32)
    bp = np.asarray(L0_attn_bp, dtype=np.float32)
    wf = np.ascontiguousarray(np.asarray(L0_ffn_W, dtype=np.float32).reshape(8 * C, C))
    bf = np.ascontiguousarray(np.asarray(L0_ffn_b, dtype=np.float32))
    lg = np.asarray(L0_ffn_logits, dtype=np.float32)
    u = np.asarray(L0_ffn_u, dtype=np.float32)

    bq_c = np.ascontiguousarray(bq.reshape(8, 128).T)
    bk_c = np.ascontiguousarray(bk.reshape(8, 128).T)
    bv_r = np.ascontiguousarray(bv.reshape(1, C))
    bp_r = np.ascontiguousarray(bp.reshape(1, C))
    lu = np.ascontiguousarray(np.concatenate([lg, u]).reshape(1, 16))
    masks = [host_masks(0), host_masks(1)]
    padbs_h = [host_padb(0), host_padb(1)]

    in_maps = []
    for core in range(8):
        b, r = core // 2, core % 2
        xb = x[b]
        rows = np.concatenate([np.arange(512 * j, 512 * j + 512) for j in QB[r]])
        xq = np.ascontiguousarray(xb[rows])
        xkv = np.ascontiguousarray(xb.reshape(T // 4, 4, C)[:, 0:3, :].reshape(KT, C))
        in_maps.append({
            "xq": xq, "xkv": xkv, "wq": wq, "wk": wk, "wv": wv, "wp": wp,
            "bq": bq_c, "bk": bk_c, "bv": bv_r, "bp": bp_r,
            "wf": wf, "bf": bf, "lu": lu, "mk": masks[r], "padb": padbs_h[r],
        })

    nc = _get_nc()
    res = run_bass_kernel_spmd(nc, in_maps, core_ids=list(range(8)))

    out = np.empty((B, T, C), dtype=np.float32)
    for core in range(8):
        b, r = core // 2, core % 2
        o = res.results[core]["out"]
        for i, j in enumerate(QB[r]):
            out[b, 512 * j:512 * j + 512] = o[512 * i:512 * i + 512]
    return out


# version 22
# speedup vs baseline: 1.0480x; 1.0122x over previous
import contextlib
import numpy as np
import ml_dtypes
import concourse.bass as bass
import concourse.mybir as mybir
from concourse.bass_utils import run_bass_kernel_spmd
from concourse.bass_core_rust import IndirectOffsetOnAxis

F32 = mybir.dt.float32
BF16 = mybir.dt.bfloat16
I32 = mybir.dt.int32
AF = mybir.ActivationFunctionType
ALU = mybir.AluOpType
AX = mybir.AxisListType

B, T, C, H, D = 4, 4096, 1024, 16, 64
KT = 3072
NCH = KT // 128
QB = [[0, 3, 4, 7], [1, 2, 5, 6]]
LCH = [6, 12, 18, 24]
NSLOT = 4
TQ = 2048
EPS = 1e-5
LOG8 = float(np.log(8.0))

DEBUG_OUTS = []


def _orig(j):
    return 4 * (j // 3) + j % 3


def host_masks(r):
    m = np.zeros((24, 128, 512), dtype=np.float32)
    for i in range(NSLOT):
        L = LCH[i]
        t = 3 * QB[r][i] + 3
        for j in range(6):
            c = L - 6 + j
            w = 6 * i + j
            if c < t and c >= t - 3:
                k = c - (t - 3)
                s = np.arange(128)
                og = _orig(128 * k + s)
                m[w] = np.where(og[:, None] <= np.arange(512)[None, :], 0.0, -60000.0)
    return np.ascontiguousarray(m)


def host_padb(r):
    p = np.zeros((128, 12), dtype=np.float32)
    for i in range(NSLOT):
        L = LCH[i]
        t = 3 * QB[r][i] + 3
        for k in range(3):
            if L - 3 + k >= t:
                p[:, 3 * i + k] = -1e30
    return np.ascontiguousarray(p)


class Sem:
    def __init__(self, nc, name):
        self.h = nc.alloc_semaphore(name=name)
        self.v = 0

    def inc(self, inst, k=1):
        inst.then_inc(self.h, k)
        self.v += k
        return self.v


def build_nc():
    nc = bass.Bass("TRN2", target_bir_lowering=False)

    xq_d = nc.dram_tensor("xq", [TQ, C], F32, kind="ExternalInput").ap()
    xkv_d = nc.dram_tensor("xkv", [KT, C], F32, kind="ExternalInput").ap()
    wq_d = nc.dram_tensor("wq", [C, C], F32, kind="ExternalInput").ap()
    wk_d = nc.dram_tensor("wk", [C, C], F32, kind="ExternalInput").ap()
    wv_d = nc.dram_tensor("wv", [C, C], F32, kind="ExternalInput").ap()
    wp_d = nc.dram_tensor("wp", [C, C], F32, kind="ExternalInput").ap()
    bq_d = nc.dram_tensor("bq", [128, 8], F32, kind="ExternalInput").ap()
    bk_d = nc.dram_tensor("bk", [128, 8], F32, kind="ExternalInput").ap()
    bv_d = nc.dram_tensor("bv", [1, C], F32, kind="ExternalInput").ap()
    bp_d = nc.dram_tensor("bp", [1, C], F32, kind="ExternalInput").ap()
    wf_d = nc.dram_tensor("wf", [8 * C, C], F32, kind="ExternalInput").ap()
    bf_d = nc.dram_tensor("bf", [8, C], F32, kind="ExternalInput").ap()
    lu_d = nc.dram_tensor("lu", [1, 16], F32, kind="ExternalInput").ap()
    mk_d = nc.dram_tensor("mk", [24, 128, 512], F32, kind="ExternalInput").ap()
    pb_d = nc.dram_tensor("padb", [128, 12], F32, kind="ExternalInput").ap()
    out_d = nc.dram_tensor("out", [TQ, C], F32, kind="ExternalOutput").ap()
    weff_d = nc.dram_tensor("weff_scr", [8, 128, C], BF16, kind="Internal").ap()

    dbg = {}
    for name, shape, dt in [
        ("dbg_q", [128, 8, TQ], BF16), ("dbg_k", [128, 8, KT], BF16),
        ("dbg_v", [128, NCH, 1040], BF16), ("dbg_yn", [128, 8, 512 * NSLOT], BF16),
        ("dbg_negc", [128, 8], F32), ("dbg_weff", [8, 128, C], BF16),
    ]:
        if name in DEBUG_OUTS:
            dbg[name] = nc.dram_tensor(name, shape, dt, kind="ExternalOutput").ap()

    ones_1x128b = nc.inline_tensor(np.ones((1, 128), dtype=ml_dtypes.bfloat16), name="c_on1b").ap()
    ones_1x128f = nc.inline_tensor(np.ones((1, 128), dtype=np.float32), name="c_on1f").ap()
    ones_128x1b = nc.inline_tensor(np.ones((128, 1), dtype=ml_dtypes.bfloat16), name="c_onc").ap()
    iota8_c = nc.inline_tensor(np.arange(8, dtype=np.float32).reshape(1, 8), name="c_iota8").ap()
    id128_c = nc.inline_tensor(np.eye(128, dtype=np.float32), name="c_id128").ap()
    id128b_c = nc.inline_tensor(np.eye(128, dtype=ml_dtypes.bfloat16), name="c_id128b").ap()

    est = contextlib.ExitStack()

    def sb(name, shape, dt):
        return est.enter_context(nc.sbuf_tensor(name, shape, dt))

    Ksb = sb("Ksb", [128, 8, KT], BF16)
    Vsb = sb("Vsb", [128, NCH, 1040], BF16)
    Qsb = sb("Qsb", [128, 8, TQ], BF16)
    wvb = sb("wvb", [128, 8, C], BF16)
    bqc = sb("bqc", [128, 8], F32)
    bkc = sb("bkc", [128, 8], F32)
    bpr_b = sb("bprb", [1, C], BF16)
    beff_b = sb("beffb", [2, C], BF16)
    bfo = sb("bfo", [2, 1], I32)
    negc = sb("negc", [128, 8], F32)
    negcP = sb("negcP", [128, 12, 8], F32)
    padbs = sb("padbs", [128, 12], F32)
    on1b = sb("on1b", [1, 128], BF16)
    on1f = sb("on1f", [1, 128], F32)
    onc = sb("onc", [128, 1], BF16)
    id128 = sb("id128", [128, 128], F32)
    id128b = sb("idb", [128, 128], BF16)
    srow = sb("srow", [1, 64], F32)
    sbf = sb("sbf", [1, 8], BF16)
    crow = sb("crow", [1, 16], F32)
    lusb = sb("lusb", [1, 16], F32)
    gsc = sb("gsc", [1, 16], F32)
    idxi = sb("idxi", [1, 1], I32)
    gcol = sb("gcol", [128, 8], I32)
    gcol8f = sb("gcol8f", [128, 8], F32)
    iotac = sb("iotac", [128, 1], I32)
    iotacf = sb("iotacf", [128, 1], F32)
    gcolf = sb("gcolf", [128, 1], F32)
    tmpc = sb("tmpc", [128, 1], F32)
    lnst = sb("lnst", [128, 32], F32)
    cst = sb("cst", [128, 2], F32)
    p2s = sb("p2s", [128, 16], F32)

    sps0 = est.enter_context(nc.psum_tensor("sps0", [128, 1024], F32))
    sps1 = est.enter_context(nc.psum_tensor("sps1", [128, 1024], F32))
    sps2 = est.enter_context(nc.psum_tensor("sps2", [128, 1024], F32))
    ya = est.enter_context(nc.psum_tensor("ya", [65, 512], F32))
    yb = est.enter_context(nc.psum_tensor("yb", [65, 512], F32))
    sps = [sps0, sps1, sps2]
    spp = sps2[:, 0:512]

    sIN = Sem(nc, "sIN")
    sCN = Sem(nc, "sCN")
    sWL = Sem(nc, "sWL")
    sTR = Sem(nc, "sTR")
    sGA = Sem(nc, "sGA")
    sDV = Sem(nc, "sDV")
    sSC = Sem(nc, "sSC")
    sPE = Sem(nc, "sPE")
    sPL = Sem(nc, "sPL")
    sWS = Sem(nc, "sWS")
    sOUT = Sem(nc, "sOUT")
    sM2 = Sem(nc, "sM2")

    V = nc.vector
    SC = nc.scalar
    PE = nc.tensor
    GP = nc.gpsimd
    SY = nc.sync

    def vo(method, *a, **kw):
        V.wait_ge(sDV.h, sDV.v)
        return sDV.inc(method(*a, **kw))

    def so(method, *a, **kw):
        SC.wait_ge(sSC.h, sSC.v)
        return sSC.inc(method(*a, **kw))

    def po(method, *a, **kw):
        GP.wait_ge(sPL.h, sPL.v)
        return sPL.inc(method(*a, **kw))

    def sy_dma(sem, out, in_, **kw):
        SY.wait_ge(sem.h, sem.v)
        SY.dma_start(out=out, in_=in_, **kw).then_inc(sem.h, 16)
        sem.v += 16
        return sem.v

    def gp_dma(sem, out, in_, **kw):
        GP.wait_ge(sem.h, sem.v)
        GP.dma_start(out=out, in_=in_, **kw).then_inc(sem.h, 16)
        sem.v += 16
        return sem.v

    sps_last_exp = [0, 0, 0]
    sps_last_dve = [0, 0, 0]

    for dst, srcc in [(on1b, ones_1x128b), (on1f, ones_1x128f), (onc, ones_128x1b),
                      (id128, id128_c), (id128b, id128b_c), (bqc, bq_d), (bkc, bk_d)]:
        sy_dma(sCN, dst[:], srcc)
    v_consts = sCN.v
    gp_dma(sGA, lusb[:], lu_d)
    gp_dma(sGA, srow[0:1, 0:8], iota8_c)
    gp_dma(sGA, bpr_b[0:1, :], bp_d)
    gp_dma(sGA, padbs[:], pb_d)
    v_small = sGA.v
    po(GP.iota, iotac[:], pattern=[[0, 1]], base=0, channel_multiplier=1)
    v_iota = po(GP.tensor_copy, out=iotacf[:], in_=iotac[:])

    vo(V.memset, cst[:, 0:1], EPS)
    v_cst = vo(V.memset, cst[:, 1:2], -LOG8)

    SC.wait_ge(sGA.h, v_small)
    so(SC.activation, out=gsc[0:1, 0:8], in_=lusb[0:1, 8:16], func=AF.Ln)
    v_g = so(SC.activation, out=gsc[0:1, 8:16], in_=gsc[0:1, 0:8], func=AF.Ln, scale=-1.0)
    V.wait_ge(sSC.h, v_g)
    vo(V.tensor_tensor, out=gsc[0:1, 0:8], in0=lusb[0:1, 0:8], in1=gsc[0:1, 8:16], op=ALU.subtract)
    vo(V.reduce_max, out=srow[0:1, 26:27], in_=gsc[0:1, 0:8], axis=AX.X)
    vo(V.tensor_scalar, out=gsc[0:1, 8:16], in0=gsc[0:1, 0:8], scalar1=srow[0:1, 26:27],
       scalar2=None, op0=ALU.is_ge)
    vo(V.tensor_tensor, out=gsc[0:1, 8:16], in0=gsc[0:1, 8:16], in1=srow[0:1, 0:8], op=ALU.mult)
    vo(V.reduce_max, out=srow[0:1, 27:28], in_=gsc[0:1, 8:16], axis=AX.X)
    v_idxi = vo(V.tensor_copy, out=idxi[:], in_=srow[0:1, 27:28])

    vo(V.tensor_scalar, out=srow[0:1, 28:29], in0=srow[0:1, 27:28], scalar1=1024.0,
       scalar2=None, op0=ALU.mult)
    v_idxb = vo(V.tensor_copy, out=sbf[0:1, 0:1], in_=srow[0:1, 28:29])
    PE.wait_ge(sDV.h, v_idxb)
    PE.wait_ge(sCN.h, v_consts)
    v_pe_idx = sPE.inc(PE.matmul(spp[:, 0:1], on1b[:], sbf[0:1, 0:1], start=True, stop=True))
    V.wait_ge(sPE.h, v_pe_idx)
    vo(V.tensor_copy, out=gcolf[:], in_=spp[:, 0:1])
    V.wait_ge(sPL.h, v_iota)
    vo(V.tensor_tensor, out=tmpc[:], in0=iotacf[:], in1=gcolf[:], op=ALU.add)
    for c in range(8):
        vo(V.tensor_scalar, out=gcol8f[:, c:c + 1], in0=tmpc[:], scalar1=float(128 * c),
           scalar2=None, op0=ALU.add)
    v_gcol = vo(V.tensor_copy, out=gcol[:], in_=gcol8f[:])

    SC.wait_ge(sDV.h, v_cst)

    def load_wbf(wd, wbf, stage=None):
        GP.wait_ge(sPE.h, sPE.v)
        for c in range(8):
            gp_dma(sWL, wbf[:, c, :], wd[128 * c:128 * c + 128, :])
        return sWL.v

    ps_ctr = [0]
    ps_evac = [sps_last_dve[0], sps_last_dve[0], 0, 0, sps_last_dve[2], sps_last_dve[2]]

    def ps_view(i):
        return sps[i // 2][:, 512 * (i % 2):512 * (i % 2) + 512]

    def proj_mm(lhsT_fn, rhs_fn, evac_fn):
        i = ps_ctr[0] % 6
        ps = ps_view(i)
        ps_ctr[0] += 1
        if ps_evac[i]:
            PE.wait_ge(sDV.h, ps_evac[i])
        last = None
        for c in range(8):
            last = PE.matmul(ps, lhsT_fn(c), rhs_fn(c), start=(c == 0), stop=(c == 7))
        v_pe = sPE.inc(last)
        V.wait_ge(sPE.h, v_pe)
        evac_fn(ps)
        ps_evac[i] = sDV.v

    def emit_ln2t(xbuf, tokdst, base):
        ls = lnst[:].rearrange("p (t w) -> p t w", w=16)
        for t in range(2):
            vo(V.bn_stats, out=ls[:, t, 0:6], in_=xbuf[:, t, 0:512])
            vo(V.bn_stats, out=ls[:, t, 6:12], in_=xbuf[:, t, 512:1024])
            vo(V.bn_aggr, out=ls[:, t, 12:14], in_=ls[:, t, 0:12])
            vo(V.tensor_scalar, out=ls[:, t, 14:15], in0=ls[:, t, 12:13],
               scalar1=-1.0, scalar2=None, op0=ALU.mult)
        v_stats = sDV.v
        SC.wait_ge(sDV.h, v_stats)
        so(SC.activation, out=ls[:, :, 15:16], in_=ls[:, :, 13:14], func=AF.Ln,
           bias=cst[:, 0:1], scale=1.0)
        v_sq = so(SC.activation, out=ls[:, :, 15:16], in_=ls[:, :, 15:16], func=AF.Exp, scale=-0.5)
        V.wait_ge(sSC.h, v_sq)
        for t in range(2):
            vo(V.tensor_scalar, out=tokdst(base + t), in0=xbuf[:, t, :],
               scalar1=ls[:, t, 14:15], scalar2=ls[:, t, 15:16],
               op0=ALU.add, op1=ALU.mult)
        return sDV.v, v_sq

    def emit_path(xd, ntok, sg_fns, prefix, xbufB=None):
        nsg = ntok // 512
        est_p = contextlib.ExitStack()
        xbuf = est_p.enter_context(nc.sbuf_tensor(prefix + "_xb", [128, 2, C], F32))
        tok4 = est_p.enter_context(nc.sbuf_tensor(prefix + "_tok", [128, 8, 4, 128], BF16))
        fm2 = est_p.enter_context(nc.sbuf_tensor(prefix + "_fm", [128, 2, 8, 512], BF16))
        nh = 2 * nsg
        grps = [xbuf, xbufB if xbufB is not None else xbuf]
        depth = 2 if xbufB is not None else 1
        v_loads = [0] * (nh + 2)
        for g in range(min(depth, nh)):
            for tt in range(2):
                row0 = 256 * g + 128 * tt
                sy_dma(sIN, grps[g % 2][:, tt, :], xd[row0:row0 + 128, :])
            v_loads[g] = sIN.v
        vtr_prev = 0
        v_pe_fm = 0

        def emit_ln_sg(sg):
            for h in range(2):
                g = 2 * sg + h
                V.wait_ge(sIN.h, v_loads[g])
                if vtr_prev and h == 0:
                    V.wait_ge(sTR.h, vtr_prev)
                v_ln, v_sq = emit_ln2t(grps[g % 2], lambda tt: tok4[:, :, tt, :], 2 * h)
                if g + depth < nh:
                    SY.wait_ge(sDV.h, v_ln)
                    for tt in range(2):
                        row0 = 256 * (g + depth) + 128 * tt
                        sy_dma(sIN, grps[(g + depth) % 2][:, tt, :], xd[row0:row0 + 128, :])
                    v_loads[g + depth] = sIN.v
            return sDV.v

        v_ln_done = emit_ln_sg(0)
        v_pe_fm2 = [0, 0]
        for sg in range(nsg):
            fm = fm2[:, sg % 2, :, :]
            SY.wait_ge(sDV.h, v_ln_done)
            if v_pe_fm2[sg % 2]:
                SY.wait_ge(sPE.h, v_pe_fm2[sg % 2])
            sy_dma(sTR, fm.rearrange("p c f -> p (c f)").rearrange("p (m g) -> p m g", g=128),
                   tok4[:].rearrange("p a b f -> p (a b f)"), transpose=True)
            vtr = sTR.v
            vtr_prev = vtr
            if sg + 1 < nsg:
                v_ln_done = emit_ln_sg(sg + 1)
            PE.wait_ge(sTR.h, vtr)
            for fn in sg_fns:
                fn(sg, fm)
            v_pe_fm2[sg % 2] = sPE.v
        est_p.close()

    est_w = contextlib.ExitStack()
    wkb = est_w.enter_context(nc.sbuf_tensor("wkb", [128, 8, C], BF16))
    bvb = est_w.enter_context(nc.sbuf_tensor("bvb", [128, C], BF16))
    gp_dma(sGA, bvb[0:1, :], bv_d)
    PE.wait_ge(sGA.h, sGA.v)
    PE.wait_ge(sDV.h, v_gcol)
    for half in range(2):
        v_pe_b = sPE.inc(PE.matmul(spp[:, 0:512], on1b[:], bvb[0:1, 512 * half:512 * half + 512],
                                   start=True, stop=True))
        V.wait_ge(sPE.h, v_pe_b)
        vo(V.tensor_copy, out=bvb[:, 512 * half:512 * half + 512], in_=spp[:, 0:512])
        PE.wait_ge(sDV.h, sDV.v)
    sps_last_dve[2] = sDV.v
    v_wk = load_wbf(wk_d, wkb)
    v_wv = load_wbf(wv_d, wvb)

    wgr = Qsb[:, 4:8, :].rearrange("p a (b f) -> p (a b) f", b=2)
    gsems = [Sem(nc, f"sG{c}") for c in range(8)]
    GP.wait_ge(sDV.h, v_gcol)
    for c in range(8):
        GP.indirect_dma_start(out=wgr[:, c, :], out_offset=None, in_=wf_d,
                              in_offset=IndirectOffsetOnAxis(gcol[:, c:c + 1], 0)).then_inc(gsems[c].h, 16)
        gsems[c].v += 16
    for c in range(8):
        GP.wait_ge(gsems[c].h, 16)
        gp_dma(sWS, weff_d[c], wgr[:, c, :])
    v_weff = sWS.v
    sBF = Sem(nc, "sBF")
    vo(V.tensor_scalar, out=tmpc[0:2, 0:1], in0=gcolf[0:2, 0:1],
       scalar1=1.0 / 1024.0, scalar2=None, op0=ALU.mult)
    v_bfo = vo(V.tensor_copy, out=bfo[:], in_=tmpc[0:2, 0:1])
    GP.wait_ge(sDV.h, v_bfo)
    GP.indirect_dma_start(out=beff_b[:], out_offset=None, in_=bf_d,
                          in_offset=IndirectOffsetOnAxis(bfo[:], 0)).then_inc(sBF.h, 16)
    sBF.v += 16
    v_kvw = vo(V.memset, Vsb[:].rearrange("p t (h c) -> p (t h) c", c=65)[:, :, 64:65], 1.0)

    def kproj(sg, fm):
        PE.wait_ge(sDV.h, v_kvw)
        PE.wait_ge(sWL.h, v_wk)
        for e in range(8):
            def lf(c, e=e):
                return wkb[:, c, 128 * e:128 * e + 128]

            def rf(c, fm=fm):
                return fm[:, c, :]

            def ev(ps, e=e, sg=sg):
                vo(V.tensor_scalar, out=Ksb[:, e, 512 * sg:512 * sg + 512], in0=ps,
                   scalar1=bkc[:, e:e + 1], scalar2=None, op0=ALU.add)
            proj_mm(lf, rf, ev)

    def vproj(sg, fm):
        PE.wait_ge(sWL.h, v_wv)
        for sc in range(4):
            for eh in range(2):
                def lf(c, fm=fm, sc=sc):
                    return fm[:, c, 128 * sc:128 * sc + 128]

                def rf(c, eh=eh):
                    return wvb[:, c, 512 * eh:512 * eh + 512]

                def ev(ps, sg=sg, sc=sc, eh=eh):
                    tile = 4 * sg + sc
                    dst = Vsb[:, tile, :].rearrange("p (h c) -> p h c", c=65)[:, 8 * eh:8 * eh + 8, 0:64]
                    vo(V.tensor_tensor, out=dst, in0=ps.rearrange("p (h c) -> p h c", c=64),
                       in1=bvb[:, 512 * eh:512 * eh + 512].rearrange("p (h c) -> p h c", c=64),
                       op=ALU.add)
                proj_mm(lf, rf, ev)

    xbufB_kv = Qsb[:, 0:2, :].bitcast(F32)
    emit_path(xkv_d, KT, [kproj, vproj], "kv", xbufB=xbufB_kv)

    V.wait_ge(sWS.h, v_weff)
    v_wq = load_wbf(wq_d, wkb)

    def qproj(sg, fm):
        PE.wait_ge(sWL.h, v_wq)
        for e in range(8):
            def lf(c, e=e):
                return wkb[:, c, 128 * e:128 * e + 128]

            def rf(c, fm=fm):
                return fm[:, c, :]

            def ev(ps, e=e, sg=sg):
                vo(V.tensor_scalar, out=Qsb[:, e, 512 * sg:512 * sg + 512], in0=ps,
                   scalar1=bqc[:, e:e + 1], scalar2=None, op0=ALU.add)
            proj_mm(lf, rf, ev)

    SY.wait_ge(sPE.h, sPE.v)
    xbufB_q = wvb[:, 0:4, :].rearrange("p (t a) f -> p t (a f)", t=2).bitcast(F32)
    emit_path(xq_d, TQ, [qproj], "q", xbufB=xbufB_q)

    GP.wait_ge(sDV.h, sDV.v)
    v_wp_wl = load_wbf(wp_d, wvb)
    wpb = wvb
    est_w.close()
    scr = sb("scr", [128, 4096], BF16)
    V.wait_ge(sPE.h, sPE.v)

    v_spp_read = sDV.v
    for hp in range(8):
        for src in [Qsb, Ksb]:
            n = TQ if src is Qsb else KT
            nwin = n // 128
            v_sq = vo(V.tensor_tensor, out=scr[:, 0:n], in0=src[:, hp, 0:n],
                      in1=src[:, hp, 0:n], op=ALU.mult)
            PE.wait_ge(sDV.h, max(v_sq, v_spp_read))
            last = None
            for w in range(nwin):
                last = PE.matmul(spp[:, w:w + 1], scr[:, 128 * w:128 * w + 128], onc[:],
                                 start=True, stop=True)
            v_pe1 = sPE.inc(last)
            V.wait_ge(sPE.h, v_pe1)
            v_rm = vo(V.reduce_max, out=tmpc[:], in_=spp[:, 0:nwin], axis=AX.X)
            PE.wait_ge(sDV.h, v_rm)
            v_pe2 = sPE.inc(PE.transpose(spp[0:1, 0:128], tmpc[:], id128[:]))
            V.wait_ge(sPE.h, v_pe2)
            cidx = hp if src is Qsb else 8 + hp
            v_spp_read = vo(V.reduce_max, out=crow[0:1, cidx:cidx + 1],
                            in_=spp[0:1, 0:128], axis=AX.X)
    vo(V.tensor_tensor, out=crow[0:1, 0:8], in0=crow[0:1, 0:8], in1=crow[0:1, 8:16], op=ALU.mult)
    v_c1 = vo(V.tensor_scalar, out=crow[0:1, 0:8], in0=crow[0:1, 0:8], scalar1=1e-12,
              scalar2=None, op0=ALU.max)
    SC.wait_ge(sDV.h, v_c1)
    so(SC.activation, out=crow[0:1, 0:8], in_=crow[0:1, 0:8], func=AF.Ln)
    v_c2 = so(SC.activation, out=crow[0:1, 0:8], in_=crow[0:1, 0:8], func=AF.Exp,
              scale=0.5, bias=cst[0:1, 1:2])
    V.wait_ge(sSC.h, v_c2)
    vo(V.tensor_scalar, out=crow[0:1, 0:8], in0=crow[0:1, 0:8], scalar1=0.5, scalar2=-1.0,
       op0=ALU.add, op1=ALU.mult)
    v_c3 = vo(V.tensor_copy, out=sbf[0:1, 0:8], in_=crow[0:1, 0:8])
    PE.wait_ge(sDV.h, v_c3)
    v_pe3 = sPE.inc(PE.matmul(spp[:, 0:8], on1b[:], sbf[0:1, 0:8], start=True, stop=True))
    V.wait_ge(sPE.h, v_pe3)
    v_negc = vo(V.tensor_copy, out=negc[:], in_=spp[:, 0:8])
    sps_last_dve[2] = v_negc
    V.wait_ge(sGA.h, v_small)
    for w in range(12):
        vo(V.tensor_scalar, out=negcP[:, w, :], in0=negc[:],
           scalar1=padbs[:, w:w + 1], scalar2=None, op0=ALU.add)
    v_negcP = sDV.v

    v_projs_done = sDV.v
    if "dbg_negc" in dbg:
        SY.wait_ge(sDV.h, v_negc)
        SY.dma_start(out=dbg["dbg_negc"], in_=negc[:]).then_inc(sOUT.h, 16); sOUT.v += 16
    if "dbg_q" in dbg:
        SY.wait_ge(sDV.h, v_projs_done)
        SY.dma_start(out=dbg["dbg_q"], in_=Qsb[:]).then_inc(sOUT.h, 16); sOUT.v += 16
    if "dbg_k" in dbg:
        SY.wait_ge(sDV.h, v_projs_done)
        SY.dma_start(out=dbg["dbg_k"], in_=Ksb[:]).then_inc(sOUT.h, 16); sOUT.v += 16
    if "dbg_v" in dbg:
        SY.wait_ge(sDV.h, v_projs_done)
        SY.dma_start(out=dbg["dbg_v"], in_=Vsb[:]).then_inc(sOUT.h, 16); sOUT.v += 16
    if "dbg_weff" in dbg:
        SY.wait_ge(sWS.h, v_weff)
        SY.dma_start(out=dbg["dbg_weff"], in_=weff_d).then_inc(sOUT.h, 16); sOUT.v += 16
    v_dbg_gate = [sOUT.v]

    nc.all_engine_barrier()
    mksb = sb("mksb", [128, 6, 512], BF16)
    Ynsb = sb("Ynsb", [128, 8, 512], BF16)
    zinvr = sb("zinvr", [1, 1024], BF16)
    wffb = sb("wffb", [128, 8, C], BF16)
    Yevb = sb("Yevb", [65, C], F32)
    tbuf = sb("tbuf", [128, C], F32)
    x1buf = sb("x1buf", [128, C], F32)
    ln3 = sb("ln3", [128, C], BF16)
    ln3T = sb("ln3t", [128, C], BF16)

    SY.wait_ge(sWS.h, v_weff)
    sy_dma(sIN, wffb[:], weff_d.rearrange("c p f -> p c f"))
    v_wffb = sIN.v

    Pv = scr[:].rearrange("p (r f) -> p r f", r=4)

    def yn_view(s):
        if s % 2 == 0:
            return Ynsb[:]
        return Qsb[:, :, 512 * (s - 1):512 * (s - 1) + 512]

    v_mask_free = 0
    v_pring = {"spe": 0}
    v_ya_free = [0]
    v_zinv_free_pe = [0]
    v_attn_start = sDV.v
    yn_ready = {}
    pend_epi = [None]
    pend_tail = [None]

    pend_ffn = [None]
    v_ffn_last = [0]

    def ln_p2(xin, xnout):
        vo(V.bn_stats, out=p2s[:, 0:6], in_=xin[:, 0:512])
        vo(V.bn_stats, out=p2s[:, 6:12], in_=xin[:, 512:1024])
        vo(V.bn_aggr, out=p2s[:, 12:14], in_=p2s[:, 0:12])
        v2 = vo(V.tensor_scalar, out=p2s[:, 14:15], in0=p2s[:, 12:13],
                scalar1=-1.0, scalar2=None, op0=ALU.mult)
        SC.wait_ge(sDV.h, v2)
        so(SC.activation, out=p2s[:, 15:16], in_=p2s[:, 13:14], func=AF.Ln,
           bias=cst[:, 0:1], scale=1.0)
        v3 = so(SC.activation, out=p2s[:, 15:16], in_=p2s[:, 15:16], func=AF.Exp, scale=-0.5)
        V.wait_ge(sSC.h, v3)
        return vo(V.tensor_scalar, out=xnout, in0=xin, scalar1=p2s[:, 14:15],
                  scalar2=p2s[:, 15:16], op0=ALU.add, op1=ALU.mult)

    v_xq_pref = [0]

    def emit_ffn_parcel(pref_row0):
        st = {}
        fillers = []
        pend = pend_ffn[0]
        pend_ffn[0] = None
        if pend is not None:
            row02, v_l3t2 = pend
            l3v = ln3T[:].rearrange("p (a f) -> p a f", a=8)

            def mmh(eh):
                def f():
                    if eh == 0:
                        PE.wait_ge(sTR.h, v_l3t2)
                        PE.wait_ge(sIN.h, v_wffb)
                        PE.wait_ge(sDV.h, sps_last_dve[2])
                        PE.wait_ge(sBF.h, 16)
                        if sps_last_exp[2]:
                            PE.wait_ge(sSC.h, sps_last_exp[2])
                    last = None
                    for cc in range(8):
                        last = PE.matmul(sps2[:, 512 * eh:512 * eh + 512], l3v[:, cc, :],
                                         wffb[:, cc, 512 * eh:512 * eh + 512],
                                         start=(cc == 0), stop=False)
                    last = PE.matmul(sps2[:, 512 * eh:512 * eh + 512], on1b[:],
                                     beff_b[0:1, 512 * eh:512 * eh + 512], start=False, stop=True)
                    st[eh] = sPE.inc(last)
                return f
            fillers = [mmh(0), mmh(1)]

        def tail():
            if pend is not None:
                v_ffn = st[1]
                v_ffn_last[0] = v_ffn
                V.wait_ge(sPE.h, v_ffn)
                v_ln4 = ln_p2(sps2[:], tbuf[:])
                sps_last_dve[2] = v_ln4
                v_outc = vo(V.tensor_tensor, out=x1buf[:], in0=x1buf[:], in1=tbuf[:], op=ALU.add)
                SY.wait_ge(sDV.h, v_outc)
                sy_dma(sOUT, out_d[row02:row02 + 128, :], x1buf[:])
            if pref_row0 is not None:
                sy_dma(sM2, tbuf[:], xq_d[pref_row0:pref_row0 + 128, :])
                v_xq_pref[0] = sM2.v
        return fillers, tail

    def emit_op_parcel(ssrc, k):
        Ysrc = yn_view(ssrc)
        row0 = 512 * ssrc + 128 * k
        st = {}

        def mmh(eh):
            def f():
                if eh == 0:
                    PE.wait_ge(sDV.h, max(yn_ready[ssrc], sps_last_dve[2]))
                    PE.wait_ge(sWL.h, v_wp_wl)
                    PE.wait_ge(sGA.h, v_small)
                    if sps_last_exp[2]:
                        PE.wait_ge(sSC.h, sps_last_exp[2])
                last = None
                for hp2 in range(8):
                    last = PE.matmul(sps2[:, 512 * eh:512 * eh + 512],
                                     Ysrc[:, hp2, 128 * k:128 * k + 128],
                                     wpb[:, hp2, 512 * eh:512 * eh + 512],
                                     start=(hp2 == 0), stop=False)
                last = PE.matmul(sps2[:, 512 * eh:512 * eh + 512], on1b[:],
                                 bpr_b[0:1, 512 * eh:512 * eh + 512], start=False, stop=True)
                st[eh] = sPE.inc(last)
            return f

        def tail():
            v_op = st[1]
            v_out_prev = sOUT.v
            V.wait_ge(sPE.h, v_op)
            V.wait_ge(sOUT.h, v_out_prev)
            v_ln2 = ln_p2(sps2[:], x1buf[:])
            sps_last_dve[2] = v_ln2
            V.wait_ge(sM2.h, v_xq_pref[0])
            vo(V.tensor_tensor, out=x1buf[:], in0=x1buf[:], in1=tbuf[:], op=ALU.add)
            v_ln3 = ln_p2(x1buf[:], ln3[:])
            SY.wait_ge(sDV.h, v_ln3)
            SY.wait_ge(sPE.h, v_ffn_last[0])
            sy_dma(sTR, ln3T[:].rearrange("p (a f) -> p a f", a=8), ln3[:], transpose=True)
            pend_ffn[0] = (row0, sTR.v)
        return [mmh(0), mmh(1)], tail

    for slot in range(NSLOT):
        L = LCH[slot]
        v_pool_slot = sPL.v
        if v_mask_free:
            GP.wait_ge(sPE.h, v_mask_free)
        gp_dma(sGA, mksb[:], mk_d[6 * slot:6 * slot + 6].rearrange("w p f -> p w f"))
        v_mk = sGA.v

        for hp in range(8):
            fillers, parcel_tail = [], None
            if slot >= 1:
                if hp % 2 == 0:
                    fillers, parcel_tail = emit_ffn_parcel(512 * (slot - 1) + 128 * (hp // 2))
                else:
                    fillers, parcel_tail = emit_op_parcel(slot - 1, hp // 2)

            v_exp = [0] * L
            v_pvc = [0] * L

            def emit_qk(c):
                ti = c % 2
                ps = sps[ti]
                msk = c >= L - 6
                PE.wait_ge(sDV.h, max(v_negcP, v_attn_start, sps_last_dve[ti]))
                if sps_last_exp[ti]:
                    PE.wait_ge(sSC.h, sps_last_exp[ti])
                PE.matmul(ps[:, 0:512], Ksb[0:64, hp, 128 * c:128 * c + 128],
                          Qsb[0:64, hp, 512 * slot:512 * slot + 512],
                          start=True, stop=not msk, tile_position=(0, 0))
                last = PE.matmul(ps[:, 512:1024], Ksb[64:128, hp, 128 * c:128 * c + 128],
                                 Qsb[64:128, hp, 512 * slot:512 * slot + 512],
                                 start=True, stop=not msk, tile_position=(64, 0))
                if msk:
                    j = c - (L - 6)
                    PE.wait_ge(sGA.h, v_mk)
                    PE.matmul(ps[:, 0:512], id128b[:], mksb[:, j, :],
                              start=False, stop=True)
                    last = PE.matmul(ps[:, 512:1024], id128b[:], mksb[:, j, :],
                                     start=False, stop=True)
                return sPE.inc(last)

            def emit_exp(c, v_qk):
                pr = c % 4
                SC.wait_ge(sPE.h, v_qk)
                if c >= 4:
                    SC.wait_ge(sPE.h, v_pvc[c - 4])
                else:
                    if v_pring["spe"]:
                        SC.wait_ge(sPE.h, v_pring["spe"])
                if c >= L - 3:
                    bias_ap = negcP[:, 3 * slot + (c - (L - 3)), hp:hp + 1]
                else:
                    bias_ap = negc[:, hp:hp + 1]
                v_exp[c] = so(SC.activation, out=Pv[:, pr, :], in_=sps[c % 2][:], func=AF.Exp,
                              bias=bias_ap, scale=0.125)
                sps_last_exp[c % 2] = v_exp[c]

            def emit_pv(c, hp=hp, L=L, v_exp=v_exp, v_pvc=v_pvc):
                pr = c % 4
                if c == 0 and v_ya_free[0]:
                    PE.wait_ge(sDV.h, v_ya_free[0])
                PE.wait_ge(sSC.h, v_exp[c])
                PE.matmul(ya[:], Vsb[:, c, 65 * (2 * hp):65 * (2 * hp) + 65],
                          Pv[:, pr, 0:512], start=(c == 0), stop=(c == L - 1))
                v_pvc[c] = sPE.inc(PE.matmul(yb[:], Vsb[:, c, 65 * (2 * hp + 1):65 * (2 * hp + 1) + 65],
                                             Pv[:, pr, 512:1024], start=(c == 0), stop=(c == L - 1)))

            for c in range(L):
                v_qk = emit_qk(c)
                if c == 0 and pend_tail[0] is not None:
                    pend_tail[0]()
                    pend_tail[0] = None
                emit_exp(c, v_qk)
                if pend_epi[0] is not None:
                    if c == 1:
                        pend_epi[0][0]()
                    elif c == 2:
                        pend_epi[0][1]()
                    elif c == 3:
                        pend_epi[0] = None
                        emit_pv(0)
                        emit_pv(1)
                        emit_pv(2)
                elif 1 <= c <= L - 2:
                    emit_pv(c - 1)
                if c in (4, 6) and fillers:
                    fillers.pop(0)()
            for f in fillers:
                f()
            if parcel_tail is not None:
                parcel_tail()

            def mk_tail(emit_pv_e, v_pvc_e, slot_e, hp_e, L_e):
                def tail():
                    emit_pv_e(L_e - 2)
                    emit_pv_e(L_e - 1)
                    V.wait_ge(sPE.h, v_pvc_e[L_e - 1])
                    vo(V.tensor_copy, out=Yevb[0:64, 0:512], in_=ya[0:64, :])
                    v_yev = vo(V.tensor_copy, out=Yevb[0:64, 512:1024], in_=yb[0:64, :])
                    v_pring["spe"] = v_pvc_e[L_e - 1]
                    pend_epi[0] = mk_epi(slot_e, hp_e, v_yev, v_pvc_e[L_e - 1])
                return tail

            def mk_epi(slot_e, hp_e, v_yev_e, v_pv_last):
                st = {}

                def emit_zchain():
                    SC.wait_ge(sPE.h, max(v_pv_last, v_zinv_free_pe[0]))
                    so(SC.activation, out=Yevb[64:65, 0:512], in_=ya[64:65, :], func=AF.Ln)
                    so(SC.activation, out=Yevb[64:65, 512:1024], in_=yb[64:65, :], func=AF.Ln)
                    v_zi = so(SC.activation, out=zinvr[:], in_=Yevb[64:65, :], func=AF.Exp, scale=-1.0)
                    st["v_zi"] = v_zi

                def emit_ynmul():
                    ynv_e = yn_view(slot_e)
                    PE.wait_ge(sSC.h, st["v_zi"])
                    PE.wait_ge(sDV.h, v_yev_e)
                    PE.matmul(ya[0:64, :], on1b[0:1, 0:64], zinvr[0:1, 0:512],
                              start=True, stop=True)
                    v_zbB = sPE.inc(PE.matmul(yb[0:64, :], on1b[0:1, 0:64],
                                              zinvr[0:1, 512:1024], start=True, stop=True))
                    v_zinv_free_pe[0] = v_zbB
                    V.wait_ge(sPE.h, v_zbB)
                    if dbg and v_dbg_gate[0]:
                        V.wait_ge(sOUT.h, v_dbg_gate[0])
                    vo(V.tensor_tensor, out=ynv_e[0:64, hp_e, :], in0=Yevb[0:64, 0:512],
                       in1=ya[0:64, :], op=ALU.mult)
                    v_done = vo(V.tensor_tensor, out=ynv_e[64:128, hp_e, :],
                                in0=Yevb[0:64, 512:1024], in1=yb[0:64, :], op=ALU.mult)
                    v_ya_free[0] = v_done
                    if hp_e == 7:
                        yn_ready[slot_e] = v_done
                        if "dbg_yn" in dbg:
                            SY.wait_ge(sDV.h, v_done)
                            SY.dma_start(out=dbg["dbg_yn"][:, :, 512 * slot_e:512 * slot_e + 512],
                                         in_=ynv_e).then_inc(sOUT.h, 16)
                            sOUT.v += 16
                            v_dbg_gate[0] = sOUT.v

                return emit_zchain, emit_ynmul

            pend_tail[0] = mk_tail(emit_pv, v_pvc, slot, hp, L)

        v_mask_free = sPE.v

    pend_tail[0]()
    pend_tail[0] = None
    pend_epi[0][0]()
    pend_epi[0][1]()
    pend_epi[0] = None

    for k in range(4):
        fl, tl = emit_ffn_parcel(512 * 3 + 128 * k)
        for f in fl:
            f()
        tl()
        fl, tl = emit_op_parcel(3, k)
        for f in fl:
            f()
        tl()
    fl, tl = emit_ffn_parcel(None)
    for f in fl:
        f()
    tl()

    SY.wait_ge(sOUT.h, sOUT.v)
    est.close()
    return nc


_NC_CACHE = {}


def _get_nc():
    if "nc" not in _NC_CACHE:
        _NC_CACHE["nc"] = build_nc()
    return _NC_CACHE["nc"]


def kernel(L0_attn_x, L0_attn_Wq, L0_attn_bq, L0_attn_Wk, L0_attn_bk,
           L0_attn_Wv, L0_attn_bv, L0_attn_Wp, L0_attn_bp,
           L0_ffn_W, L0_ffn_b, L0_ffn_logits, L0_ffn_u):
    x = np.ascontiguousarray(np.asarray(L0_attn_x, dtype=np.float32))
    wq = np.ascontiguousarray(np.asarray(L0_attn_Wq, dtype=np.float32))
    wk = np.ascontiguousarray(np.asarray(L0_attn_Wk, dtype=np.float32))
    wv = np.ascontiguousarray(np.asarray(L0_attn_Wv, dtype=np.float32))
    wp = np.ascontiguousarray(np.asarray(L0_attn_Wp, dtype=np.float32))
    bq = np.asarray(L0_attn_bq, dtype=np.float32)
    bk = np.asarray(L0_attn_bk, dtype=np.float32)
    bv = np.asarray(L0_attn_bv, dtype=np.float32)
    bp = np.asarray(L0_attn_bp, dtype=np.float32)
    wf = np.ascontiguousarray(np.asarray(L0_ffn_W, dtype=np.float32).reshape(8 * C, C))
    bf = np.ascontiguousarray(np.asarray(L0_ffn_b, dtype=np.float32))
    lg = np.asarray(L0_ffn_logits, dtype=np.float32)
    u = np.asarray(L0_ffn_u, dtype=np.float32)

    bq_c = np.ascontiguousarray(bq.reshape(8, 128).T)
    bk_c = np.ascontiguousarray(bk.reshape(8, 128).T)
    bv_r = np.ascontiguousarray(bv.reshape(1, C))
    bp_r = np.ascontiguousarray(bp.reshape(1, C))
    lu = np.ascontiguousarray(np.concatenate([lg, u]).reshape(1, 16))
    masks = [host_masks(0), host_masks(1)]
    padbs_h = [host_padb(0), host_padb(1)]

    in_maps = []
    for core in range(8):
        b, r = core // 2, core % 2
        xb = x[b]
        rows = np.concatenate([np.arange(512 * j, 512 * j + 512) for j in QB[r]])
        xq = np.ascontiguousarray(xb[rows])
        xkv = np.ascontiguousarray(xb.reshape(T // 4, 4, C)[:, 0:3, :].reshape(KT, C))
        in_maps.append({
            "xq": xq, "xkv": xkv, "wq": wq, "wk": wk, "wv": wv, "wp": wp,
            "bq": bq_c, "bk": bk_c, "bv": bv_r, "bp": bp_r,
            "wf": wf, "bf": bf, "lu": lu, "mk": masks[r], "padb": padbs_h[r],
        })

    nc = _get_nc()
    res = run_bass_kernel_spmd(nc, in_maps, core_ids=list(range(8)))

    out = np.empty((B, T, C), dtype=np.float32)
    for core in range(8):
        b, r = core // 2, core % 2
        o = res.results[core]["out"]
        for i, j in enumerate(QB[r]):
            out[b, 512 * j:512 * j + 512] = o[512 * i:512 * i + 512]
    return out


# version 23
# speedup vs baseline: 1.0746x; 1.0254x over previous
import contextlib
import numpy as np
import ml_dtypes
import concourse.bass as bass
import concourse.mybir as mybir
from concourse.bass_utils import run_bass_kernel_spmd
from concourse.bass_core_rust import IndirectOffsetOnAxis

F32 = mybir.dt.float32
BF16 = mybir.dt.bfloat16
I32 = mybir.dt.int32
AF = mybir.ActivationFunctionType
ALU = mybir.AluOpType
AX = mybir.AxisListType

B, T, C, H, D = 4, 4096, 1024, 16, 64
KT = 3072
NCH = KT // 128
QB = [[0, 3, 4, 7], [1, 2, 5, 6]]
LCH = [6, 12, 18, 24]
NSLOT = 4
TQ = 2048
EPS = 1e-5
LOG8 = float(np.log(8.0))

DEBUG_OUTS = []


def _orig(j):
    return 4 * (j // 3) + j % 3


def host_masks(r):
    m = np.zeros((24, 128, 512), dtype=np.float32)
    for i in range(NSLOT):
        L = LCH[i]
        t = 3 * QB[r][i] + 3
        for j in range(6):
            c = L - 6 + j
            w = 6 * i + j
            if c < t and c >= t - 3:
                k = c - (t - 3)
                s = np.arange(128)
                og = _orig(128 * k + s)
                m[w] = np.where(og[:, None] <= np.arange(512)[None, :], 0.0, -60000.0)
    return np.ascontiguousarray(m)


def host_padb(r):
    p = np.zeros((128, 12), dtype=np.float32)
    for i in range(NSLOT):
        L = LCH[i]
        t = 3 * QB[r][i] + 3
        for k in range(3):
            if L - 3 + k >= t:
                p[:, 3 * i + k] = -1e30
    return np.ascontiguousarray(p)


class Sem:
    def __init__(self, nc, name):
        self.h = nc.alloc_semaphore(name=name)
        self.v = 0

    def inc(self, inst, k=1):
        inst.then_inc(self.h, k)
        self.v += k
        return self.v


def build_nc():
    nc = bass.Bass("TRN2", target_bir_lowering=False)

    xq_d = nc.dram_tensor("xq", [TQ, C], F32, kind="ExternalInput").ap()
    xkv_d = nc.dram_tensor("xkv", [KT, C], F32, kind="ExternalInput").ap()
    wq_d = nc.dram_tensor("wq", [C, C], F32, kind="ExternalInput").ap()
    wk_d = nc.dram_tensor("wk", [C, C], F32, kind="ExternalInput").ap()
    wv_d = nc.dram_tensor("wv", [C, C], F32, kind="ExternalInput").ap()
    wp_d = nc.dram_tensor("wp", [C, C], F32, kind="ExternalInput").ap()
    bq_d = nc.dram_tensor("bq", [128, 8], F32, kind="ExternalInput").ap()
    bk_d = nc.dram_tensor("bk", [128, 8], F32, kind="ExternalInput").ap()
    bv_d = nc.dram_tensor("bv", [1, C], F32, kind="ExternalInput").ap()
    bp_d = nc.dram_tensor("bp", [1, C], F32, kind="ExternalInput").ap()
    wf_d = nc.dram_tensor("wf", [8 * C, C], F32, kind="ExternalInput").ap()
    bf_d = nc.dram_tensor("bf", [8, C], F32, kind="ExternalInput").ap()
    lu_d = nc.dram_tensor("lu", [1, 16], F32, kind="ExternalInput").ap()
    mk_d = nc.dram_tensor("mk", [24, 128, 512], F32, kind="ExternalInput").ap()
    pb_d = nc.dram_tensor("padb", [128, 12], F32, kind="ExternalInput").ap()
    out_d = nc.dram_tensor("out", [TQ, C], F32, kind="ExternalOutput").ap()
    weff_d = nc.dram_tensor("weff_scr", [8, 128, C], BF16, kind="Internal").ap()

    dbg = {}
    for name, shape, dt in [
        ("dbg_q", [128, 8, TQ], BF16), ("dbg_k", [128, 8, KT], BF16),
        ("dbg_v", [128, NCH, 1040], BF16), ("dbg_yn", [128, 8, 512 * NSLOT], BF16),
        ("dbg_negc", [128, 8], F32), ("dbg_weff", [8, 128, C], BF16),
    ]:
        if name in DEBUG_OUTS:
            dbg[name] = nc.dram_tensor(name, shape, dt, kind="ExternalOutput").ap()

    ones_1x128b = nc.inline_tensor(np.ones((1, 128), dtype=ml_dtypes.bfloat16), name="c_on1b").ap()
    ones_1x128f = nc.inline_tensor(np.ones((1, 128), dtype=np.float32), name="c_on1f").ap()
    ones_128x1b = nc.inline_tensor(np.ones((128, 1), dtype=ml_dtypes.bfloat16), name="c_onc").ap()
    iota8_c = nc.inline_tensor(np.arange(8, dtype=np.float32).reshape(1, 8), name="c_iota8").ap()
    id128_c = nc.inline_tensor(np.eye(128, dtype=np.float32), name="c_id128").ap()
    id128b_c = nc.inline_tensor(np.eye(128, dtype=ml_dtypes.bfloat16), name="c_id128b").ap()

    est = contextlib.ExitStack()

    def sb(name, shape, dt):
        return est.enter_context(nc.sbuf_tensor(name, shape, dt))

    Ksb = sb("Ksb", [128, 8, KT], BF16)
    Vsb = sb("Vsb", [128, NCH, 1040], BF16)
    Qsb = sb("Qsb", [128, 8, TQ], BF16)
    wvb = sb("wvb", [128, 8, C], BF16)
    bqc = sb("bqc", [128, 8], F32)
    bkc = sb("bkc", [128, 8], F32)
    bpr_b = sb("bprb", [1, C], BF16)
    beff_b = sb("beffb", [2, C], BF16)
    bfo = sb("bfo", [2, 1], I32)
    negc = sb("negc", [128, 8], F32)
    negcP = sb("negcP", [128, 12, 8], F32)
    padbs = sb("padbs", [128, 12], F32)
    on1b = sb("on1b", [1, 128], BF16)
    on1f = sb("on1f", [1, 128], F32)
    onc = sb("onc", [128, 1], BF16)
    id128 = sb("id128", [128, 128], F32)
    id128b = sb("idb", [128, 128], BF16)
    srow = sb("srow", [1, 64], F32)
    sbf = sb("sbf", [1, 8], BF16)
    crow = sb("crow", [1, 16], F32)
    lusb = sb("lusb", [1, 16], F32)
    gsc = sb("gsc", [1, 16], F32)
    idxi = sb("idxi", [1, 1], I32)
    gcol = sb("gcol", [128, 8], I32)
    gcol8f = sb("gcol8f", [128, 8], F32)
    iotac = sb("iotac", [128, 1], I32)
    iotacf = sb("iotacf", [128, 1], F32)
    gcolf = sb("gcolf", [128, 1], F32)
    tmpc = sb("tmpc", [128, 1], F32)
    lnst = sb("lnst", [128, 32], F32)
    cst = sb("cst", [128, 2], F32)
    p2s = sb("p2s", [128, 16], F32)

    sps0 = est.enter_context(nc.psum_tensor("sps0", [128, 1024], F32))
    sps1 = est.enter_context(nc.psum_tensor("sps1", [128, 1024], F32))
    sps2 = est.enter_context(nc.psum_tensor("sps2", [128, 1024], F32))
    ya = est.enter_context(nc.psum_tensor("ya", [65, 512], F32))
    yb = est.enter_context(nc.psum_tensor("yb", [65, 512], F32))
    sps = [sps0, sps1, sps2]
    spp = sps2[:, 0:512]

    sIN = Sem(nc, "sIN")
    sCN = Sem(nc, "sCN")
    sWL = Sem(nc, "sWL")
    sTR = Sem(nc, "sTR")
    sGA = Sem(nc, "sGA")
    sDV = Sem(nc, "sDV")
    sSC = Sem(nc, "sSC")
    sPE = Sem(nc, "sPE")
    sPL = Sem(nc, "sPL")
    sWS = Sem(nc, "sWS")
    sOUT = Sem(nc, "sOUT")
    sM2 = Sem(nc, "sM2")

    V = nc.vector
    SC = nc.scalar
    PE = nc.tensor
    GP = nc.gpsimd
    SY = nc.sync

    def vo(method, *a, **kw):
        V.wait_ge(sDV.h, sDV.v)
        return sDV.inc(method(*a, **kw))

    def so(method, *a, **kw):
        SC.wait_ge(sSC.h, sSC.v)
        return sSC.inc(method(*a, **kw))

    def po(method, *a, **kw):
        GP.wait_ge(sPL.h, sPL.v)
        return sPL.inc(method(*a, **kw))

    def sy_dma(sem, out, in_, **kw):
        SY.wait_ge(sem.h, sem.v)
        SY.dma_start(out=out, in_=in_, **kw).then_inc(sem.h, 16)
        sem.v += 16
        return sem.v

    def gp_dma(sem, out, in_, **kw):
        GP.wait_ge(sem.h, sem.v)
        GP.dma_start(out=out, in_=in_, **kw).then_inc(sem.h, 16)
        sem.v += 16
        return sem.v

    sps_last_exp = [0, 0, 0]
    sps_last_dve = [0, 0, 0]

    for dst, srcc in [(on1b, ones_1x128b), (on1f, ones_1x128f), (onc, ones_128x1b),
                      (id128, id128_c), (id128b, id128b_c), (bqc, bq_d), (bkc, bk_d)]:
        sy_dma(sCN, dst[:], srcc)
    v_consts = sCN.v
    gp_dma(sGA, lusb[:], lu_d)
    gp_dma(sGA, srow[0:1, 0:8], iota8_c)
    gp_dma(sGA, bpr_b[0:1, :], bp_d)
    gp_dma(sGA, padbs[:], pb_d)
    v_small = sGA.v
    po(GP.iota, iotac[:], pattern=[[0, 1]], base=0, channel_multiplier=1)
    v_iota = po(GP.tensor_copy, out=iotacf[:], in_=iotac[:])

    vo(V.memset, cst[:, 0:1], EPS)
    v_cst = vo(V.memset, cst[:, 1:2], -LOG8)

    SC.wait_ge(sGA.h, v_small)
    so(SC.activation, out=gsc[0:1, 0:8], in_=lusb[0:1, 8:16], func=AF.Ln)
    v_g = so(SC.activation, out=gsc[0:1, 8:16], in_=gsc[0:1, 0:8], func=AF.Ln, scale=-1.0)
    V.wait_ge(sSC.h, v_g)
    vo(V.tensor_tensor, out=gsc[0:1, 0:8], in0=lusb[0:1, 0:8], in1=gsc[0:1, 8:16], op=ALU.subtract)
    vo(V.reduce_max, out=srow[0:1, 26:27], in_=gsc[0:1, 0:8], axis=AX.X)
    vo(V.tensor_scalar, out=gsc[0:1, 8:16], in0=gsc[0:1, 0:8], scalar1=srow[0:1, 26:27],
       scalar2=None, op0=ALU.is_ge)
    vo(V.tensor_tensor, out=gsc[0:1, 8:16], in0=gsc[0:1, 8:16], in1=srow[0:1, 0:8], op=ALU.mult)
    vo(V.reduce_max, out=srow[0:1, 27:28], in_=gsc[0:1, 8:16], axis=AX.X)
    v_idxi = vo(V.tensor_copy, out=idxi[:], in_=srow[0:1, 27:28])

    vo(V.tensor_scalar, out=srow[0:1, 28:29], in0=srow[0:1, 27:28], scalar1=1024.0,
       scalar2=None, op0=ALU.mult)
    v_idxb = vo(V.tensor_copy, out=sbf[0:1, 0:1], in_=srow[0:1, 28:29])
    PE.wait_ge(sDV.h, v_idxb)
    PE.wait_ge(sCN.h, v_consts)
    v_pe_idx = sPE.inc(PE.matmul(spp[:, 0:1], on1b[:], sbf[0:1, 0:1], start=True, stop=True))
    V.wait_ge(sPE.h, v_pe_idx)
    vo(V.tensor_copy, out=gcolf[:], in_=spp[:, 0:1])
    V.wait_ge(sPL.h, v_iota)
    vo(V.tensor_tensor, out=tmpc[:], in0=iotacf[:], in1=gcolf[:], op=ALU.add)
    for c in range(8):
        vo(V.tensor_scalar, out=gcol8f[:, c:c + 1], in0=tmpc[:], scalar1=float(128 * c),
           scalar2=None, op0=ALU.add)
    v_gcol = vo(V.tensor_copy, out=gcol[:], in_=gcol8f[:])

    SC.wait_ge(sDV.h, v_cst)

    def load_wbf(wd, wbf, stage=None):
        GP.wait_ge(sPE.h, sPE.v)
        for c in range(8):
            gp_dma(sWL, wbf[:, c, :], wd[128 * c:128 * c + 128, :])
        return sWL.v

    ps_ctr = [0]
    ps_evac = [sps_last_dve[0], sps_last_dve[0], 0, 0, sps_last_dve[2], sps_last_dve[2]]

    def ps_view(i):
        return sps[i // 2][:, 512 * (i % 2):512 * (i % 2) + 512]

    def proj_mm(lhsT_fn, rhs_fn, evac_fn):
        i = ps_ctr[0] % 6
        ps = ps_view(i)
        ps_ctr[0] += 1
        if ps_evac[i]:
            PE.wait_ge(sDV.h, ps_evac[i])
        last = None
        for c in range(8):
            last = PE.matmul(ps, lhsT_fn(c), rhs_fn(c), start=(c == 0), stop=(c == 7))
        v_pe = sPE.inc(last)
        V.wait_ge(sPE.h, v_pe)
        evac_fn(ps)
        ps_evac[i] = sDV.v

    def emit_ln2t(xbuf, tokdst, base):
        ls = lnst[:].rearrange("p (t w) -> p t w", w=16)
        for t in range(2):
            vo(V.bn_stats, out=ls[:, t, 0:6], in_=xbuf[:, t, 0:512])
            vo(V.bn_stats, out=ls[:, t, 6:12], in_=xbuf[:, t, 512:1024])
            vo(V.bn_aggr, out=ls[:, t, 12:14], in_=ls[:, t, 0:12])
            vo(V.tensor_scalar, out=ls[:, t, 14:15], in0=ls[:, t, 12:13],
               scalar1=-1.0, scalar2=None, op0=ALU.mult)
        v_stats = sDV.v
        SC.wait_ge(sDV.h, v_stats)
        so(SC.activation, out=ls[:, :, 15:16], in_=ls[:, :, 13:14], func=AF.Ln,
           bias=cst[:, 0:1], scale=1.0)
        v_sq = so(SC.activation, out=ls[:, :, 15:16], in_=ls[:, :, 15:16], func=AF.Exp, scale=-0.5)
        V.wait_ge(sSC.h, v_sq)
        for t in range(2):
            vo(V.tensor_scalar, out=tokdst(base + t), in0=xbuf[:, t, :],
               scalar1=ls[:, t, 14:15], scalar2=ls[:, t, 15:16],
               op0=ALU.add, op1=ALU.mult)
        return sDV.v, v_sq

    def emit_path(xd, ntok, sg_fns, prefix, xbufB=None):
        nsg = ntok // 512
        est_p = contextlib.ExitStack()
        xbuf = est_p.enter_context(nc.sbuf_tensor(prefix + "_xb", [128, 2, C], F32))
        tok4 = est_p.enter_context(nc.sbuf_tensor(prefix + "_tok", [128, 8, 4, 128], BF16))
        fm2 = est_p.enter_context(nc.sbuf_tensor(prefix + "_fm", [128, 2, 8, 512], BF16))
        nh = 2 * nsg
        grps = [xbuf, xbufB if xbufB is not None else xbuf]
        depth = 2 if xbufB is not None else 1
        v_loads = [0] * (nh + 2)
        for g in range(min(depth, nh)):
            for tt in range(2):
                row0 = 256 * g + 128 * tt
                sy_dma(sIN, grps[g % 2][:, tt, :], xd[row0:row0 + 128, :])
            v_loads[g] = sIN.v
        vtr_prev = 0
        v_pe_fm = 0

        def emit_ln_sg(sg):
            for h in range(2):
                g = 2 * sg + h
                V.wait_ge(sIN.h, v_loads[g])
                if vtr_prev and h == 0:
                    V.wait_ge(sTR.h, vtr_prev)
                v_ln, v_sq = emit_ln2t(grps[g % 2], lambda tt: tok4[:, :, tt, :], 2 * h)
                if g + depth < nh:
                    SY.wait_ge(sDV.h, v_ln)
                    for tt in range(2):
                        row0 = 256 * (g + depth) + 128 * tt
                        sy_dma(sIN, grps[(g + depth) % 2][:, tt, :], xd[row0:row0 + 128, :])
                    v_loads[g + depth] = sIN.v
            return sDV.v

        v_ln_done = emit_ln_sg(0)
        v_pe_fm2 = [0, 0]
        for sg in range(nsg):
            fm = fm2[:, sg % 2, :, :]
            SY.wait_ge(sDV.h, v_ln_done)
            if v_pe_fm2[sg % 2]:
                SY.wait_ge(sPE.h, v_pe_fm2[sg % 2])
            sy_dma(sTR, fm.rearrange("p c f -> p (c f)").rearrange("p (m g) -> p m g", g=128),
                   tok4[:].rearrange("p a b f -> p (a b f)"), transpose=True)
            vtr = sTR.v
            vtr_prev = vtr
            if sg + 1 < nsg:
                v_ln_done = emit_ln_sg(sg + 1)
            PE.wait_ge(sTR.h, vtr)
            for fn in sg_fns:
                fn(sg, fm)
            v_pe_fm2[sg % 2] = sPE.v
        est_p.close()

    est_w = contextlib.ExitStack()
    wkb = est_w.enter_context(nc.sbuf_tensor("wkb", [128, 8, C], BF16))
    bvb = est_w.enter_context(nc.sbuf_tensor("bvb", [128, C], BF16))
    gp_dma(sGA, bvb[0:1, :], bv_d)
    PE.wait_ge(sGA.h, sGA.v)
    PE.wait_ge(sDV.h, v_gcol)
    for half in range(2):
        v_pe_b = sPE.inc(PE.matmul(spp[:, 0:512], on1b[:], bvb[0:1, 512 * half:512 * half + 512],
                                   start=True, stop=True))
        V.wait_ge(sPE.h, v_pe_b)
        vo(V.tensor_copy, out=bvb[:, 512 * half:512 * half + 512], in_=spp[:, 0:512])
        PE.wait_ge(sDV.h, sDV.v)
    sps_last_dve[2] = sDV.v
    v_wk = load_wbf(wk_d, wkb)
    v_wv = load_wbf(wv_d, wvb)

    wgr = Qsb[:, 4:8, :].rearrange("p a (b f) -> p (a b) f", b=2)
    gsems = [Sem(nc, f"sG{c}") for c in range(8)]
    GP.wait_ge(sDV.h, v_gcol)
    for c in range(8):
        GP.indirect_dma_start(out=wgr[:, c, :], out_offset=None, in_=wf_d,
                              in_offset=IndirectOffsetOnAxis(gcol[:, c:c + 1], 0)).then_inc(gsems[c].h, 16)
        gsems[c].v += 16
    for c in range(8):
        GP.wait_ge(gsems[c].h, 16)
        gp_dma(sWS, weff_d[c], wgr[:, c, :])
    v_weff = sWS.v
    sBF = Sem(nc, "sBF")
    vo(V.tensor_scalar, out=tmpc[0:2, 0:1], in0=gcolf[0:2, 0:1],
       scalar1=1.0 / 1024.0, scalar2=None, op0=ALU.mult)
    v_bfo = vo(V.tensor_copy, out=bfo[:], in_=tmpc[0:2, 0:1])
    GP.wait_ge(sDV.h, v_bfo)
    GP.indirect_dma_start(out=beff_b[:], out_offset=None, in_=bf_d,
                          in_offset=IndirectOffsetOnAxis(bfo[:], 0)).then_inc(sBF.h, 16)
    sBF.v += 16
    v_kvw = vo(V.memset, Vsb[:].rearrange("p t (h c) -> p (t h) c", c=65)[:, :, 64:65], 1.0)

    def kproj(sg, fm):
        PE.wait_ge(sDV.h, v_kvw)
        PE.wait_ge(sWL.h, v_wk)
        for e in range(8):
            def lf(c, e=e):
                return wkb[:, c, 128 * e:128 * e + 128]

            def rf(c, fm=fm):
                return fm[:, c, :]

            def ev(ps, e=e, sg=sg):
                vo(V.tensor_scalar, out=Ksb[:, e, 512 * sg:512 * sg + 512], in0=ps,
                   scalar1=bkc[:, e:e + 1], scalar2=None, op0=ALU.add)
            proj_mm(lf, rf, ev)

    def vproj(sg, fm):
        PE.wait_ge(sWL.h, v_wv)
        for sc in range(4):
            for eh in range(2):
                def lf(c, fm=fm, sc=sc):
                    return fm[:, c, 128 * sc:128 * sc + 128]

                def rf(c, eh=eh):
                    return wvb[:, c, 512 * eh:512 * eh + 512]

                def ev(ps, sg=sg, sc=sc, eh=eh):
                    tile = 4 * sg + sc
                    dst = Vsb[:, tile, :].rearrange("p (h c) -> p h c", c=65)[:, 8 * eh:8 * eh + 8, 0:64]
                    vo(V.tensor_tensor, out=dst, in0=ps.rearrange("p (h c) -> p h c", c=64),
                       in1=bvb[:, 512 * eh:512 * eh + 512].rearrange("p (h c) -> p h c", c=64),
                       op=ALU.add)
                proj_mm(lf, rf, ev)

    xbufB_kv = Qsb[:, 0:2, :].bitcast(F32)
    emit_path(xkv_d, KT, [kproj, vproj], "kv", xbufB=xbufB_kv)

    V.wait_ge(sWS.h, v_weff)
    v_wq = load_wbf(wq_d, wkb)

    def qproj(sg, fm):
        PE.wait_ge(sWL.h, v_wq)
        for e in range(8):
            def lf(c, e=e):
                return wkb[:, c, 128 * e:128 * e + 128]

            def rf(c, fm=fm):
                return fm[:, c, :]

            def ev(ps, e=e, sg=sg):
                vo(V.tensor_scalar, out=Qsb[:, e, 512 * sg:512 * sg + 512], in0=ps,
                   scalar1=bqc[:, e:e + 1], scalar2=None, op0=ALU.add)
            proj_mm(lf, rf, ev)

    SY.wait_ge(sPE.h, sPE.v)
    xbufB_q = wvb[:, 0:4, :].rearrange("p (t a) f -> p t (a f)", t=2).bitcast(F32)
    emit_path(xq_d, TQ, [qproj], "q", xbufB=xbufB_q)

    GP.wait_ge(sDV.h, sDV.v)
    v_wp_wl = load_wbf(wp_d, wvb)
    wpb = wvb
    est_w.close()
    scr = sb("scr", [128, 4096], BF16)
    V.wait_ge(sPE.h, sPE.v)

    v_spp_read = sDV.v
    for hp in range(8):
        for src in [Qsb, Ksb]:
            n = TQ if src is Qsb else KT
            nwin = n // 128
            v_sq = vo(V.tensor_tensor, out=scr[:, 0:n], in0=src[:, hp, 0:n],
                      in1=src[:, hp, 0:n], op=ALU.mult)
            PE.wait_ge(sDV.h, max(v_sq, v_spp_read))
            last = None
            for w in range(nwin):
                last = PE.matmul(spp[:, w:w + 1], scr[:, 128 * w:128 * w + 128], onc[:],
                                 start=True, stop=True)
            v_pe1 = sPE.inc(last)
            V.wait_ge(sPE.h, v_pe1)
            v_rm = vo(V.reduce_max, out=tmpc[:], in_=spp[:, 0:nwin], axis=AX.X)
            PE.wait_ge(sDV.h, v_rm)
            v_pe2 = sPE.inc(PE.transpose(spp[0:1, 0:128], tmpc[:], id128[:]))
            V.wait_ge(sPE.h, v_pe2)
            cidx = hp if src is Qsb else 8 + hp
            v_spp_read = vo(V.reduce_max, out=crow[0:1, cidx:cidx + 1],
                            in_=spp[0:1, 0:128], axis=AX.X)
    vo(V.tensor_tensor, out=crow[0:1, 0:8], in0=crow[0:1, 0:8], in1=crow[0:1, 8:16], op=ALU.mult)
    v_c1 = vo(V.tensor_scalar, out=crow[0:1, 0:8], in0=crow[0:1, 0:8], scalar1=1e-12,
              scalar2=None, op0=ALU.max)
    SC.wait_ge(sDV.h, v_c1)
    so(SC.activation, out=crow[0:1, 0:8], in_=crow[0:1, 0:8], func=AF.Ln)
    v_c2 = so(SC.activation, out=crow[0:1, 0:8], in_=crow[0:1, 0:8], func=AF.Exp,
              scale=0.5, bias=cst[0:1, 1:2])
    V.wait_ge(sSC.h, v_c2)
    vo(V.tensor_scalar, out=crow[0:1, 0:8], in0=crow[0:1, 0:8], scalar1=0.5, scalar2=-1.0,
       op0=ALU.add, op1=ALU.mult)
    v_c3 = vo(V.tensor_copy, out=sbf[0:1, 0:8], in_=crow[0:1, 0:8])
    PE.wait_ge(sDV.h, v_c3)
    v_pe3 = sPE.inc(PE.matmul(spp[:, 0:8], on1b[:], sbf[0:1, 0:8], start=True, stop=True))
    V.wait_ge(sPE.h, v_pe3)
    v_negc = vo(V.tensor_copy, out=negc[:], in_=spp[:, 0:8])
    sps_last_dve[2] = v_negc
    V.wait_ge(sGA.h, v_small)
    for w in range(12):
        vo(V.tensor_scalar, out=negcP[:, w, :], in0=negc[:],
           scalar1=padbs[:, w:w + 1], scalar2=None, op0=ALU.add)
    v_negcP = sDV.v

    v_projs_done = sDV.v
    if "dbg_negc" in dbg:
        SY.wait_ge(sDV.h, v_negc)
        SY.dma_start(out=dbg["dbg_negc"], in_=negc[:]).then_inc(sOUT.h, 16); sOUT.v += 16
    if "dbg_q" in dbg:
        SY.wait_ge(sDV.h, v_projs_done)
        SY.dma_start(out=dbg["dbg_q"], in_=Qsb[:]).then_inc(sOUT.h, 16); sOUT.v += 16
    if "dbg_k" in dbg:
        SY.wait_ge(sDV.h, v_projs_done)
        SY.dma_start(out=dbg["dbg_k"], in_=Ksb[:]).then_inc(sOUT.h, 16); sOUT.v += 16
    if "dbg_v" in dbg:
        SY.wait_ge(sDV.h, v_projs_done)
        SY.dma_start(out=dbg["dbg_v"], in_=Vsb[:]).then_inc(sOUT.h, 16); sOUT.v += 16
    if "dbg_weff" in dbg:
        SY.wait_ge(sWS.h, v_weff)
        SY.dma_start(out=dbg["dbg_weff"], in_=weff_d).then_inc(sOUT.h, 16); sOUT.v += 16
    v_dbg_gate = [sOUT.v]

    nc.all_engine_barrier()
    mksb = sb("mksb", [128, 6, 512], BF16)
    Ynsb = sb("Ynsb", [128, 8, 512], BF16)
    zinvr = sb("zinvr", [1, 1024], BF16)
    wffb = sb("wffb", [128, 8, C], BF16)
    Yevb = sb("Yevb", [65, C], F32)
    tbuf = sb("tbuf", [128, C], F32)
    x1buf = sb("x1buf", [128, C], F32)
    ln3 = sb("ln3", [128, C], BF16)
    ln3T = sb("ln3t", [128, C], BF16)

    SY.wait_ge(sWS.h, v_weff)
    sy_dma(sIN, wffb[:], weff_d.rearrange("c p f -> p c f"))
    v_wffb = sIN.v

    Pv = scr[:].rearrange("p (r f) -> p r f", r=4)

    def yn_view(s):
        if s % 2 == 0:
            return Ynsb[:]
        return Qsb[:, :, 512 * (s - 1):512 * (s - 1) + 512]

    v_mask_free = 0
    v_pring = {"spe": 0}
    v_ya_free = [0]
    v_zinv_free_pe = [0]
    v_attn_start = sDV.v
    yn_ready = {}
    pend_epi = [None]
    pend_tail = [None]

    pend_ffn = [None]
    pend_ptail = [None]
    v_ffn_last = [0]

    def ln_p2_stats(xin):
        vo(V.bn_stats, out=p2s[:, 0:6], in_=xin[:, 0:512])
        vo(V.bn_stats, out=p2s[:, 6:12], in_=xin[:, 512:1024])
        vo(V.bn_aggr, out=p2s[:, 12:14], in_=p2s[:, 0:12])
        return vo(V.tensor_scalar, out=p2s[:, 14:15], in0=p2s[:, 12:13],
                  scalar1=-1.0, scalar2=None, op0=ALU.mult)

    def ln_p2_fin(xin, xnout, v2):
        SC.wait_ge(sDV.h, v2)
        so(SC.activation, out=p2s[:, 15:16], in_=p2s[:, 13:14], func=AF.Ln,
           bias=cst[:, 0:1], scale=1.0)
        v3 = so(SC.activation, out=p2s[:, 15:16], in_=p2s[:, 15:16], func=AF.Exp, scale=-0.5)
        V.wait_ge(sSC.h, v3)
        return vo(V.tensor_scalar, out=xnout, in0=xin, scalar1=p2s[:, 14:15],
                  scalar2=p2s[:, 15:16], op0=ALU.add, op1=ALU.mult)

    def ln_p2(xin, xnout):
        return ln_p2_fin(xin, xnout, ln_p2_stats(xin))

    v_xq_pref = [0]

    def emit_ffn_parcel(pref_row0):
        st = {}
        l3v = ln3T[:].rearrange("p (a f) -> p a f", a=8)

        def mmh(eh):
            def f():
                if eh == 0:
                    st["pend"] = pend_ffn[0]
                    pend_ffn[0] = None
                if st["pend"] is None:
                    return
                row02, v_l3t2 = st["pend"]
                if eh == 0:
                    PE.wait_ge(sTR.h, v_l3t2)
                    PE.wait_ge(sIN.h, v_wffb)
                    PE.wait_ge(sDV.h, sps_last_dve[2])
                    PE.wait_ge(sBF.h, 16)
                    if sps_last_exp[2]:
                        PE.wait_ge(sSC.h, sps_last_exp[2])
                last = None
                for cc in range(8):
                    last = PE.matmul(sps2[:, 512 * eh:512 * eh + 512], l3v[:, cc, :],
                                     wffb[:, cc, 512 * eh:512 * eh + 512],
                                     start=(cc == 0), stop=False)
                last = PE.matmul(sps2[:, 512 * eh:512 * eh + 512], on1b[:],
                                 beff_b[0:1, 512 * eh:512 * eh + 512], start=False, stop=True)
                st[eh] = sPE.inc(last)
            return f

        def tailA():
            if st.get("pend") is None:
                return
            v_ffn = st[1]
            v_ffn_last[0] = v_ffn
            V.wait_ge(sPE.h, v_ffn)
            st["v2"] = ln_p2_stats(sps2[:])

        def tail2():
            if st.get("pend") is not None:
                row02, _ = st["pend"]
                v_ln4 = ln_p2_fin(sps2[:], tbuf[:], st["v2"])
                sps_last_dve[2] = v_ln4
                v_outc = vo(V.tensor_tensor, out=x1buf[:], in0=x1buf[:], in1=tbuf[:], op=ALU.add)
                SY.wait_ge(sDV.h, v_outc)
                sy_dma(sOUT, out_d[row02:row02 + 128, :], x1buf[:])
            if pref_row0 is not None:
                sy_dma(sM2, tbuf[:], xq_d[pref_row0:pref_row0 + 128, :])
                v_xq_pref[0] = sM2.v
        return [mmh(0), mmh(1)], tailA, tail2

    def emit_op_parcel(ssrc, k):
        Ysrc = yn_view(ssrc)
        row0 = 512 * ssrc + 128 * k
        st = {}

        def mmh(eh):
            def f():
                if eh == 0:
                    PE.wait_ge(sDV.h, max(yn_ready[ssrc], sps_last_dve[2]))
                    PE.wait_ge(sWL.h, v_wp_wl)
                    PE.wait_ge(sGA.h, v_small)
                    if sps_last_exp[2]:
                        PE.wait_ge(sSC.h, sps_last_exp[2])
                last = None
                for hp2 in range(8):
                    last = PE.matmul(sps2[:, 512 * eh:512 * eh + 512],
                                     Ysrc[:, hp2, 128 * k:128 * k + 128],
                                     wpb[:, hp2, 512 * eh:512 * eh + 512],
                                     start=(hp2 == 0), stop=False)
                last = PE.matmul(sps2[:, 512 * eh:512 * eh + 512], on1b[:],
                                 bpr_b[0:1, 512 * eh:512 * eh + 512], start=False, stop=True)
                st[eh] = sPE.inc(last)
            return f

        def tailA():
            v_op = st[1]
            v_out_prev = sOUT.v
            V.wait_ge(sPE.h, v_op)
            V.wait_ge(sOUT.h, v_out_prev)
            v_ln2 = ln_p2(sps2[:], x1buf[:])
            sps_last_dve[2] = v_ln2
            V.wait_ge(sM2.h, v_xq_pref[0])
            vo(V.tensor_tensor, out=x1buf[:], in0=x1buf[:], in1=tbuf[:], op=ALU.add)
            st["v2"] = ln_p2_stats(x1buf[:])

        def tail2():
            v_ln3 = ln_p2_fin(x1buf[:], ln3[:], st["v2"])
            SY.wait_ge(sDV.h, v_ln3)
            SY.wait_ge(sPE.h, v_ffn_last[0])
            sy_dma(sTR, ln3T[:].rearrange("p (a f) -> p a f", a=8), ln3[:], transpose=True)
            pend_ffn[0] = (row0, sTR.v)
        return [mmh(0), mmh(1)], tailA, tail2

    for slot in range(NSLOT):
        L = LCH[slot]
        v_pool_slot = sPL.v
        if v_mask_free:
            GP.wait_ge(sPE.h, v_mask_free)
        gp_dma(sGA, mksb[:], mk_d[6 * slot:6 * slot + 6].rearrange("w p f -> p w f"))
        v_mk = sGA.v

        for hp in range(8):
            fillers, parcel_tailA, parcel_tail2 = [], None, None
            if slot >= 1:
                if hp % 2 == 0:
                    fillers, parcel_tailA, parcel_tail2 = emit_ffn_parcel(
                        512 * (slot - 1) + 128 * (hp // 2))
                else:
                    fillers, parcel_tailA, parcel_tail2 = emit_op_parcel(slot - 1, hp // 2)

            v_exp = [0] * L
            v_pvc = [0] * L

            def emit_qk(c):
                ti = c % 2
                ps = sps[ti]
                msk = c >= L - 6
                PE.wait_ge(sDV.h, max(v_negcP, v_attn_start, sps_last_dve[ti]))
                if sps_last_exp[ti]:
                    PE.wait_ge(sSC.h, sps_last_exp[ti])
                PE.matmul(ps[:, 0:512], Ksb[0:64, hp, 128 * c:128 * c + 128],
                          Qsb[0:64, hp, 512 * slot:512 * slot + 512],
                          start=True, stop=not msk, tile_position=(0, 0))
                last = PE.matmul(ps[:, 512:1024], Ksb[64:128, hp, 128 * c:128 * c + 128],
                                 Qsb[64:128, hp, 512 * slot:512 * slot + 512],
                                 start=True, stop=not msk, tile_position=(64, 0))
                if msk:
                    j = c - (L - 6)
                    PE.wait_ge(sGA.h, v_mk)
                    PE.matmul(ps[:, 0:512], id128b[:], mksb[:, j, :],
                              start=False, stop=True)
                    last = PE.matmul(ps[:, 512:1024], id128b[:], mksb[:, j, :],
                                     start=False, stop=True)
                return sPE.inc(last)

            def emit_exp(c, v_qk):
                pr = c % 4
                SC.wait_ge(sPE.h, v_qk)
                if c >= 4:
                    SC.wait_ge(sPE.h, v_pvc[c - 4])
                else:
                    if v_pring["spe"]:
                        SC.wait_ge(sPE.h, v_pring["spe"])
                if c >= L - 3:
                    bias_ap = negcP[:, 3 * slot + (c - (L - 3)), hp:hp + 1]
                else:
                    bias_ap = negc[:, hp:hp + 1]
                v_exp[c] = so(SC.activation, out=Pv[:, pr, :], in_=sps[c % 2][:], func=AF.Exp,
                              bias=bias_ap, scale=0.125)
                sps_last_exp[c % 2] = v_exp[c]

            def emit_pv(c, hp=hp, L=L, v_exp=v_exp, v_pvc=v_pvc):
                pr = c % 4
                if c == 0 and v_ya_free[0]:
                    PE.wait_ge(sDV.h, v_ya_free[0])
                PE.wait_ge(sSC.h, v_exp[c])
                PE.matmul(ya[:], Vsb[:, c, 65 * (2 * hp):65 * (2 * hp) + 65],
                          Pv[:, pr, 0:512], start=(c == 0), stop=(c == L - 1))
                v_pvc[c] = sPE.inc(PE.matmul(yb[:], Vsb[:, c, 65 * (2 * hp + 1):65 * (2 * hp + 1) + 65],
                                             Pv[:, pr, 512:1024], start=(c == 0), stop=(c == L - 1)))

            for c in range(L):
                v_qk = emit_qk(c)
                if c == 0 and pend_tail[0] is not None:
                    pend_tail[0]()
                    pend_tail[0] = None
                emit_exp(c, v_qk)
                if pend_epi[0] is not None:
                    if c == 1:
                        pend_epi[0][0]()
                    elif c == 2:
                        pend_epi[0][1]()
                    elif c == 3:
                        pend_epi[0] = None
                        emit_pv(0)
                        emit_pv(1)
                        emit_pv(2)
                elif 1 <= c <= L - 2:
                    emit_pv(c - 1)
                if c == 2 and pend_ptail[0] is not None:
                    pend_ptail[0]()
                    pend_ptail[0] = None
                if c in (4, 6) and fillers:
                    fillers.pop(0)()
            for f in fillers:
                f()
            if parcel_tailA is not None:
                parcel_tailA()
                pend_ptail[0] = parcel_tail2

            def mk_tail(emit_pv_e, v_pvc_e, slot_e, hp_e, L_e):
                def tail():
                    emit_pv_e(L_e - 2)
                    emit_pv_e(L_e - 1)
                    V.wait_ge(sPE.h, v_pvc_e[L_e - 1])
                    vo(V.tensor_copy, out=Yevb[0:64, 0:512], in_=ya[0:64, :])
                    v_yev = vo(V.tensor_copy, out=Yevb[0:64, 512:1024], in_=yb[0:64, :])
                    v_pring["spe"] = v_pvc_e[L_e - 1]
                    pend_epi[0] = mk_epi(slot_e, hp_e, v_yev, v_pvc_e[L_e - 1])
                return tail

            def mk_epi(slot_e, hp_e, v_yev_e, v_pv_last):
                st = {}

                def emit_zchain():
                    SC.wait_ge(sPE.h, max(v_pv_last, v_zinv_free_pe[0]))
                    so(SC.activation, out=Yevb[64:65, 0:512], in_=ya[64:65, :], func=AF.Ln)
                    so(SC.activation, out=Yevb[64:65, 512:1024], in_=yb[64:65, :], func=AF.Ln)
                    v_zi = so(SC.activation, out=zinvr[:], in_=Yevb[64:65, :], func=AF.Exp, scale=-1.0)
                    st["v_zi"] = v_zi

                def emit_ynmul():
                    ynv_e = yn_view(slot_e)
                    PE.wait_ge(sSC.h, st["v_zi"])
                    PE.wait_ge(sDV.h, v_yev_e)
                    PE.matmul(ya[0:64, :], on1b[0:1, 0:64], zinvr[0:1, 0:512],
                              start=True, stop=True)
                    v_zbB = sPE.inc(PE.matmul(yb[0:64, :], on1b[0:1, 0:64],
                                              zinvr[0:1, 512:1024], start=True, stop=True))
                    v_zinv_free_pe[0] = v_zbB
                    V.wait_ge(sPE.h, v_zbB)
                    if dbg and v_dbg_gate[0]:
                        V.wait_ge(sOUT.h, v_dbg_gate[0])
                    vo(V.tensor_tensor, out=ynv_e[0:64, hp_e, :], in0=Yevb[0:64, 0:512],
                       in1=ya[0:64, :], op=ALU.mult)
                    v_done = vo(V.tensor_tensor, out=ynv_e[64:128, hp_e, :],
                                in0=Yevb[0:64, 512:1024], in1=yb[0:64, :], op=ALU.mult)
                    v_ya_free[0] = v_done
                    if hp_e == 7:
                        yn_ready[slot_e] = v_done
                        if "dbg_yn" in dbg:
                            SY.wait_ge(sDV.h, v_done)
                            SY.dma_start(out=dbg["dbg_yn"][:, :, 512 * slot_e:512 * slot_e + 512],
                                         in_=ynv_e).then_inc(sOUT.h, 16)
                            sOUT.v += 16
                            v_dbg_gate[0] = sOUT.v

                return emit_zchain, emit_ynmul

            pend_tail[0] = mk_tail(emit_pv, v_pvc, slot, hp, L)

        v_mask_free = sPE.v

    pend_tail[0]()
    pend_tail[0] = None
    pend_epi[0][0]()
    pend_epi[0][1]()
    pend_epi[0] = None

    if pend_ptail[0] is not None:
        pend_ptail[0]()
        pend_ptail[0] = None
    for k in range(4):
        fl, ta, t2 = emit_ffn_parcel(512 * 3 + 128 * k)
        for f in fl:
            f()
        ta()
        t2()
        fl, ta, t2 = emit_op_parcel(3, k)
        for f in fl:
            f()
        ta()
        t2()
    fl, ta, t2 = emit_ffn_parcel(None)
    for f in fl:
        f()
    ta()
    t2()

    SY.wait_ge(sOUT.h, sOUT.v)
    est.close()
    return nc


_NC_CACHE = {}


def _get_nc():
    if "nc" not in _NC_CACHE:
        _NC_CACHE["nc"] = build_nc()
    return _NC_CACHE["nc"]


def kernel(L0_attn_x, L0_attn_Wq, L0_attn_bq, L0_attn_Wk, L0_attn_bk,
           L0_attn_Wv, L0_attn_bv, L0_attn_Wp, L0_attn_bp,
           L0_ffn_W, L0_ffn_b, L0_ffn_logits, L0_ffn_u):
    x = np.ascontiguousarray(np.asarray(L0_attn_x, dtype=np.float32))
    wq = np.ascontiguousarray(np.asarray(L0_attn_Wq, dtype=np.float32))
    wk = np.ascontiguousarray(np.asarray(L0_attn_Wk, dtype=np.float32))
    wv = np.ascontiguousarray(np.asarray(L0_attn_Wv, dtype=np.float32))
    wp = np.ascontiguousarray(np.asarray(L0_attn_Wp, dtype=np.float32))
    bq = np.asarray(L0_attn_bq, dtype=np.float32)
    bk = np.asarray(L0_attn_bk, dtype=np.float32)
    bv = np.asarray(L0_attn_bv, dtype=np.float32)
    bp = np.asarray(L0_attn_bp, dtype=np.float32)
    wf = np.ascontiguousarray(np.asarray(L0_ffn_W, dtype=np.float32).reshape(8 * C, C))
    bf = np.ascontiguousarray(np.asarray(L0_ffn_b, dtype=np.float32))
    lg = np.asarray(L0_ffn_logits, dtype=np.float32)
    u = np.asarray(L0_ffn_u, dtype=np.float32)

    bq_c = np.ascontiguousarray(bq.reshape(8, 128).T)
    bk_c = np.ascontiguousarray(bk.reshape(8, 128).T)
    bv_r = np.ascontiguousarray(bv.reshape(1, C))
    bp_r = np.ascontiguousarray(bp.reshape(1, C))
    lu = np.ascontiguousarray(np.concatenate([lg, u]).reshape(1, 16))
    masks = [host_masks(0), host_masks(1)]
    padbs_h = [host_padb(0), host_padb(1)]

    in_maps = []
    for core in range(8):
        b, r = core // 2, core % 2
        xb = x[b]
        rows = np.concatenate([np.arange(512 * j, 512 * j + 512) for j in QB[r]])
        xq = np.ascontiguousarray(xb[rows])
        xkv = np.ascontiguousarray(xb.reshape(T // 4, 4, C)[:, 0:3, :].reshape(KT, C))
        in_maps.append({
            "xq": xq, "xkv": xkv, "wq": wq, "wk": wk, "wv": wv, "wp": wp,
            "bq": bq_c, "bk": bk_c, "bv": bv_r, "bp": bp_r,
            "wf": wf, "bf": bf, "lu": lu, "mk": masks[r], "padb": padbs_h[r],
        })

    nc = _get_nc()
    res = run_bass_kernel_spmd(nc, in_maps, core_ids=list(range(8)))

    out = np.empty((B, T, C), dtype=np.float32)
    for core in range(8):
        b, r = core // 2, core % 2
        o = res.results[core]["out"]
        for i, j in enumerate(QB[r]):
            out[b, 512 * j:512 * j + 512] = o[512 * i:512 * i + 512]
    return out
